# version 14
# baseline (speedup 1.0000x reference)
import numpy as np
import ml_dtypes
from math import comb
import contextlib
import os

import concourse.bass as bass
import concourse.mybir as mybir
from concourse.bass_utils import run_bass_kernel_spmd

F32 = mybir.dt.float32
BF16 = mybir.dt.bfloat16
U8 = mybir.dt.uint8
AF = mybir.ActivationFunctionType
ALU = mybir.AluOpType

NCORES = 8
DM = 2048
SEQ = 8192
NH = 16
HD = 128
INTER = 8192
TPC = 2048
TCH = 512
NCH = TPC // TCH
HT = DM // 128
NIT = INTER // 128
NCT = DM // 128
NCTRL = 130156
NSAMP = DM * INTER
FGRID = NSAMP // 128
FPC = FGRID // NCORES
SCALE = float(1.0 / np.sqrt(HD))
EPS = 1e-5
BLK = 64
NQB = SEQ // BLK
WINB = 16
DILB = 4
GLOB = 2
NEG = -1.0e9

_CACHE = {}
MK_PHASES = int(os.environ.get("MK_PHASES", "9"))
MK_NORACE = bool(int(os.environ.get("MK_NORACE", "0")))
MK_CUT = int(os.environ.get("MK_CUT", "0"))


def _grid_tables():
    if "grid" in _CACHE:
        return _CACHE["grid"]
    n = NSAMP
    t = np.linspace(0.0, NCTRL - 1.0, n, dtype=np.float64)
    i_arr = np.clip(np.floor(t).astype(np.int64), 0, NCTRL - 2)
    i2 = i_arr.reshape(FGRID, 128)
    t2 = t.reshape(FGRID, 128)
    IA = i2[:, 0].copy()
    IB = np.minimum(IA + 1, NCTRL - 2)
    crosses = i2[:, 127] > IA
    first_b = (i2 > IA[:, None]).argmax(axis=1)
    pstar = np.where(crosses, first_b, 128)
    u0A = t2[:, 0] - IA
    u0B = t2[:, 0] - IB
    Bm = 0.5 * np.array([[0, 2, 0, 0], [-1, 0, 1, 0], [2, -5, 4, -1], [-1, 3, -3, 1]],
                        dtype=np.float64)
    step = (NCTRL - 1.0) / (n - 1.0)

    def g_tables(I, u0):
        idx = np.stack([np.clip(I - 1 + j, 0, NCTRL - 1) for j in range(4)], axis=0)
        G = np.zeros((4, 4, FGRID), dtype=np.float64)
        for d in range(4):
            for j in range(4):
                for e in range(d, 4):
                    G[d, j] += comb(e, d) * (u0 ** (e - d)) * Bm[e, j]
        return idx.astype(np.int32), G

    idxA, GA = g_tables(IA, u0A)
    idxB, GB = g_tables(IB, u0B)
    vp = np.stack([(np.arange(128) * step) ** d for d in range(4)], axis=0)
    _CACHE["grid"] = dict(idxA=idxA, idxB=idxB, GA=GA, GB=GB, pstar=pstar, vp=vp)
    return _CACHE["grid"]


def _col_order(core, which):
    if which == "gu":
        il = np.arange(8)[:, None, None]
        hb = np.arange(HT)[None, :, None]
        iin = np.arange(128)[None, None, :]
        i_glob = (core * 8 + il) * 128 + iin
        return (i_glob * HT + hb).reshape(-1)
    ctl = np.arange(2)[:, None, None]
    it = np.arange(NIT)[None, :, None]
    hin = np.arange(128)[None, None, :]
    hout = core * 256 + ctl * 128 + hin
    return (hout * NIT + it).reshape(-1)


def _pack_f(vals):
    v = vals.reshape(32, 512)
    return np.ascontiguousarray(np.broadcast_to(v[None], (4, 32, 512)).reshape(128, 512))


def _pack_df(vals):
    return np.ascontiguousarray(vals.reshape(4, 32, 512).reshape(128, 512))


def _expansion_consts(core):
    key = ("exp", core)
    if key in _CACHE:
        return _CACHE[key]
    g = _grid_tables()
    out = {}
    for which in ("gu", "d"):
        fg = _col_order(core, which)
        scale = 0.5 if which == "d" else 1.0
        for pl, G, idx in (("A", g["GA"], g["idxA"]), ("B", g["GB"], g["idxB"])):
            Gc = G[:, :, fg] * scale
            for j in range(4):
                out[f"G_{which}_{pl}_{j}"] = _pack_df(Gc[:, j, :].astype(np.float32))
            out[f"idx_{which}_{pl}"] = idx[:, fg]
        ps = g["pstar"][fg]
        out[f"mask_{which}"] = np.ascontiguousarray(
            (np.arange(128)[:, None] >= ps[None, :]).astype(np.uint8))
    vph = g["vp"].astype(ml_dtypes.bfloat16)
    vpl = (g["vp"] - vph.astype(np.float64)).astype(ml_dtypes.bfloat16)
    out["vph"] = np.asarray(vph)
    out["vpl"] = np.asarray(vpl)
    _CACHE[key] = out
    return out


def _rope_tables():
    if "rope" in _CACHE:
        return _CACHE["rope"]
    t = np.arange(SEQ, dtype=np.float64)
    inv = 1.0 / (10000.0 ** (np.arange(0, HD, 2, dtype=np.float64) / HD))
    fr = np.outer(t, inv)
    emb = np.concatenate([fr, fr], axis=1)
    cosT = np.ascontiguousarray(np.cos(emb).T.astype(np.float32))
    sinT = np.sin(emb).T
    sg = np.ones((HD, 1)); sg[:64] = -1.0
    sinS = np.ascontiguousarray((sinT * sg).astype(np.float32))
    _CACHE["rope"] = (cosT, sinS)
    return _CACHE["rope"]


def _allowed(qb, kb):
    if kb > qb:
        return False
    return (qb - kb <= WINB) or (kb < GLOB) or (kb % DILB == 0)


def _attn_schedule():
    if "attn" in _CACHE:
        return _CACHE["attn"]
    packed_kbs = [0, 1] + list(range(4, NQB, 4))
    masks = {}
    mask_list = []

    def mask_id(bitmap):
        key = tuple(map(tuple, bitmap))
        if key not in masks:
            m = np.zeros((128, 512), dtype=np.float32)
            for kk in range(2):
                for qq in range(8):
                    if not bitmap[kk][qq]:
                        m[kk * 64:(kk + 1) * 64, qq * 64:(qq + 1) * 64] = NEG
            masks[key] = len(mask_list)
            mask_list.append(m)
        return masks[key]

    sched = []
    for qc in range(16):
        qbs = list(range(8 * qc, 8 * qc + 8))
        wlo = max(0, 8 * qc - WINB)
        pk = [kb for kb in packed_kbs if kb < wlo]
        for kb in pk:
            assert all(_allowed(qb, kb) for qb in qbs)
        n_pk_tok = len(pk) * BLK
        n_pk_tiles = (n_pk_tok + 127) // 128
        pk_bnd = mask_id([[True] * 8, [False] * 8]) if n_pk_tok % 128 == 64 else None
        wtiles = []
        kb0 = wlo - (wlo % 2)
        for kbt in range(kb0, 8 * qc + 8, 2):
            kbs = [kbt, kbt + 1]
            bm = [[_allowed(qb, kb) for qb in qbs] for kb in kbs]
            if not any(any(r) for r in bm):
                continue
            mid = None if all(all(r) for r in bm) else mask_id(bm)
            qsub = [any(bm[kk][qq] for kk in range(2) for qq in (2 * qs, 2 * qs + 1))
                    for qs in range(4)]
            wtiles.append((kbt // 2, mid, qsub))
        sched.append(dict(n_pk_tiles=n_pk_tiles, n_pk_tok=n_pk_tok, pk_bnd=pk_bnd,
                          wtiles=wtiles))
    res = dict(sched=sched,
               masks=np.stack(mask_list) if mask_list else np.zeros((1, 128, 512), np.float32),
               packed_kbs=packed_kbs)
    _CACHE["attn"] = res
    return res


def _host_prep(inputs):
    x = np.asarray(inputs["L0_attn_x"])
    wq = np.asarray(inputs["L0_attn_w_q"]); wk = np.asarray(inputs["L0_attn_w_k"])
    wv = np.asarray(inputs["L0_attn_w_v"]); wo = np.asarray(inputs["L0_attn_w_o"])
    cps = {"g": np.asarray(inputs["L0_ffn_gate_cp"]),
           "u": np.asarray(inputs["L0_ffn_up_cp"]),
           "d": np.asarray(inputs["L0_ffn_down_cp"])}

    wqT = np.ascontiguousarray(wq.T); wkT = np.ascontiguousarray(wk.T)
    wvT = wv.T.reshape(HT, 128, 4, TCH).transpose(1, 2, 0, 3).reshape(128, 4 * HT * TCH)
    wvP = np.ascontiguousarray(wvT)
    woT = wo.T
    perm = np.empty(DM, dtype=np.int64)
    for b in range(2):
        for r in range(8):
            perm[(b * 8 + r) * 128:(b * 8 + r + 1) * 128] = np.arange(
                (2 * r + b) * 128, (2 * r + b + 1) * 128)
    woP0 = woT[perm]
    woP = np.ascontiguousarray(
        woP0.reshape(HT, 128, NCT, 128).transpose(1, 2, 0, 3).reshape(128, NCT * HT * 128))

    rope_cos, rope_sin = _rope_tables()
    att = _attn_schedule()
    amask = np.ascontiguousarray(att["masks"].transpose(1, 0, 2).reshape(128, -1))

    in_maps = []
    for c in range(NCORES):
        s, pos = c // 4, (c % 4) * TPC
        xcT = x[s, pos:pos + TPC, :].T
        xp = np.ascontiguousarray(
            xcT.reshape(HT, 128, NCH, TCH).transpose(1, 2, 0, 3).reshape(128, NCH * HT * TCH))
        ec = _expansion_consts(c)
        m = {
            "xt": xp.astype(np.float32),
            "wqsl": np.ascontiguousarray(wqT[:, 2 * c * 128:(2 * c + 2) * 128]),
            "wksl": np.ascontiguousarray(wkT[:, 2 * c * 128:(2 * c + 2) * 128]),
            "wvt": wvP.astype(np.float32),
            "wop": woP.astype(np.float32),
            "ropec": rope_cos, "ropes": rope_sin, "amask": amask,
            "identin": np.asarray(np.eye(128, dtype=np.float32).astype(ml_dtypes.bfloat16)),
            "vph": ec["vph"], "vpl": ec["vpl"],
            "maskgu": ec["mask_gu"], "maskd": ec["mask_d"],
        }
        for which in ("gu", "d"):
            for pl in ("A", "B"):
                for j in range(4):
                    m[f"G{which}{pl}{j}"] = ec[f"G_{which}_{pl}_{j}"]
        for w in ("g", "u", "d"):
            which = "gu" if w != "d" else "d"
            for pl in ("A", "B"):
                idx = ec[f"idx_{which}_{pl}"]
                for j in range(4):
                    m[f"C{w}{pl}{j}"] = _pack_f(cps[w][idx[j]]).astype(np.float32)
        in_maps.append(m)
    return in_maps


WLIST = ("g", "u", "d")
XB_TC, APL_TC = 32, 32


def build_nc():
    att = _attn_schedule()
    n_amask = att["masks"].shape[0]
    assert n_amask <= 16, n_amask
    nc = bass.Bass("TRN2", target_bir_lowering=False,
                   detect_race_conditions=not MK_NORACE)

    din = {}
    def DI(name, shape, dt=F32):
        din[name] = nc.dram_tensor(name, list(shape), dt, kind="ExternalInput").ap()
        return din[name]

    DI("xt", [128, NCH * HT * TCH])
    DI("wqsl", [DM, 256]); DI("wksl", [DM, 256])
    DI("wvt", [128, 4 * HT * TCH]); DI("wop", [128, NCT * HT * 128])
    DI("ropec", [HD, SEQ]); DI("ropes", [HD, SEQ])
    DI("amask", [128, n_amask * 512])
    DI("identin", [128, 128], BF16)
    DI("vph", [4, 128], BF16); DI("vpl", [4, 128], BF16)
    DI("maskgu", [128, FPC], U8); DI("maskd", [128, FPC], U8)
    for which in ("gu", "d"):
        for pl in ("A", "B"):
            for j in range(4):
                DI(f"G{which}{pl}{j}", [128, 512])
    for w in WLIST:
        for pl in ("A", "B"):
            for j in range(4):
                DI(f"C{w}{pl}{j}", [128, 512])

    out = nc.dram_tensor("out", [128, NCH * HT * TCH], F32, kind="ExternalOutput").ap()

    resid = nc.dram_tensor("resid", [128, NCH * HT * TCH], F32).ap()
    qdram = {w: nc.dram_tensor(f"qdram_{w}", [4, 4 * FPC], BF16).ap() for w in WLIST}
    wsh = {w: nc.dram_tensor(f"wsh_{w}", [128, FPC], BF16).ap() for w in WLIST}
    wfull = {w: nc.dram_tensor(f"wfull_{w}", [NCORES * 128, FPC], BF16).ap()
             for w in WLIST}
    wq_bf = nc.dram_tensor("wq_bf", [DM, 256], BF16).ap()
    wk_bf = nc.dram_tensor("wk_bf", [DM, 256], BF16).ap()
    wv_bf = nc.dram_tensor("wv_bf", [128, 4 * HT * TCH], BF16).ap()
    wo_bf = nc.dram_tensor("wo_bf", [128, NCT * HT * 128], BF16).ap()
    agx_in = [nc.dram_tensor(f"agxin{j}", [128, HT * TCH], BF16).ap() for j in range(NCH)]
    agx = [nc.dram_tensor(f"agx{j}", [NCORES * 128, HT * TCH], BF16).ap() for j in range(NCH)]
    agv_in = [nc.dram_tensor(f"agvin{j}", [128, 4 * DM], BF16).ap() for j in range(NCH)]
    agv = [nc.dram_tensor(f"agv{j}", [NCORES * 128, 4 * DM], BF16).ap() for j in range(NCH)]
    agc_in = [nc.dram_tensor(f"agcin{u}", [128, SEQ], BF16).ap() for u in range(4)]
    vstage = nc.dram_tensor("vstage", [4 * 512, 512], BF16).ap()
    cstage_h = nc.dram_tensor("cstage", [2048, 2048], BF16)
    cstage = cstage_h.ap()
    agc_h = nc.dram_tensor("agc", [4 * NCORES * 128, SEQ], BF16)
    agc = agc_h.ap()

    es = contextlib.ExitStack()
    def SB(name, shape, dt):
        return es.enter_context(nc.sbuf_tensor(name, list(shape), dt))

    ones_bf = SB("ones_bf", [128, 1], BF16)
    ones1f = SB("ones1f", [1, 128], F32)
    ident = SB("ident", [128, 128], BF16)
    eps_t = SB("eps_t", [1, 1], F32)
    gc_area = SB("gc_area", [128, 4 * 512], F32)
    q_scr = SB("q_scr", [128, 2 * 512], F32)
    q_hl = SB("q_hl", [128, 2 * 512], BF16)
    qrows = SB("qrows", [4, 2 * 2048], BF16)
    vp_sb = SB("vp_sb", [4, 2 * 128], BF16)
    mask_sb = SB("mask_sb", [128, FPC], U8)
    wexp = SB("wexp", [128, 2 * 512], BF16)
    xch = SB("xch", [128, HT * TCH], F32)
    xbx = SB("xbx", [128, 4 * 512], BF16)
    xln = SB("xln", [128, HT * TCH], BF16)
    inter = SB("inter", [128, NIT * TCH], BF16)
    arena = SB("arena", [128, 16384], BF16)
    scr = SB("scr", [128, 2048], F32)
    small = SB("small", [128, 256], F32)

    ROW_M, ROW_V, ROW_T = 512, 1024, 1536

    psum = [es.enter_context(nc.psum_tensor(f"pb{i}", [128, 512], F32)) for i in range(7)]
    psbf = es.enter_context(nc.psum_tensor("pb7", [128, 1024], BF16))

    sem_names = [
        "ldc", "gc0", "gc1", "cast", "qst0", "qst1", "qst2", "qr0", "qr1",
        "wst0", "wst1", "x", "wg0", "wg1", "wu0", "wu1", "wd0", "wd1", "wv0",
        "vw0", "vw1", "vw2", "vw3", "wvb", "ap2", "rv2",
        "st_x1", "st_xln2", "st_agv", "fin",
        "ag",
        "ax0", "ax1", "rop", "qkm", "radd", "sqn", "nmm", "nrm", "msv",
        "ngv", "ngs", "ngm", "ngc", "vdl", "vwn", "amk", "smm", "msk",
        "sexp", "ctm", "ctp", "rcp", "rc2", "rp2", "ctt", "cte", "st_ctx", "wqk", "vst", "cst",
        "cxi", "wo0", "wo1", "opj", "res2", "st_out",
        "qprep", "mm", "ev", "ve", "xb", "stat", "rowv", "rows", "bc", "apl",
        "silu", "int_", "gu", "dn", "res", "vmm", "vev",
    ]
    sems = {n: es.enter_context(nc.semaphore(n)) for n in sem_names}
    S = lambda n: sems[n]

    PS1, PS2, PM, PR, PG, PU, PD = 0, 1, 2, 3, 4, 5, 6
    PV = 4
    VE0 = 1

    def xcol(ht, c0=0, n=TCH):
        return slice(ht * TCH + c0, ht * TCH + c0 + n)

    NCH_E = NCH if MK_PHASES >= 2 else 0
    DO_GU = MK_CUT == 0 or MK_CUT >= 2
    DO_SILU = MK_CUT in (0, 22, 2, 3)
    DO_INT = MK_CUT in (0, 2, 3)
    DO_DN = MK_CUT in (0, 3)
    DO_LN2V = MK_CUT == 0
    NLN = 2 if DO_LN2V else 1
    XBT = 16 * NLN
    APLT = 16 * NLN

    with nc.Block() as block:
        nc.cache_partition_id()
        @block.sync
        def _(sy):
            sy.dma_start(out=vp_sb[:, 0:128], in_=din["vph"]).then_inc(S("ldc"), 16)
            sy.dma_start(out=vp_sb[:, 128:256], in_=din["vpl"]).then_inc(S("ldc"), 16)
            sy.dma_start(out=mask_sb[:, :], in_=din["maskgu"]).then_inc(S("ldc"), 16)

        @block.gpsimd
        def _(g):
            g.dma_start(out=wq_bf, in_=din["wqsl"]).then_inc(S("cast"), 16)
            g.dma_start(out=wk_bf, in_=din["wksl"]).then_inc(S("cast"), 16)
            g.dma_start(out=wv_bf, in_=din["wvt"]).then_inc(S("cast"), 16)
            g.dma_start(out=wo_bf, in_=din["wop"]).then_inc(S("cast"), 16)
            g.dma_start(out=ident[:, :], in_=din["identin"]).then_inc(S("cast"), 16)

        @block.vector
        def _(v):
            v.memset(ones_bf[:, :], 1.0)
            v.memset(ones1f[:, :], 1.0)
            v.memset(eps_t[:, :], EPS).then_inc(S("ve"), 1)

        @block.sync
        def _(sy):
            for wi, w in enumerate(WLIST):
                which = "gu" if w != "d" else "d"
                for pi, pl in enumerate(("A", "B")):
                    wpl = wi * 2 + pi
                    for j in range(4):
                        gp = wpl * 4 + j
                        if gp >= 2:
                            w2, j2 = divmod(gp - 2, 4)
                            sy.wait_ge(S("qprep"), w2 * 5 + j2 + 1)
                        slot = (gp % 2) * 1024
                        gcs = S("gc0") if gp % 2 == 0 else S("gc1")
                        sy.dma_start(out=gc_area[:, slot:slot + 512],
                                     in_=din[f"G{which}{pl}{j}"]).then_inc(gcs, 16)
                        sy.dma_start(out=gc_area[:, slot + 512:slot + 1024],
                                     in_=din[f"C{w}{pl}{j}"]).then_inc(gcs, 16)

        @block.vector
        def _(v):
            for wi, w in enumerate(WLIST):
                for pi in range(2):
                    wpl = wi * 2 + pi
                    acc = q_scr[:, 0:512]; tmp = q_scr[:, 512:1024]
                    for j in range(4):
                        gp = wpl * 4 + j
                        gcs = S("gc0") if gp % 2 == 0 else S("gc1")
                        v.wait_ge(gcs, 32 * (gp // 2 + 1))
                        slot = (gp % 2) * 1024
                        if j == 0:
                            v.tensor_tensor(acc, gc_area[:, slot:slot + 512],
                                            gc_area[:, slot + 512:slot + 1024],
                                            ALU.mult).then_inc(S("qprep"), 1)
                        else:
                            v.tensor_tensor(tmp, gc_area[:, slot:slot + 512],
                                            gc_area[:, slot + 512:slot + 1024],
                                            ALU.mult).then_inc(S("qprep"), 1)
                            v.wait_ge(S("qprep"), wpl * 5 + j + 1)
                            v.tensor_tensor(acc, acc, tmp, ALU.add)
                    if wpl > 0:
                        qs_prev = S(f"qst{(wpl - 1) // 2}")
                        v.wait_ge(qs_prev, 32 * (2 if (wpl - 1) % 2 else 1))
                    v.tensor_copy(q_hl[:, 0:512], acc)
                    v.tensor_tensor(q_hl[:, 512:1024], acc, q_hl[:, 0:512],
                                    ALU.subtract).then_inc(S("qprep"), 1)

        @block.scalar
        def _(sc):
            for wi, w in enumerate(WLIST):
                qv = qdram[w].rearrange("d (pl hl f) -> d pl hl f", pl=2, hl=2)
                for pi in range(2):
                    wpl = wi * 2 + pi
                    sc.wait_ge(S("qprep"), (wpl + 1) * 5)
                    qs = S(f"qst{wi}")
                    sc.dma_start(out=qv[:, pi, 0, :].rearrange("d (b n) -> d b n", b=32),
                                 in_=q_hl[:, 0:512]).then_inc(qs, 16)
                    sc.dma_start(out=qv[:, pi, 1, :].rearrange("d (b n) -> d b n", b=32),
                                 in_=q_hl[:, 512:1024]).then_inc(qs, 16)

        @block.sync
        def _(sy):
            for wi, w in enumerate(WLIST):
                qv = qdram[w].rearrange("d (pl hl f) -> d pl hl f", pl=2, hl=2)
                sy.wait_ge(S(f"qst{wi}"), 64)
                for ch in range(32):
                    gch = wi * 32 + ch
                    if gch >= 2:
                        sy.wait_ge(S("mm"), 2 * (gch - 1))
                    slot = (gch % 2) * 2048
                    qrs = S("qr0") if gch % 2 == 0 else S("qr1")
                    for q, (pi, hl) in enumerate(((0, 0), (0, 1), (1, 0), (1, 1))):
                        sy.dma_start(out=qrows[0:4, slot + q * 512:slot + (q + 1) * 512],
                                     in_=qv[:, pi, hl, ch * 512:(ch + 1) * 512]
                                     ).then_inc(qrs, 16)

        @block.tensor
        def _(t):
            t.wait_ge(S("ldc"), 48)
            for wi, w in enumerate(WLIST):
                for ch in range(32):
                    gch = wi * 32 + ch
                    qrs = S("qr0") if gch % 2 == 0 else S("qr1")
                    t.wait_ge(qrs, 64 * (gch // 2 + 1))
                    if gch >= 1:
                        t.wait_ge(S("ve"), VE0 + gch)
                    slot = (gch % 2) * 2048
                    for bank, pi in ((0, 0), (1, 1)):
                        ah = qrows[0:4, slot + (2 * pi) * 512:slot + (2 * pi + 1) * 512]
                        al = qrows[0:4, slot + (2 * pi + 1) * 512:slot + (2 * pi + 2) * 512]
                        t.matmul(psum[bank][:, :], vp_sb[:, 0:128], ah, start=True, stop=False)
                        t.matmul(psum[bank][:, :], vp_sb[:, 128:256], ah, start=False, stop=False)
                        t.matmul(psum[bank][:, :], vp_sb[:, 0:128], al,
                                 start=False, stop=True).then_inc(S("mm"), 1)

        @block.scalar
        def _(sc):
            for gch in range(96):
                sc.wait_ge(S("mm"), 2 * gch + 1)
                if gch >= 2:
                    ws = S("wst0") if gch % 2 == 0 else S("wst1")
                    sc.wait_ge(ws, 16 * (gch // 2))
                sc.copy(wexp[:, (gch % 2) * 512:(gch % 2 + 1) * 512],
                        psum[0][:, :]).then_inc(S("ev"), 1)

        @block.vector
        def _(v):
            v.wait_ge(S("ldc"), 48)
            for wi, w in enumerate(WLIST):
                for ch in range(32):
                    gch = wi * 32 + ch
                    v.wait_ge(S("ev"), gch + 1)
                    v.wait_ge(S("mm"), 2 * gch + 2)
                    if w == "d" and ch == 0:
                        v.wait_ge(S("cast"), 96)
                    v.copy_predicated(wexp[:, (gch % 2) * 512:(gch % 2 + 1) * 512],
                                      mask_sb[:, ch * 512:(ch + 1) * 512],
                                      psum[1][:, :]).then_inc(S("ve"), 1)

        @block.gpsimd
        def _(g):
            for wi, w in enumerate(WLIST):
                if w == "d":
                    g.wait_ge(S("ve"), VE0 + 64)
                    g.dma_start(out=mask_sb[:, :], in_=din["maskd"]).then_inc(S("cast"), 16)
                for ch in range(32):
                    gch = wi * 32 + ch
                    g.wait_ge(S("ve"), VE0 + gch + 1)
                    ws = S("wst0") if gch % 2 == 0 else S("wst1")
                    g.dma_start(out=wsh[w][:, ch * 512:(ch + 1) * 512],
                                in_=wexp[:, (gch % 2) * 512:(gch % 2 + 1) * 512]
                                ).then_inc(ws, 16)
                g.wait_ge(S("wst0"), 16 * 16 * (wi + 1))
                g.wait_ge(S("wst1"), 16 * 16 * (wi + 1))
                g.collective_compute("AllGather", ALU.bypass,
                                     replica_groups=[list(range(NCORES))],
                                     ins=[wsh[w].opt()], outs=[wfull[w].opt()]
                                     ).then_inc(S("ag"), 1)

        @block.sync
        def _(sy):
            for tc in range(NCH_E):
                if tc > 0:
                    sy.wait_ge(S("st_x1"), tc * 16)
                    if DO_LN2V:
                        sy.wait_ge(S("st_agv"), tc * 16)
                    sy.wait_ge(S("xb"), XBT * tc)
                    sy.wait_ge(S("apl"), APLT * tc)
                sy.dma_start(out=xch[:, :],
                             in_=din["xt"][:, tc * HT * TCH:(tc + 1) * HT * TCH]
                             ).then_inc(S("x"), 16)
                if not DO_GU:
                    continue
                sy.wait_ge(S("ag"), 2)
                if tc > 0 and DO_LN2V:
                    sy.wait_ge(S("vmm"), tc * 16)
                for it in range(NIT):
                    for gu, w in ((0, "g"), (1, "u")):
                        base = 0 if gu == 0 else 4096
                        slot = base + (it % 2) * 2048
                        wsl = S(f"w{'g' if gu == 0 else 'u'}{it % 2}")
                        if it >= 2:
                            sy.wait_ge(S("gu"), tc * 128 + 2 * (it - 2) + gu + 1)
                        r, il = it // 8, it % 8
                        sy.dma_start(out=arena[:, slot:slot + 2048],
                                     in_=wfull[w][r * 128:(r + 1) * 128,
                                                  il * 2048:(il + 1) * 2048]
                                     ).then_inc(wsl, 16)
                if not DO_DN:
                    continue
                sy.wait_ge(S("ag"), 3)
                sy.wait_ge(S("gu"), tc * 128 + 128)
                for ct in range(NCT):
                    slot = (ct % 2) * 8192
                    wsl = S(f"wd{ct % 2}")
                    if ct >= 2:
                        sy.wait_ge(S("dn"), tc * 16 + ct - 1)
                    r, ctl = ct // 2, ct % 2
                    sy.dma_start(out=arena[:, slot:slot + 8192],
                                 in_=wfull["d"][r * 128:(r + 1) * 128,
                                                ctl * 8192:(ctl + 1) * 8192]
                                 ).then_inc(wsl, 16)
                if not DO_LN2V:
                    continue
                sy.wait_ge(S("dn"), tc * 16 + 16)
                for dc in range(4):
                    if dc >= 2:
                        sy.wait_ge(S("vmm"), tc * 16 + 4 * (dc - 1))
                    sy.dma_start(out=arena[:, (dc % 2) * 8192:(dc % 2) * 8192 + 8192],
                                 in_=wv_bf[:, dc * 8192:(dc + 1) * 8192]
                                 ).then_inc(S("wv0") if dc % 2 == 0 else S("wvb"), 16)

        @block.scalar
        def _(sc):
            for tc in range(NCH_E):
                sc.wait_ge(S("x"), (tc + 1) * 16)
                for ln in range(NLN):
                    lnn = tc * NLN + ln
                    if ln == 1:
                        sc.wait_ge(S("dn"), tc * 16 + 16)
                    elif tc > 0:
                        sc.wait_ge(S("st_xln2") if DO_LN2V else S("st_x1"), tc * 16)
                    for ht in range(HT):
                        if ln == 1:
                            sc.wait_ge(S("res"), tc * 16 + ht + 1)
                        sc.copy(xln[:, xcol(ht)], xch[:, xcol(ht)])
                        sc.activation(inter[:, ht * 512:(ht + 1) * 512],
                                      xch[:, xcol(ht)], AF.Square).then_inc(S("xb"), 1)
                    sc.wait_ge(S("rowv"), lnn + 1)
                    sc.activation(scr[0:1, ROW_T:ROW_T + 512], scr[0:1, ROW_V:ROW_V + 512],
                                  AF.Ln, bias=eps_t[0:1, :])
                    sc.activation(scr[0:1, ROW_T:ROW_T + 512], scr[0:1, ROW_T:ROW_T + 512],
                                  AF.Exp, scale=-0.5).then_inc(S("rows"), 1)
                    if ln == 0 and DO_SILU:
                        for it in range(NIT):
                            sc.wait_ge(S("gu"), tc * 128 + 2 * it + 1)
                            sl = it % 2
                            sc.activation(xbx[:, sl * 512:(sl + 1) * 512], psum[PG][:, :],
                                          AF.Silu).then_inc(S("silu"), 1)
                if DO_LN2V:
                    for dc in range(4):
                        for tt in range(4):
                            gg = tc * 16 + dc * 4 + tt
                            sc.wait_ge(S("vmm"), gg + 1)
                            if dc == 0 and tt == 0:
                                sc.wait_ge(S("st_x1"), (tc + 1) * 16)
                            sc.copy(xch[:, tt * DM + dc * 512:tt * DM + (dc + 1) * 512],
                                    psum[PV][:, :]).then_inc(S("vev"), 1)

        @block.tensor
        def _(t):
            for tc in range(NCH_E):
                if tc == 0:
                    t.wait_ge(S("ve"), VE0 + 96)
                for ln in range(NLN):
                    lnn = tc * NLN + ln
                    if lnn >= 1:
                        t.wait_ge(S("rowv"), lnn)
                    t.wait_ge(S("xb"), lnn * 16 + 16)
                    for ht in range(HT):
                        t.matmul(psum[PS1][0:1, :], ones_bf[:, :], xln[:, xcol(ht)],
                                 start=(ht == 0), stop=(ht == HT - 1))
                    for ht in range(HT):
                        mmi = t.matmul(psum[PS2][0:1, :], ones_bf[:, :],
                                       inter[:, ht * 512:(ht + 1) * 512],
                                       start=(ht == 0), stop=(ht == HT - 1))
                        if ht == HT - 1:
                            mmi.then_inc(S("stat"), 1)
                    t.wait_ge(S("rows"), lnn + 1)
                    if lnn >= 1:
                        t.wait_ge(S("apl"), lnn * 16)
                    t.matmul(psum[PM][:, :], ones1f[:, :], scr[0:1, ROW_M:ROW_M + 512],
                             start=True, stop=True)
                    t.matmul(psum[PR][:, :], ones1f[:, :], scr[0:1, ROW_T:ROW_T + 512],
                             start=True, stop=True).then_inc(S("bc"), 1)
                    if ln == 0 and not DO_GU:
                        pass
                    elif ln == 0:
                        t.wait_ge(S("apl"), lnn * 16 + 16)
                        if tc > 0 and DO_LN2V:
                            t.wait_ge(S("vev"), tc * 16)
                        for it in range(NIT):
                            for gu, bank in ((0, PG), (1, PU)):
                                wsl = S(f"w{'g' if gu == 0 else 'u'}{it % 2}")
                                t.wait_ge(wsl, 16 * (tc * 32 + it // 2 + 1))
                                if it >= 1:
                                    if gu == 0 and DO_SILU:
                                        t.wait_ge(S("silu"), tc * 64 + it)
                                    elif gu == 1 and DO_INT:
                                        t.wait_ge(S("int_"), tc * 64 + it)
                                base = 0 if gu == 0 else 4096
                                slot = base + (it % 2) * 2048
                                for hb in range(HT):
                                    mmi = t.matmul(psum[bank][:, :],
                                                   arena[:, slot + hb * 128:slot + (hb + 1) * 128],
                                                   xln[:, xcol(hb)],
                                                   start=(hb == 0),
                                                   stop=(hb == HT - 1))
                                mmi.then_inc(S("gu"), 1)
                        for ct in range(NCT if DO_DN else 0):
                            wsl = S(f"wd{ct % 2}")
                            t.wait_ge(wsl, 16 * (tc * 8 + ct // 2 + 1))
                            t.wait_ge(S("int_"), tc * 64 + 64)
                            t.wait_ge(S("res"), tc * 16 + ct)
                            slot = (ct % 2) * 8192
                            for it in range(NIT):
                                mmi = t.matmul(psum[PD][:, :],
                                               arena[:, slot + it * 128:slot + (it + 1) * 128],
                                               inter[:, it * 512:(it + 1) * 512],
                                               start=(it == 0), stop=(it == NIT - 1))
                            mmi.then_inc(S("dn"), 1)
                    else:
                        t.wait_ge(S("apl"), lnn * 16 + 16)
                        t.wait_ge(S("silu"), (tc + 1) * 64)
                        for dc in range(4):
                            t.wait_ge(S("wv0") if dc % 2 == 0 else S("wvb"),
                                      16 * (tc * 2 + dc // 2 + 1))
                            for tt in range(4):
                                gg = tc * 16 + dc * 4 + tt
                                if gg >= 1:
                                    t.wait_ge(S("vev"), gg)
                                for hb in range(HT):
                                    mmi = t.matmul(psum[PV][:, :],
                                                   xln[:, xcol(hb, tt * 128, 128)],
                                                   arena[:, (dc % 2) * 8192 + hb * 512:
                                                          (dc % 2) * 8192 + (hb + 1) * 512],
                                                   start=(hb == 0), stop=(hb == HT - 1))
                                mmi.then_inc(S("vmm"), 1)

        @block.vector
        def _(v):
            for tc in range(NCH_E):
                for ln in range(NLN):
                    lnn = tc * NLN + ln
                    v.wait_ge(S("stat"), lnn + 1)
                    v.tensor_scalar(scr[0:1, ROW_M:ROW_M + 512], psum[PS1][0:1, :],
                                    1.0 / DM, None, ALU.mult)
                    v.tensor_scalar(scr[0:1, ROW_V:ROW_V + 512], psum[PS2][0:1, :],
                                    1.0 / DM, None, ALU.mult)
                    v.tensor_tensor(scr[0:1, ROW_T:ROW_T + 512], scr[0:1, ROW_M:ROW_M + 512],
                                    scr[0:1, ROW_M:ROW_M + 512], ALU.mult
                                    ).then_inc(S("rv2"), 1)
                    v.wait_ge(S("rv2"), lnn + 1)
                    v.tensor_tensor(scr[0:1, ROW_V:ROW_V + 512], scr[0:1, ROW_V:ROW_V + 512],
                                    scr[0:1, ROW_T:ROW_T + 512],
                                    ALU.subtract).then_inc(S("rowv"), 1)
                    v.wait_ge(S("bc"), lnn + 1)
                    if ln == 0 and tc > 0:
                        v.wait_ge(S("st_xln2") if DO_LN2V else S("st_x1"), tc * 16)
                    for ht in range(HT):
                        v.tensor_tensor(scr[:, 0:512], xch[:, xcol(ht)], psum[PM][:, :],
                                        ALU.subtract).then_inc(S("ap2"), 1)
                        v.wait_ge(S("ap2"), lnn * 16 + ht + 1)
                        v.tensor_tensor(xln[:, xcol(ht)], scr[:, 0:512], psum[PR][:, :],
                                        ALU.mult).then_inc(S("apl"), 1)
                    if ln == 0 and DO_INT:
                        for it in range(NIT):
                            v.wait_ge(S("silu"), tc * 64 + it + 1)
                            v.wait_ge(S("gu"), tc * 128 + 2 * it + 2)
                            sl = it % 2
                            v.tensor_tensor(inter[:, it * 512:(it + 1) * 512],
                                            xbx[:, sl * 512:(sl + 1) * 512], psum[PU][:, :],
                                            ALU.mult).then_inc(S("int_"), 1)
                        for ct in range(NCT if DO_DN else 0):
                            v.wait_ge(S("dn"), tc * 16 + ct + 1)
                            v.tensor_tensor(xch[:, xcol(ct)], psum[PD][:, :],
                                            xch[:, xcol(ct)], ALU.add).then_inc(S("res"), 1)

        @block.gpsimd
        def _(g):
            for tc in range(NCH_E):
                if DO_DN:
                    g.wait_ge(S("res"), tc * 16 + 16)
                elif DO_INT:
                    g.wait_ge(S("int_"), tc * 64 + 64)
                elif DO_SILU:
                    g.wait_ge(S("silu"), tc * 64 + 64)
                elif DO_GU:
                    g.wait_ge(S("gu"), tc * 128 + 128)
                else:
                    g.wait_ge(S("apl"), tc * APLT + 16)
                g.dma_start(out=resid[:, tc * HT * TCH:(tc + 1) * HT * TCH], in_=xch[:, :]
                            ).then_inc(S("st_x1"), 16)
                if not DO_LN2V:
                    continue
                g.wait_ge(S("apl"), tc * 32 + 32)
                g.dma_start(out=agx_in[tc], in_=xln[:, :]).then_inc(S("st_xln2"), 16)
                g.wait_ge(S("st_xln2"), (tc + 1) * 16)
                g.collective_compute("AllGather", ALU.bypass,
                                     replica_groups=[list(range(NCORES))],
                                     ins=[agx_in[tc].opt()], outs=[agx[tc].opt()]
                                     ).then_inc(S("ag"), 1)
                g.wait_ge(S("vev"), tc * 16 + 16)
                g.dma_start(out=agv_in[tc], in_=xch[:, :]).then_inc(S("st_agv"), 16)
                g.wait_ge(S("st_agv"), (tc + 1) * 16)
                g.collective_compute("AllGather", ALU.bypass,
                                     replica_groups=[list(range(NCORES))],
                                     ins=[agv_in[tc].opt()], outs=[agv[tc].opt()]
                                     ).then_inc(S("ag"), 1)

        if MK_PHASES >= 3 and MK_CUT == 0:
            attp = _attn_schedule()
            SCHED = attp["sched"]
            PKB = attp["packed_kbs"]
            NMASK = attp["masks"].shape[0]
            PP0 = 0
            KD0 = 1536
            VD0 = 3712
            VW0 = 5905
            Qsb = [inter[:, 0:8192], inter[:, 8192:16384]]
            Ksb = [inter[:, 16384:24576], inter[:, 24576:32768]]
            wq_sb = xln[:, 0:4096]
            wk_sb = xln[:, 4096:8192]
            COS = scr[:, 0:512]; SIN = scr[:, 512:1024]
            T1 = scr[:, 1024:1536]; QS = scr[:, 1536:2048]

            tile_tab = {}
            for s in range(2):
                for b in range(2):
                    u = s * 2 + b
                    for qc in range(16):
                        sc_ = SCHED[qc]
                        tl = []
                        npk = sc_["n_pk_tiles"]
                        for tau in range(npk):
                            mid = sc_["pk_bnd"] if (tau == npk - 1 and
                                                    sc_["n_pk_tok"] % 128 == 64) else None
                            tl.append(dict(kind="pk", idx=tau, mid=mid,
                                           qs=[True] * 4))
                        for (wt, mid, qsub) in sc_["wtiles"]:
                            tl.append(dict(kind="win", idx=wt, mid=mid, qs=qsub))
                        tile_tab[(u, qc)] = tl
            win_first = {}; win_last = {}
            for qc in range(16):
                for t_ in tile_tab[(0, qc)]:
                    if t_["kind"] == "win":
                        wt = t_["idx"]
                        win_first.setdefault(wt, qc)
                        win_last[wt] = qc
            new_tiles = {qc: sorted(wt for wt, f in win_first.items() if f == qc)
                         for qc in range(16)}
            for wt in win_first:
                if wt >= 16:
                    assert win_last[wt - 16] < win_first[wt], (wt,)
            for qc in range(16):
                wl = [t_["idx"] % 16 for t_ in tile_tab[(0, qc)]
                      if t_["kind"] == "win"]
                assert len(set(wl)) == len(wl), (qc,)
            smm_ord = {}; msk_ord = {}; exp_ord = {}; ctm_ord = {}
            vwn_cnt = {}; vdl_cnt = {}
            n_smm = n_msk = n_exp = n_ctm = n_vdl = 0
            n_vw = [0, 0, 0, 0]
            for s in range(2):
                for b in range(2):
                    u = s * 2 + b
                    n_vdl += 33
                    vdl_cnt[u] = n_vdl
                    for qc in range(16):
                        tl = tile_tab[(u, qc)]
                        n_vw[qc % 4] += len(new_tiles[qc])
                        vwn_cnt[(u, qc)] = n_vw[qc % 4]
                        for ti, t_ in enumerate(tl):
                            n_smm += 1
                            smm_ord[(u, qc, ti)] = n_smm
                            if t_["mid"] is not None:
                                n_msk += 1
                                msk_ord[(u, qc, ti)] = n_msk
                            n_exp += 1
                            exp_ord[(u, qc, ti)] = n_exp
                        for qs in range(4):
                            n_ctm += 1
                            ctm_ord[(u, qc, qs)] = n_ctm
            TOT_SMM, TOT_EXP, TOT_CTM = n_smm, n_exp, n_ctm
            ctx_fl = {}
            for key, tl in tile_tab.items():
                u, qc = key
                for qs in range(4):
                    idxs = [ti for ti, t_ in enumerate(tl) if t_["qs"][qs]]
                    ctx_fl[(u, qc, qs)] = (idxs[0], idxs[-1], idxs)
            ctp_fence = {}
            _ctp_run = 0
            for s in range(2):
                for b in range(2):
                    u = s * 2 + b
                    for qc in range(16):
                        tl = tile_tab[(u, qc)]
                        for ti, t_ in enumerate(tl):
                            o = exp_ord[(u, qc, ti)]
                            inc_qs = [qs for qs in range(4) if t_["qs"][qs]]
                            endings = [qs for qs in inc_qs
                                       if ctx_fl[(u, qc, qs)][1] == ti]
                            if inc_qs and inc_qs[-1] in endings:
                                ctp_fence[o] = ("ctm", ctm_ord[(u, qc, inc_qs[-1])])
                            else:
                                _ctp_run += 1
                                ctp_fence[o] = ("ctp", _ctp_run)

            def vsrc_col(t_, i_win):
                if t_["kind"] == "pk":
                    return VD0 + t_["idx"] * 129
                return VW0 + i_win * 129

            def win_rows(s, wt):
                g = wt * 128
                rv = 4 * s + g // 2048
                tlc = (g % 2048)
                j = tlc // 512
                tt = (tlc % 512) // 128
                return rv, j, tt

            rope_ord = lambda s, tcg, b, qk: (s * 16 + tcg) * 4 + b * 2 + qk

            ACUT = int(os.environ.get("MK_ACUT", "0")) or 99
            A_NEG = ACUT >= 2
            A_STG = ACUT >= 3
            A_SCR = ACUT >= 4
            A_CTX = ACUT >= 5
            A_TRN = ACUT >= 6
            A_EVA = ACUT >= 7
            A_STC = ACUT >= 8
            A_AGC = ACUT >= 9

            pid = nc.partition_id()
            hv_b = [pid * 2, pid * 2 + 1]

            @block.sync
            def _(sy):
                sy.wait_ge(S("vmm"), 64)
                sy.wait_ge(S("st_xln2"), 64)
                sy.wait_ge(S("cast"), 96)
                for ht in range(HT):
                    sy.dma_start(out=wq_sb[:, ht * 256:(ht + 1) * 256],
                                 in_=wq_bf[ht * 128:(ht + 1) * 128, :]).then_inc(S("wqk"), 16)
                    sy.dma_start(out=wk_sb[:, ht * 256:(ht + 1) * 256],
                                 in_=wk_bf[ht * 128:(ht + 1) * 128, :]).then_inc(S("wqk"), 16)
                sy.wait_ge(S("st_agv"), 64)
                sy.wait_ge(S("vev"), 64)
                sy.dma_start(out=xch[:, 0:NMASK * 512], in_=din["amask"]
                             ).then_inc(S("amk"), 16)
                for s in range(2):
                    for tcg in range(16):
                        o = s * 16 + tcg
                        if s == 1 and tcg <= 1 and A_CTX:
                            sy.wait_ge(S("ctm"), ctm_ord[(1, 15, 3)])
                        if o >= 2:
                            sy.wait_ge(S("qkm"), o * 4 - 4)
                        rg, j = 4 * s + tcg // 4, tcg % 4
                        sy.wait_ge(S("ag"), 4 + 2 * j)
                        axs = S("ax0") if o % 2 == 0 else S("ax1")
                        sy.dma_start(out=arena[:, (o % 2) * 8192:(o % 2 + 1) * 8192],
                                     in_=agx[j][rg * 128:(rg + 1) * 128, :]
                                     ).then_inc(axs, 16)
                        if o >= 1:
                            sy.wait_ge(S("radd"), o * 4)
                        sy.dma_start(out=COS, in_=din["ropec"][:, tcg * 512:(tcg + 1) * 512]
                                     ).then_inc(S("rop"), 16)
                        sy.dma_start(out=SIN, in_=din["ropes"][:, tcg * 512:(tcg + 1) * 512]
                                     ).then_inc(S("rop"), 16)
                    for b in range(2):
                        if not A_STG:
                            continue
                        u = s * 2 + b
                        sy.wait_ge(S("radd"), (s + 1) * 64)
                        if u > 0 and A_CTX:
                            sy.wait_ge(S("ctm"), ctm_ord[(u - 1, 15, 3)])
                        sy.wait_ge(S("msv"), u + 1)
                        sy.wait_ge(S("ag"), 11)
                        for j in range(4):
                            sy.dma_start(
                                out=vstage[j * 512:(j + 1) * 512, :].rearrange(
                                    "r (tt d) -> r tt d", tt=4),
                                in_=agv[j][4 * s * 128:(4 * s + 4) * 128, :].rearrange(
                                    "r (tt d) -> r tt d", tt=4)[:, :, bass.ts(hv_b[b], 128)],
                                bounds_check=None
                            ).then_inc(S("vst"), 16)
                        sy.wait_ge(S("vst"), 16 * 4 * (u + 1))
                        for m, kb in enumerate(PKB):
                            g0 = kb * 64
                            rv_l = (g0 // 2048)
                            tlc = g0 % 2048
                            j = tlc // 512
                            tt = (tlc % 512) // 128
                            prow = tlc % 128
                            row0 = j * 512 + rv_l * 128 + prow
                            drow = (m % 2) * 64
                            dst = arena[drow:drow + 64,
                                        VD0 + (m // 2) * 129 + 0:VD0 + (m // 2) * 129 + 128]
                            sy.dma_start(out=dst,
                                         in_=vstage[row0:row0 + 64,
                                                    tt * 128:(tt + 1) * 128]
                                         ).then_inc(S("vdl"), 16)
                        for qc in range(16):
                            for wt in new_tiles[qc]:
                                if wt >= 16 and A_CTX:
                                    sy.wait_ge(S("ctm"),
                                               ctm_ord[(u, win_last[wt - 16], 3)])
                                rv, j, tt = win_rows(s, wt)
                                rv_l = rv - 4 * s
                                row0 = j * 512 + rv_l * 128
                                sl16 = wt % 16
                                dst = arena[:, VW0 + sl16 * 129:VW0 + sl16 * 129 + 128]
                                sy.dma_start(out=dst,
                                             in_=vstage[row0:row0 + 128,
                                                        tt * 128:(tt + 1) * 128]
                                             ).then_inc(S(f"vw{qc % 4}"), 16)

            @block.vector
            def _(v):
                v.wait_ge(S("stat"), 8)
                v.memset(small[0:1, 0:16], 0.0).then_inc(S("nrm"), 1)
                n_rcp = 0
                for s in range(2):
                    for tcg in range(16):
                        o = s * 16 + tcg
                        for b in range(2):
                            for qk in range(2):
                                R = rope_ord(s, tcg, b, qk)
                                v.wait_ge(S("qkm"), o * 4 + b * 2 + qk + 1)
                                v.wait_ge(S("rop"), 32 * (o + 1))
                                if o == 0 and b == 0 and qk == 0:
                                    v.wait_ge(S("dn"), 64)
                                pb = psum[PG] if qk == 0 else psum[PU]
                                dst = (Qsb if qk == 0 else Ksb)[b][:, tcg * 512:(tcg + 1) * 512]
                                v.tensor_tensor(T1, pb[:, :], COS, ALU.mult)
                                v.tensor_copy(QS[0:64, :], pb[64:128, :])
                                v.tensor_copy(QS[64:128, :], pb[0:64, :]
                                              ).then_inc(S("rp2"), 1)
                                v.wait_ge(S("rp2"), 2 * R + 1)
                                v.tensor_tensor(QS, QS, SIN, ALU.mult
                                                ).then_inc(S("rp2"), 1)
                                v.wait_ge(S("rp2"), 2 * R + 2)
                                v.tensor_tensor(dst, T1, QS, ALU.add).then_inc(S("radd"), 1)
                                v.wait_ge(S("nmm"), R + 1)
                                v.tensor_reduce(small[0:1, 30:31], psum[PD][0:1, :],
                                                mybir.AxisListType.X, ALU.max)
                                col = (0 if qk == 0 else 4) + s * 2 + b
                                v.tensor_tensor(small[0:1, col:col + 1],
                                                small[0:1, col:col + 1],
                                                small[0:1, 30:31], ALU.max).then_inc(S("nrm"), 1)
                    for b in range(2):
                        if not A_NEG:
                            continue
                        u = s * 2 + b
                        v.wait_ge(S("nrm"), 1 + (s + 1) * 64)
                        v.tensor_tensor(small[0:1, 8 + u:9 + u], small[0:1, u:u + 1],
                                        small[0:1, 4 + u:5 + u], ALU.mult).then_inc(S("ngv"), 1)
                    for b in range(2):
                        if not A_STG:
                            continue
                        u = s * 2 + b
                        v.wait_ge(S("radd"), (s + 1) * 64)
                        if u > 0 and A_CTX:
                            v.wait_ge(S("ctm"), ctm_ord[(u - 1, 15, 3)])
                        for τ in range(17):
                            v.memset(arena[:, VD0 + τ * 129 + 128:VD0 + τ * 129 + 129], 1.0)
                        for i_w in range(16):
                            v.memset(arena[:, VW0 + i_w * 129 + 128:VW0 + i_w * 129 + 129], 1.0)
                        v.memset(arena[64:128, VD0 + 16 * 129:VD0 + 16 * 129 + 128], 0.0)
                        v.memset(arena[:, KD0 + 2112:KD0 + 2176], 0.0)
                        for m, kb in enumerate(PKB):
                            cpi = v.tensor_copy(arena[:, KD0 + m * 64:KD0 + (m + 1) * 64],
                                                Ksb[b][:, kb * 64:(kb + 1) * 64])
                        cpi.then_inc(S("msv"), 1)
                        for qc in range(16):
                            tl = tile_tab[(u, qc)]
                            for ti, t_ in enumerate(tl):
                                if t_["mid"] is None or not A_SCR:
                                    continue
                                bank = psum[(0, 1, 6)[smm_ord[(u, qc, ti)] % 3]]
                                v.wait_ge(S("smm"), smm_ord[(u, qc, ti)])
                                v.wait_ge(S("amk"), 16)
                                v.tensor_tensor(bank[:, :], bank[:, :],
                                                xch[:, t_["mid"] * 512:(t_["mid"] + 1) * 512],
                                                ALU.add).then_inc(S("msk"), 1)
                            for qs in range(4):
                                if not A_CTX:
                                    continue
                                od = ctm_ord[(u, qc, qs)]
                                v.wait_ge(S("ctm"), od)
                                if od > 4 and A_TRN:
                                    v.wait_ge(S("ctt"), od - 4)
                                v.reciprocal(small[:, 20 + qs:21 + qs],
                                             psum[2 + qs][:, 128:129]
                                             ).then_inc(S("rc2"), 1)
                                v.wait_ge(S("rc2"), od)
                                v.tensor_scalar(xbx[:, 1024 + (od % 4) * 128:
                                                    1024 + (od % 4) * 128 + 128],
                                                psum[2 + qs][:, 0:128],
                                                small[:, 20 + qs:21 + qs], None,
                                                ALU.mult).then_inc(S("rcp"), 1)

            @block.scalar
            def _(sc):
                for s in range(2):
                    for tcg in range(16):
                        for b in range(2):
                            for qk in range(2):
                                R = rope_ord(s, tcg, b, qk)
                                sc.wait_ge(S("radd"), R + 1)
                                if R >= 2:
                                    sc.wait_ge(S("nmm"), R - 1)
                                dst = (Qsb if qk == 0 else Ksb)[b][:, tcg * 512:(tcg + 1) * 512]
                                sc.activation(xbx[:, (R % 2) * 512:(R % 2 + 1) * 512],
                                              dst, AF.Square).then_inc(S("sqn"), 1)
                    for b in range(2):
                        if not A_NEG:
                            continue
                        u = s * 2 + b
                        sc.wait_ge(S("ngv"), u + 1)
                        sc.activation(small[0:1, 8 + u:9 + u], small[0:1, 8 + u:9 + u],
                                      AF.Ln, bias=eps_t[0:1, :])
                        sc.activation(small[0:1, 8 + u:9 + u], small[0:1, 8 + u:9 + u],
                                      AF.Exp, scale=0.5).then_inc(S("ngs"), 1)
                        sc.wait_ge(S("ngm"), u + 1)
                        sc.activation(small[:, 16 + u:17 + u], psum[PD][:, 0:1],
                                      AF.Copy, scale=-SCALE).then_inc(S("ngc"), 1)
                    for b in range(2):
                        u = s * 2 + b
                        if A_NEG:
                            sc.wait_ge(S("ngc"), u + 1)
                        for qc in range(16):
                            tl = tile_tab[(u, qc)]
                            for ti, t_ in enumerate(tl):
                                if not A_SCR:
                                    continue
                                o = exp_ord[(u, qc, ti)]
                                sc.wait_ge(S("smm"), smm_ord[(u, qc, ti)])
                                if t_["mid"] is not None:
                                    sc.wait_ge(S("msk"), msk_ord[(u, qc, ti)])
                                if o > 3 and A_CTX:
                                    fs, fv = ctp_fence[o - 3]
                                    sc.wait_ge(S(fs), fv)
                                sc.activation(arena[:, (o % 3) * 512:(o % 3 + 1) * 512],
                                              psum[(0, 1, 6)[smm_ord[(u, qc, ti)] % 3]][:, :], AF.Exp,
                                              bias=small[:, 16 + u:17 + u],
                                              scale=SCALE).then_inc(S("sexp"), 1)
                            def emit_evac(qcX):
                                for qs in range(4):
                                    if not A_EVA:
                                        continue
                                    od = ctm_ord[(u, qcX, qs)]
                                    sc.wait_ge(S("ctt"), od)
                                    if od > 2 and A_STC:
                                        sc.wait_ge(S("st_ctx"), 16 * (od - 2))
                                    sc.copy(q_hl[:, (od % 2) * 128:(od % 2) * 128 + 128],
                                            psbf[:, (od % 2) * 128:(od % 2) * 128 + 128]
                                            ).then_inc(S("cte"), 1)
                            if qc > 0:
                                emit_evac(qc - 1)
                            if qc == 15:
                                emit_evac(15)

            @block.tensor
            def _(t):
                t.wait_ge(S("wqk"), 32 * 16)
                t.wait_ge(S("vev"), 64)
                t.wait_ge(S("int_"), 256)
                t.wait_ge(S("res"), 64)
                t.wait_ge(S("rowv"), 8)
                t.wait_ge(S("apl"), 128)
                t.wait_ge(S("dn"), 64)
                for s in range(2):
                    if s == 1 and A_CTX:
                        t.wait_ge(S("rcp"), ctm_ord[(1, 15, 3)])
                    for tcg in range(16):
                        o = s * 16 + tcg
                        axs = S("ax0") if o % 2 == 0 else S("ax1")
                        t.wait_ge(axs, 16 * (o // 2 + 1))
                        slab = arena[:, (o % 2) * 8192:(o % 2 + 1) * 8192]
                        for b in range(2):
                            for qk in range(2):
                                R = rope_ord(s, tcg, b, qk)
                                if R >= 2:
                                    t.wait_ge(S("radd"), R - 1)
                                w_sb = wq_sb if qk == 0 else wk_sb
                                pb = psum[PG] if qk == 0 else psum[PU]
                                for ht in range(HT):
                                    mmi = t.matmul(pb[:, :],
                                                   w_sb[:, ht * 256 + b * 128:
                                                        ht * 256 + b * 128 + 128],
                                                   slab[:, ht * 512:(ht + 1) * 512],
                                                   start=(ht == 0), stop=(ht == HT - 1))
                                mmi.then_inc(S("qkm"), 1)
                                t.wait_ge(S("sqn"), R + 1)
                                if R >= 1:
                                    t.wait_ge(S("nrm"), 1 + R)
                                if R == 64 and A_NEG:
                                    t.wait_ge(S("ngc"), 2)
                                t.matmul(psum[PD][0:1, :], ones_bf[:, :],
                                         xbx[:, (R % 2) * 512:(R % 2 + 1) * 512],
                                         start=True, stop=True).then_inc(S("nmm"), 1)
                    for b in range(2):
                        if not A_NEG:
                            continue
                        u = s * 2 + b
                        t.wait_ge(S("ngs"), u + 1)
                        t.wait_ge(S("nrm"), 1 + (s + 1) * 64)
                        t.matmul(psum[PD][:, 0:1], ones1f[:, :],
                                 small[0:1, 8 + u:9 + u],
                                 start=True, stop=True).then_inc(S("ngm"), 1)
                    t.wait_ge(S("radd"), (s + 1) * 64)
                    if A_NEG:
                        t.wait_ge(S("ngc"), 2 * (s + 1))
                    for b in range(2):
                        if not A_STG:
                            continue
                        u = s * 2 + b
                        t.wait_ge(S("msv"), u + 1)
                        for qc in range(16):
                            tl = tile_tab[(u, qc)]
                            wins_before = {}
                            iw = 0
                            for ti, t_ in enumerate(tl):
                                if t_["kind"] == "win":
                                    wins_before[ti] = iw
                                    iw += 1
                            def emit_ctx(ti):
                                t_ = tl[ti]
                                o = exp_ord[(u, qc, ti)]
                                t.wait_ge(S("sexp"), o)
                                last_mmi = None
                                if t_["kind"] == "pk":
                                    t.wait_ge(S("vdl"), 16 * 33 * (u + 1))
                                    vc = VD0 + t_["idx"] * 129
                                else:
                                    t.wait_ge(S(f"vw{qc % 4}"), 16 * vwn_cnt[(u, qc)])
                                    vc = VW0 + (t_["idx"] % 16) * 129
                                for qs in range(4):
                                    if not t_["qs"][qs]:
                                        continue
                                    fi, la, _ = ctx_fl[(u, qc, qs)]
                                    if ti == fi:
                                        prev = (u, qc - 1, qs) if qc > 0 else \
                                               ((u - 1, 15, qs) if u > 0 else None)
                                        if prev is not None:
                                            t.wait_ge(S("rcp"), ctm_ord[prev])
                                    mmi = t.matmul(psum[2 + qs][:, 0:129],
                                                   arena[:, (o % 3) * 512 + qs * 128:
                                                          (o % 3) * 512 + qs * 128 + 128],
                                                   arena[:, vc:vc + 129],
                                                   start=(ti == fi), stop=(ti == la))
                                    last_mmi = mmi
                                    if ti == la:
                                        mmi.then_inc(S("ctm"), 1)
                                if last_mmi is not None and \
                                        ctp_fence[o][0] == "ctp":
                                    last_mmi.then_inc(S("ctp"), 1)
                            for ti, t_ in enumerate(tl):
                                if not A_SCR:
                                    continue
                                smo = smm_ord[(u, qc, ti)]
                                if smo > 3:
                                    t.wait_ge(S("sexp"), smo - 3)
                                if t_["kind"] == "pk":
                                    ksl = arena[:, KD0 + t_["idx"] * 128:
                                                KD0 + (t_["idx"] + 1) * 128]
                                else:
                                    ksl = Ksb[b][:, t_["idx"] * 128:(t_["idx"] + 1) * 128]
                                t.matmul(psum[(0, 1, 6)[smo % 3]][:, :], ksl,
                                         Qsb[b][:, qc * 512:(qc + 1) * 512],
                                         start=True, stop=True).then_inc(S("smm"), 1)
                                if ti >= 1 and A_CTX:
                                    emit_ctx(ti - 1)
                            if A_CTX:
                                emit_ctx(len(tl) - 1)
                            def emit_trn(qcX):
                                for qs in range(4):
                                    if not A_TRN:
                                        continue
                                    od = ctm_ord[(u, qcX, qs)]
                                    t.wait_ge(S("rcp"), od)
                                    if od > 1 and A_EVA:
                                        t.wait_ge(S("cte"), od - 1)
                                    t.transpose(psbf[:, (od % 2) * 128:(od % 2) * 128 + 128],
                                                xbx[:, 1024 + (od % 4) * 128:
                                                    1024 + (od % 4) * 128 + 128],
                                                ident[:, :]).then_inc(S("ctt"), 1)
                            if qc > 0:
                                emit_trn(qc - 1)
                            if qc == 15:
                                emit_trn(15)

            @block.gpsimd
            def _(g):
                for s in range(2):
                    for b in range(2):
                        if not A_STC:
                            continue
                        u = s * 2 + b
                        for qc in range(16):
                            for qs in range(4):
                                od = ctm_ord[(u, qc, qs)]
                                g.wait_ge(S("cte"), od)
                                g.dma_start(out=agc_in[u][:, qc * 512 + qs * 128:
                                                          qc * 512 + qs * 128 + 128],
                                            in_=q_hl[:, (od % 2) * 128:(od % 2) * 128 + 128]
                                            ).then_inc(S("st_ctx"), 16)
                        if not A_AGC:
                            continue
                        g.wait_ge(S("st_ctx"), 16 * 64 * (u + 1))
                        g.collective_compute("AllGather", ALU.bypass,
                                             replica_groups=[list(range(NCORES))],
                                             ins=[agc_in[u].opt()],
                                             outs=[agc[u * 1024:(u + 1) * 1024, :].opt()]
                                             ).then_inc(S("ag"), 1)

        if MK_PHASES >= 4 and MK_CUT == 0:
            pid4 = nc.partition_id()
            iu_b = [(pid4 // 4) * 2, (pid4 // 4) * 2 + 1]
            col_tc = [(pid4 % 4) * 4 + t_ for t_ in range(NCH)]
            CXB = 8192

            @block.sync
            def _(sy):
                sy.wait_ge(S("ag"), 15)
                for tc in range(NCH):
                    if tc > 0:
                        sy.wait_ge(S("st_out"), tc * 16)
                        sy.wait_ge(S("xb"), 128 + 16 * tc)
                        sy.wait_ge(S("apl"), 128 + 16 * tc)
                    sy.dma_start(out=xch[:, :],
                                 in_=resid[:, tc * HT * TCH:(tc + 1) * HT * TCH]
                                 ).then_inc(S("x"), 16)
                    if tc == 0:
                        from concourse.ap import AP as _BAP
                        for b in range(2):
                            off = iu_b[b] * (1024 * SEQ) + (pid4 % 4) * TPC
                            sy.dma_start(
                                out=cstage[b * 1024:(b + 1) * 1024, :],
                                in_=_BAP(agc_h, off, [[SEQ, 1024], [1, TPC]]),
                                bounds_check=None
                            ).then_inc(S("cst"), 16)
                        sy.wait_ge(S("cst"), 32)
                    if tc > 0:
                        sy.wait_ge(S("opj"), 16 * tc)
                    for b in range(2):
                        sy.dma_start(
                            out=inter[:, CXB + b * 4096:CXB + (b + 1) * 4096].rearrange(
                                "p (r c) -> p r c", r=8),
                            in_=cstage[b * 1024:(b + 1) * 1024,
                                       tc * 512:(tc + 1) * 512].rearrange(
                                "(r d) c -> d r c", d=128)
                        ).then_inc(S("cxi"), 16)
                    for ct in range(NCT):
                        wos = S("wo0") if ct % 2 == 0 else S("wo1")
                        if ct >= 2:
                            sy.wait_ge(S("opj"), tc * 16 + ct - 1)
                        sy.dma_start(out=arena[:, (ct % 2) * 2048:(ct % 2 + 1) * 2048],
                                     in_=wo_bf[:, ct * 2048:(ct + 1) * 2048]
                                     ).then_inc(wos, 16)
                    sy.wait_ge(S("opj"), 16 * (tc + 1))
                    tw = 4 + tc
                    for it in range(NIT):
                        for gu, w in ((0, "g"), (1, "u")):
                            base = 0 if gu == 0 else 4096
                            slot = base + (it % 2) * 2048
                            wsl = S(f"w{'g' if gu == 0 else 'u'}{it % 2}")
                            if it >= 2:
                                sy.wait_ge(S("gu"), 512 + tc * 128 + 2 * (it - 2) + gu + 1)
                            r, il = it // 8, it % 8
                            sy.dma_start(out=arena[:, slot:slot + 2048],
                                         in_=wfull[w][r * 128:(r + 1) * 128,
                                                      il * 2048:(il + 1) * 2048]
                                         ).then_inc(wsl, 16)
                    sy.wait_ge(S("gu"), 512 + tc * 128 + 128)
                    for ct in range(NCT):
                        slot = (ct % 2) * 8192
                        wsl = S(f"wd{ct % 2}")
                        if ct >= 2:
                            sy.wait_ge(S("dn"), 64 + tc * 16 + ct - 1)
                        r, ctl = ct // 2, ct % 2
                        sy.dma_start(out=arena[:, slot:slot + 8192],
                                     in_=wfull["d"][r * 128:(r + 1) * 128,
                                                    ctl * 8192:(ctl + 1) * 8192]
                                     ).then_inc(wsl, 16)

            @block.scalar
            def _(sc):
                sc.wait_ge(S("ag"), 15)
                for tc in range(NCH):
                    lnn = 8 + tc
                    sc.wait_ge(S("x"), (4 + tc + 1) * 16)
                    for ht in range(HT):
                        sc.wait_ge(S("res2"), tc * 16 + ht + 1)
                        sc.copy(xln[:, xcol(ht)], xch[:, xcol(ht)])
                        sc.activation(inter[:, ht * 512:(ht + 1) * 512],
                                      xch[:, xcol(ht)], AF.Square).then_inc(S("xb"), 1)
                    sc.wait_ge(S("rowv"), lnn + 1)
                    sc.activation(scr[0:1, 1536:2048], scr[0:1, 1024:1536],
                                  AF.Ln, bias=eps_t[0:1, :])
                    sc.activation(scr[0:1, 1536:2048], scr[0:1, 1536:2048],
                                  AF.Exp, scale=-0.5).then_inc(S("rows"), 1)
                    for it in range(NIT):
                        sc.wait_ge(S("gu"), 512 + tc * 128 + 2 * it + 1)
                        sl = it % 2
                        sc.activation(xbx[:, sl * 512:(sl + 1) * 512], psum[PG][:, :],
                                      AF.Silu).then_inc(S("silu"), 1)

            @block.tensor
            def _(t):
                t.wait_ge(S("ag"), 15)
                for tc in range(NCH):
                    lnn = 8 + tc
                    t.wait_ge(S("cxi"), 32 * (tc + 1))
                    for ct in range(NCT):
                        wos = S("wo0") if ct % 2 == 0 else S("wo1")
                        t.wait_ge(wos, 16 * (tc * 8 + ct // 2 + 1))
                        if ct == 0:
                            t.wait_ge(S("res"), 64 + 16 * tc)
                        t.wait_ge(S("res2"), tc * 16 + ct)
                        for b in range(2):
                            for r in range(8):
                                ht2 = b * 8 + r
                                mmi = t.matmul(
                                    psum[PD][:, :],
                                    arena[:, (ct % 2) * 2048 + ht2 * 128:
                                           (ct % 2) * 2048 + ht2 * 128 + 128],
                                    inter[:, CXB + b * 4096 + r * 512:
                                           CXB + b * 4096 + (r + 1) * 512],
                                    start=(ht2 == 0), stop=(ht2 == 15))
                        mmi.then_inc(S("opj"), 1)
                    if True:
                        t.wait_ge(S("rowv"), lnn)
                        t.wait_ge(S("xb"), 128 + tc * 16 + 16)
                        for ht in range(HT):
                            t.matmul(psum[PS1][0:1, :], ones_bf[:, :], xln[:, xcol(ht)],
                                     start=(ht == 0), stop=(ht == HT - 1))
                        for ht in range(HT):
                            mmi = t.matmul(psum[PS2][0:1, :], ones_bf[:, :],
                                           inter[:, ht * 512:(ht + 1) * 512],
                                           start=(ht == 0), stop=(ht == HT - 1))
                        mmi.then_inc(S("stat"), 1)
                        t.wait_ge(S("rows"), lnn + 1)
                        t.wait_ge(S("apl"), 128 + 16 * tc)
                        t.matmul(psum[PM][:, :], ones1f[:, :], scr[0:1, 512 + 0:512 + 512],
                                 start=True, stop=True)
                        t.matmul(psum[PR][:, :], ones1f[:, :], scr[0:1, 1536:2048],
                                 start=True, stop=True).then_inc(S("bc"), 1)
                    t.wait_ge(S("apl"), 128 + 16 * tc + 16)
                    for it in range(NIT):
                        for gu, bank in ((0, PG), (1, PU)):
                            wsl = S(f"w{'g' if gu == 0 else 'u'}{it % 2}")
                            t.wait_ge(wsl, 16 * ((4 + tc) * 32 + it // 2 + 1))
                            if it >= 1:
                                if gu == 0:
                                    t.wait_ge(S("silu"), 256 + tc * 64 + it)
                                else:
                                    t.wait_ge(S("int_"), 256 + tc * 64 + it)
                            base = 0 if gu == 0 else 4096
                            slot = base + (it % 2) * 2048
                            for hb in range(HT):
                                mmi = t.matmul(psum[bank][:, :],
                                               arena[:, slot + hb * 128:slot + (hb + 1) * 128],
                                               xln[:, xcol(hb)],
                                               start=(hb == 0), stop=(hb == HT - 1))
                            mmi.then_inc(S("gu"), 1)
                    for ct in range(NCT):
                        wsl = S(f"wd{ct % 2}")
                        t.wait_ge(wsl, 16 * ((4 + tc) * 8 + ct // 2 + 1))
                        t.wait_ge(S("int_"), 256 + tc * 64 + 64)
                        t.wait_ge(S("res"), 64 + tc * 16 + ct)
                        if ct == 0:
                            t.wait_ge(S("res2"), 16 * (tc + 1))
                        slot = (ct % 2) * 8192
                        for it in range(NIT):
                            mmi = t.matmul(psum[PD][:, :],
                                           arena[:, slot + it * 128:slot + (it + 1) * 128],
                                           inter[:, it * 512:(it + 1) * 512],
                                           start=(it == 0), stop=(it == NIT - 1))
                        mmi.then_inc(S("dn"), 1)

            @block.vector
            def _(v):
                v.wait_ge(S("ag"), 15)
                for tc in range(NCH):
                    lnn = 8 + tc
                    for ct in range(NCT):
                        v.wait_ge(S("opj"), tc * 16 + ct + 1)
                        v.tensor_tensor(xch[:, xcol(ct)], psum[PD][:, :],
                                        xch[:, xcol(ct)], ALU.add).then_inc(S("res2"), 1)
                    v.wait_ge(S("stat"), lnn + 1)
                    v.tensor_scalar(scr[0:1, 512:1024], psum[PS1][0:1, :],
                                    1.0 / DM, None, ALU.mult)
                    v.tensor_scalar(scr[0:1, 1024:1536], psum[PS2][0:1, :],
                                    1.0 / DM, None, ALU.mult)
                    v.tensor_tensor(scr[0:1, 1536:2048], scr[0:1, 512:1024],
                                    scr[0:1, 512:1024], ALU.mult).then_inc(S("rv2"), 1)
                    v.wait_ge(S("rv2"), 8 + tc + 1)
                    v.tensor_tensor(scr[0:1, 1024:1536], scr[0:1, 1024:1536],
                                    scr[0:1, 1536:2048], ALU.subtract).then_inc(S("rowv"), 1)
                    v.wait_ge(S("bc"), lnn + 1)
                    for ht in range(HT):
                        v.tensor_tensor(scr[:, 0:512], xch[:, xcol(ht)], psum[PM][:, :],
                                        ALU.subtract).then_inc(S("ap2"), 1)
                        v.wait_ge(S("ap2"), 128 + tc * 16 + ht + 1)
                        v.tensor_tensor(xln[:, xcol(ht)], scr[:, 0:512], psum[PR][:, :],
                                        ALU.mult).then_inc(S("apl"), 1)
                    for it in range(NIT):
                        v.wait_ge(S("silu"), 256 + tc * 64 + it + 1)
                        v.wait_ge(S("gu"), 512 + tc * 128 + 2 * it + 2)
                        sl = it % 2
                        v.tensor_tensor(inter[:, it * 512:(it + 1) * 512],
                                        xbx[:, sl * 512:(sl + 1) * 512], psum[PU][:, :],
                                        ALU.mult).then_inc(S("int_"), 1)
                    for ct in range(NCT):
                        v.wait_ge(S("dn"), 64 + tc * 16 + ct + 1)
                        v.tensor_tensor(xch[:, xcol(ct)], psum[PD][:, :],
                                        xch[:, xcol(ct)], ALU.add).then_inc(S("res"), 1)

            @block.gpsimd
            def _(g):
                for tc in range(NCH):
                    g.wait_ge(S("res"), 64 + tc * 16 + 16)
                    g.dma_start(out=out[:, tc * HT * TCH:(tc + 1) * HT * TCH],
                                in_=xch[:, :]).then_inc(S("st_out"), 16)
                g.wait_ge(S("st_out"), 64)

        if MK_PHASES < 4 or MK_CUT != 0:
            @block.gpsimd
            def _(g):
                if NCH_E:
                    g.wait_ge(S("st_x1"), NCH_E * 16)
                g.wait_ge(S("ag"), 3 + (2 * NCH_E if MK_CUT == 0 else 0))
                g.dma_start(out=out, in_=resid if NCH_E else din["xt"]).then_inc(S("fin"), 16)
                g.wait_ge(S("fin"), 16)

    es.close()
    return nc


def kernel(**inputs):
    in_maps = _host_prep(inputs)
    nc = build_nc()
    res = run_bass_kernel_spmd(nc, in_maps, core_ids=list(range(NCORES)))
    x = np.asarray(inputs["L0_attn_x"])
    outp = np.empty_like(x)
    for c in range(NCORES):
        s, pos = c // 4, (c % 4) * TPC
        o = res.results[c]["out"]
        oT = o.reshape(128, NCH, HT, TCH).transpose(2, 0, 1, 3).reshape(DM, TPC)
        outp[s, pos:pos + TPC, :] = oT.T
    return outp


# version 15
# speedup vs baseline: 1.0475x; 1.0475x over previous
import numpy as np
import ml_dtypes
from math import comb
import contextlib
import os

import concourse.bass as bass
import concourse.mybir as mybir
from concourse.bass_utils import run_bass_kernel_spmd

F32 = mybir.dt.float32
BF16 = mybir.dt.bfloat16
U8 = mybir.dt.uint8
AF = mybir.ActivationFunctionType
ALU = mybir.AluOpType

NCORES = 8
DM = 2048
SEQ = 8192
NH = 16
HD = 128
INTER = 8192
TPC = 2048
TCH = 512
NCH = TPC // TCH
HT = DM // 128
NIT = INTER // 128
NCT = DM // 128
NCTRL = 130156
NSAMP = DM * INTER
FGRID = NSAMP // 128
FPC = FGRID // NCORES
SCALE = float(1.0 / np.sqrt(HD))
EPS = 1e-5
BLK = 64
NQB = SEQ // BLK
WINB = 16
DILB = 4
GLOB = 2
NEG = -1.0e9

_CACHE = {}
MK_PHASES = int(os.environ.get("MK_PHASES", "9"))
MK_NORACE = bool(int(os.environ.get("MK_NORACE", "0")))
MK_CUT = int(os.environ.get("MK_CUT", "0"))


def _grid_tables():
    if "grid" in _CACHE:
        return _CACHE["grid"]
    n = NSAMP
    t = np.linspace(0.0, NCTRL - 1.0, n, dtype=np.float64)
    i_arr = np.clip(np.floor(t).astype(np.int64), 0, NCTRL - 2)
    i2 = i_arr.reshape(FGRID, 128)
    t2 = t.reshape(FGRID, 128)
    IA = i2[:, 0].copy()
    IB = np.minimum(IA + 1, NCTRL - 2)
    crosses = i2[:, 127] > IA
    first_b = (i2 > IA[:, None]).argmax(axis=1)
    pstar = np.where(crosses, first_b, 128)
    u0A = t2[:, 0] - IA
    u0B = t2[:, 0] - IB
    Bm = 0.5 * np.array([[0, 2, 0, 0], [-1, 0, 1, 0], [2, -5, 4, -1], [-1, 3, -3, 1]],
                        dtype=np.float64)
    step = (NCTRL - 1.0) / (n - 1.0)

    def g_tables(I, u0):
        idx = np.stack([np.clip(I - 1 + j, 0, NCTRL - 1) for j in range(4)], axis=0)
        G = np.zeros((4, 4, FGRID), dtype=np.float64)
        for d in range(4):
            for j in range(4):
                for e in range(d, 4):
                    G[d, j] += comb(e, d) * (u0 ** (e - d)) * Bm[e, j]
        return idx.astype(np.int32), G

    idxA, GA = g_tables(IA, u0A)
    idxB, GB = g_tables(IB, u0B)
    vp = np.stack([(np.arange(128) * step) ** d for d in range(4)], axis=0)
    _CACHE["grid"] = dict(idxA=idxA, idxB=idxB, GA=GA, GB=GB, pstar=pstar, vp=vp)
    return _CACHE["grid"]


def _col_order(core, which):
    if which == "gu":
        il = np.arange(8)[:, None, None]
        hb = np.arange(HT)[None, :, None]
        iin = np.arange(128)[None, None, :]
        i_glob = (core * 8 + il) * 128 + iin
        return (i_glob * HT + hb).reshape(-1)
    ctl = np.arange(2)[:, None, None]
    it = np.arange(NIT)[None, :, None]
    hin = np.arange(128)[None, None, :]
    hout = core * 256 + ctl * 128 + hin
    return (hout * NIT + it).reshape(-1)


def _pack_f(vals):
    v = vals.reshape(32, 512)
    return np.ascontiguousarray(np.broadcast_to(v[None], (4, 32, 512)).reshape(128, 512))


def _pack_df(vals):
    return np.ascontiguousarray(vals.reshape(4, 32, 512).reshape(128, 512))


def _expansion_consts(core):
    key = ("exp", core)
    if key in _CACHE:
        return _CACHE[key]
    g = _grid_tables()
    out = {}
    for which in ("gu", "d"):
        fg = _col_order(core, which)
        scale = 0.5 if which == "d" else 1.0
        for pl, G, idx in (("A", g["GA"], g["idxA"]), ("B", g["GB"], g["idxB"])):
            Gc = G[:, :, fg] * scale
            for j in range(4):
                out[f"G_{which}_{pl}_{j}"] = _pack_df(Gc[:, j, :].astype(np.float32))
            out[f"idx_{which}_{pl}"] = idx[:, fg]
        ps = g["pstar"][fg]
        out[f"mask_{which}"] = np.ascontiguousarray(
            (np.arange(128)[:, None] >= ps[None, :]).astype(np.uint8))
    vph = g["vp"].astype(ml_dtypes.bfloat16)
    vpl = (g["vp"] - vph.astype(np.float64)).astype(ml_dtypes.bfloat16)
    out["vph"] = np.asarray(vph)
    out["vpl"] = np.asarray(vpl)
    _CACHE[key] = out
    return out


def _rope_tables():
    if "rope" in _CACHE:
        return _CACHE["rope"]
    t = np.arange(SEQ, dtype=np.float64)
    inv = 1.0 / (10000.0 ** (np.arange(0, HD, 2, dtype=np.float64) / HD))
    fr = np.outer(t, inv)
    emb = np.concatenate([fr, fr], axis=1)
    cosT = np.ascontiguousarray(np.cos(emb).T.astype(np.float32))
    sinT = np.sin(emb).T
    sg = np.ones((HD, 1)); sg[:64] = -1.0
    sinS = np.ascontiguousarray((sinT * sg).astype(np.float32))
    _CACHE["rope"] = (cosT, sinS)
    return _CACHE["rope"]


def _allowed(qb, kb):
    if kb > qb:
        return False
    return (qb - kb <= WINB) or (kb < GLOB) or (kb % DILB == 0)


def _attn_schedule():
    if "attn" in _CACHE:
        return _CACHE["attn"]
    packed_kbs = [0, 1] + list(range(4, NQB, 4))
    masks = {}
    mask_list = []

    def mask_id(bitmap):
        key = tuple(map(tuple, bitmap))
        if key not in masks:
            m = np.zeros((128, 512), dtype=np.float32)
            for kk in range(2):
                for qq in range(8):
                    if not bitmap[kk][qq]:
                        m[kk * 64:(kk + 1) * 64, qq * 64:(qq + 1) * 64] = NEG
            masks[key] = len(mask_list)
            mask_list.append(m)
        return masks[key]

    sched = []
    for qc in range(16):
        qbs = list(range(8 * qc, 8 * qc + 8))
        wlo = max(0, 8 * qc - WINB)
        pk = [kb for kb in packed_kbs if kb < wlo]
        for kb in pk:
            assert all(_allowed(qb, kb) for qb in qbs)
        n_pk_tok = len(pk) * BLK
        n_pk_tiles = (n_pk_tok + 127) // 128
        pk_bnd = mask_id([[True] * 8, [False] * 8]) if n_pk_tok % 128 == 64 else None
        wtiles = []
        kb0 = wlo - (wlo % 2)
        for kbt in range(kb0, 8 * qc + 8, 2):
            kbs = [kbt, kbt + 1]
            bm = [[_allowed(qb, kb) for qb in qbs] for kb in kbs]
            if not any(any(r) for r in bm):
                continue
            mid = None if all(all(r) for r in bm) else mask_id(bm)
            qsub = [any(bm[kk][qq] for kk in range(2) for qq in (2 * qs, 2 * qs + 1))
                    for qs in range(4)]
            wtiles.append((kbt // 2, mid, qsub))
        sched.append(dict(n_pk_tiles=n_pk_tiles, n_pk_tok=n_pk_tok, pk_bnd=pk_bnd,
                          wtiles=wtiles))
    res = dict(sched=sched,
               masks=np.stack(mask_list) if mask_list else np.zeros((1, 128, 512), np.float32),
               packed_kbs=packed_kbs)
    _CACHE["attn"] = res
    return res


def _host_prep(inputs):
    x = np.asarray(inputs["L0_attn_x"])
    wq = np.asarray(inputs["L0_attn_w_q"]); wk = np.asarray(inputs["L0_attn_w_k"])
    wv = np.asarray(inputs["L0_attn_w_v"]); wo = np.asarray(inputs["L0_attn_w_o"])
    cps = {"g": np.asarray(inputs["L0_ffn_gate_cp"]),
           "u": np.asarray(inputs["L0_ffn_up_cp"]),
           "d": np.asarray(inputs["L0_ffn_down_cp"])}

    wqT = np.ascontiguousarray(wq.T); wkT = np.ascontiguousarray(wk.T)
    wvT = wv.T.reshape(HT, 128, 4, TCH).transpose(1, 2, 0, 3).reshape(128, 4 * HT * TCH)
    wvP = np.ascontiguousarray(wvT)
    woT = wo.T
    perm = np.empty(DM, dtype=np.int64)
    for b in range(2):
        for r in range(8):
            perm[(b * 8 + r) * 128:(b * 8 + r + 1) * 128] = np.arange(
                (2 * r + b) * 128, (2 * r + b + 1) * 128)
    woP0 = woT[perm]
    woP = np.ascontiguousarray(
        woP0.reshape(HT, 128, NCT, 128).transpose(1, 2, 0, 3).reshape(128, NCT * HT * 128))

    rope_cos, rope_sin = _rope_tables()
    att = _attn_schedule()
    amask = np.ascontiguousarray(att["masks"].transpose(1, 0, 2).reshape(128, -1))

    in_maps = []
    for c in range(NCORES):
        s, pos = c // 4, (c % 4) * TPC
        xcT = x[s, pos:pos + TPC, :].T
        xp = np.ascontiguousarray(
            xcT.reshape(HT, 128, NCH, TCH).transpose(1, 2, 0, 3).reshape(128, NCH * HT * TCH))
        ec = _expansion_consts(c)
        m = {
            "xt": xp.astype(np.float32),
            "wqsl": np.ascontiguousarray(wqT[:, 2 * c * 128:(2 * c + 2) * 128]),
            "wksl": np.ascontiguousarray(wkT[:, 2 * c * 128:(2 * c + 2) * 128]),
            "wvt": wvP.astype(np.float32),
            "wop": woP.astype(np.float32),
            "ropec": rope_cos, "ropes": rope_sin, "amask": amask,
            "identin": np.asarray(np.eye(128, dtype=np.float32).astype(ml_dtypes.bfloat16)),
            "vph": ec["vph"], "vpl": ec["vpl"],
            "maskgu": ec["mask_gu"], "maskd": ec["mask_d"],
        }
        for which in ("gu", "d"):
            for pl in ("A", "B"):
                for j in range(4):
                    m[f"G{which}{pl}{j}"] = ec[f"G_{which}_{pl}_{j}"]
        for w in ("g", "u", "d"):
            which = "gu" if w != "d" else "d"
            for pl in ("A", "B"):
                idx = ec[f"idx_{which}_{pl}"]
                for j in range(4):
                    m[f"C{w}{pl}{j}"] = _pack_f(cps[w][idx[j]]).astype(np.float32)
        in_maps.append(m)
    return in_maps


WLIST = ("g", "u", "d")
XB_TC, APL_TC = 32, 32


def build_nc():
    att = _attn_schedule()
    n_amask = att["masks"].shape[0]
    assert n_amask <= 16, n_amask
    nc = bass.Bass("TRN2", target_bir_lowering=False,
                   detect_race_conditions=not MK_NORACE)

    din = {}
    def DI(name, shape, dt=F32):
        din[name] = nc.dram_tensor(name, list(shape), dt, kind="ExternalInput").ap()
        return din[name]

    DI("xt", [128, NCH * HT * TCH])
    DI("wqsl", [DM, 256]); DI("wksl", [DM, 256])
    DI("wvt", [128, 4 * HT * TCH]); DI("wop", [128, NCT * HT * 128])
    DI("ropec", [HD, SEQ]); DI("ropes", [HD, SEQ])
    DI("amask", [128, n_amask * 512])
    DI("identin", [128, 128], BF16)
    DI("vph", [4, 128], BF16); DI("vpl", [4, 128], BF16)
    DI("maskgu", [128, FPC], U8); DI("maskd", [128, FPC], U8)
    for which in ("gu", "d"):
        for pl in ("A", "B"):
            for j in range(4):
                DI(f"G{which}{pl}{j}", [128, 512])
    for w in WLIST:
        for pl in ("A", "B"):
            for j in range(4):
                DI(f"C{w}{pl}{j}", [128, 512])

    out = nc.dram_tensor("out", [128, NCH * HT * TCH], F32, kind="ExternalOutput").ap()

    resid = nc.dram_tensor("resid", [128, NCH * HT * TCH], F32).ap()
    qdram = {w: nc.dram_tensor(f"qdram_{w}", [4, 4 * FPC], BF16).ap() for w in WLIST}
    wsh = {w: nc.dram_tensor(f"wsh_{w}", [128, FPC], BF16).ap() for w in WLIST}
    wfull = {w: nc.dram_tensor(f"wfull_{w}", [NCORES * 128, FPC], BF16,
                               addr_space="Shared").ap()
             for w in WLIST}
    wq_bf = nc.dram_tensor("wq_bf", [DM, 256], BF16).ap()
    wk_bf = nc.dram_tensor("wk_bf", [DM, 256], BF16).ap()
    wv_bf = nc.dram_tensor("wv_bf", [128, 4 * HT * TCH], BF16).ap()
    wo_bf = nc.dram_tensor("wo_bf", [128, NCT * HT * 128], BF16).ap()
    agx_in = [nc.dram_tensor(f"agxin{j}", [128, HT * TCH], BF16).ap() for j in range(NCH)]
    agx = [nc.dram_tensor(f"agx{j}", [NCORES * 128, HT * TCH], BF16,
                          addr_space="Shared").ap() for j in range(NCH)]
    agv_in = [nc.dram_tensor(f"agvin{j}", [128, 4 * DM], BF16).ap() for j in range(NCH)]
    agv = [nc.dram_tensor(f"agv{j}", [NCORES * 128, 4 * DM], BF16,
                          addr_space="Shared").ap() for j in range(NCH)]
    agc_in = [nc.dram_tensor(f"agcin{u}", [128, SEQ], BF16).ap() for u in range(4)]
    vstage = nc.dram_tensor("vstage", [4 * 512, 512], BF16).ap()
    cstage_h = nc.dram_tensor("cstage", [2048, 2048], BF16)
    cstage = cstage_h.ap()
    agc_h = nc.dram_tensor("agc", [4 * NCORES * 128, SEQ], BF16,
                           addr_space="Shared")
    agc = agc_h.ap()

    es = contextlib.ExitStack()
    def SB(name, shape, dt):
        return es.enter_context(nc.sbuf_tensor(name, list(shape), dt))

    ones_bf = SB("ones_bf", [128, 1], BF16)
    ones1f = SB("ones1f", [1, 128], F32)
    ident = SB("ident", [128, 128], BF16)
    eps_t = SB("eps_t", [1, 1], F32)
    gc_area = SB("gc_area", [128, 4 * 512], F32)
    q_scr = SB("q_scr", [128, 2 * 512], F32)
    q_hl = SB("q_hl", [128, 2 * 512], BF16)
    qrows = SB("qrows", [4, 2 * 2048], BF16)
    vp_sb = SB("vp_sb", [4, 2 * 128], BF16)
    mask_sb = SB("mask_sb", [128, FPC], U8)
    wexp = SB("wexp", [128, 2 * 512], BF16)
    xch = SB("xch", [128, HT * TCH], F32)
    xbx = SB("xbx", [128, 4 * 512], BF16)
    xln = SB("xln", [128, HT * TCH], BF16)
    inter = SB("inter", [128, NIT * TCH], BF16)
    arena = SB("arena", [128, 16384], BF16)
    scr = SB("scr", [128, 2048], F32)
    small = SB("small", [128, 256], F32)

    ROW_M, ROW_V, ROW_T = 512, 1024, 1536

    psum = [es.enter_context(nc.psum_tensor(f"pb{i}", [128, 512], F32)) for i in range(7)]
    psbf = es.enter_context(nc.psum_tensor("pb7", [128, 1024], BF16))

    sem_names = [
        "ldc", "gc0", "gc1", "cast", "qst0", "qst1", "qst2", "qr0", "qr1",
        "wst0", "wst1", "x", "wg0", "wg1", "wu0", "wu1", "wd0", "wd1", "wv0",
        "vw0", "vw1", "vw2", "vw3", "wvb", "ap2", "rv2",
        "st_x1", "st_xln2", "st_agv", "fin",
        "ag",
        "ax0", "ax1", "rop", "qkm", "radd", "sqn", "nmm", "nrm", "msv",
        "ngv", "ngs", "ngm", "ngc", "vdl", "vwn", "amk", "smm", "msk",
        "sexp", "ctm", "ctp", "rcp", "rc2", "rp2", "ctt", "cte", "st_ctx", "wqk", "vst", "cst",
        "cxi", "wo0", "wo1", "opj", "res2", "st_out",
        "qprep", "mm", "ev", "ve", "xb", "stat", "rowv", "rows", "bc", "apl",
        "silu", "int_", "gu", "dn", "res", "vmm", "vev",
    ]
    sems = {n: es.enter_context(nc.semaphore(n)) for n in sem_names}
    S = lambda n: sems[n]

    PS1, PS2, PM, PR, PG, PU, PD = 0, 1, 2, 3, 4, 5, 6
    PV = 4
    VE0 = 1

    def xcol(ht, c0=0, n=TCH):
        return slice(ht * TCH + c0, ht * TCH + c0 + n)

    NCH_E = NCH if MK_PHASES >= 2 else 0
    DO_GU = MK_CUT == 0 or MK_CUT >= 2
    DO_SILU = MK_CUT in (0, 22, 2, 3)
    DO_INT = MK_CUT in (0, 2, 3)
    DO_DN = MK_CUT in (0, 3)
    DO_LN2V = MK_CUT == 0
    NLN = 2 if DO_LN2V else 1
    XBT = 16 * NLN
    APLT = 16 * NLN

    with nc.Block() as block:
        nc.cache_partition_id()
        @block.sync
        def _(sy):
            sy.dma_start(out=vp_sb[:, 0:128], in_=din["vph"]).then_inc(S("ldc"), 16)
            sy.dma_start(out=vp_sb[:, 128:256], in_=din["vpl"]).then_inc(S("ldc"), 16)
            sy.dma_start(out=mask_sb[:, :], in_=din["maskgu"]).then_inc(S("ldc"), 16)

        @block.gpsimd
        def _(g):
            g.dma_start(out=wq_bf, in_=din["wqsl"]).then_inc(S("cast"), 16)
            g.dma_start(out=wk_bf, in_=din["wksl"]).then_inc(S("cast"), 16)
            g.dma_start(out=wv_bf, in_=din["wvt"]).then_inc(S("cast"), 16)
            g.dma_start(out=wo_bf, in_=din["wop"]).then_inc(S("cast"), 16)
            g.dma_start(out=ident[:, :], in_=din["identin"]).then_inc(S("cast"), 16)

        @block.vector
        def _(v):
            v.memset(ones_bf[:, :], 1.0)
            v.memset(ones1f[:, :], 1.0)
            v.memset(eps_t[:, :], EPS).then_inc(S("ve"), 1)

        @block.sync
        def _(sy):
            for wi, w in enumerate(WLIST):
                which = "gu" if w != "d" else "d"
                for pi, pl in enumerate(("A", "B")):
                    wpl = wi * 2 + pi
                    for j in range(4):
                        gp = wpl * 4 + j
                        if gp >= 2:
                            w2, j2 = divmod(gp - 2, 4)
                            sy.wait_ge(S("qprep"), w2 * 5 + j2 + 1)
                        slot = (gp % 2) * 1024
                        gcs = S("gc0") if gp % 2 == 0 else S("gc1")
                        sy.dma_start(out=gc_area[:, slot:slot + 512],
                                     in_=din[f"G{which}{pl}{j}"]).then_inc(gcs, 16)
                        sy.dma_start(out=gc_area[:, slot + 512:slot + 1024],
                                     in_=din[f"C{w}{pl}{j}"]).then_inc(gcs, 16)

        @block.vector
        def _(v):
            for wi, w in enumerate(WLIST):
                for pi in range(2):
                    wpl = wi * 2 + pi
                    acc = q_scr[:, 0:512]; tmp = q_scr[:, 512:1024]
                    for j in range(4):
                        gp = wpl * 4 + j
                        gcs = S("gc0") if gp % 2 == 0 else S("gc1")
                        v.wait_ge(gcs, 32 * (gp // 2 + 1))
                        slot = (gp % 2) * 1024
                        if j == 0:
                            v.tensor_tensor(acc, gc_area[:, slot:slot + 512],
                                            gc_area[:, slot + 512:slot + 1024],
                                            ALU.mult).then_inc(S("qprep"), 1)
                        else:
                            v.tensor_tensor(tmp, gc_area[:, slot:slot + 512],
                                            gc_area[:, slot + 512:slot + 1024],
                                            ALU.mult).then_inc(S("qprep"), 1)
                            v.wait_ge(S("qprep"), wpl * 5 + j + 1)
                            v.tensor_tensor(acc, acc, tmp, ALU.add)
                    if wpl > 0:
                        qs_prev = S(f"qst{(wpl - 1) // 2}")
                        v.wait_ge(qs_prev, 32 * (2 if (wpl - 1) % 2 else 1))
                    v.tensor_copy(q_hl[:, 0:512], acc)
                    v.tensor_tensor(q_hl[:, 512:1024], acc, q_hl[:, 0:512],
                                    ALU.subtract).then_inc(S("qprep"), 1)

        @block.scalar
        def _(sc):
            for wi, w in enumerate(WLIST):
                qv = qdram[w].rearrange("d (pl hl f) -> d pl hl f", pl=2, hl=2)
                for pi in range(2):
                    wpl = wi * 2 + pi
                    sc.wait_ge(S("qprep"), (wpl + 1) * 5)
                    qs = S(f"qst{wi}")
                    sc.dma_start(out=qv[:, pi, 0, :].rearrange("d (b n) -> d b n", b=32),
                                 in_=q_hl[:, 0:512]).then_inc(qs, 16)
                    sc.dma_start(out=qv[:, pi, 1, :].rearrange("d (b n) -> d b n", b=32),
                                 in_=q_hl[:, 512:1024]).then_inc(qs, 16)

        @block.sync
        def _(sy):
            for wi, w in enumerate(WLIST):
                qv = qdram[w].rearrange("d (pl hl f) -> d pl hl f", pl=2, hl=2)
                sy.wait_ge(S(f"qst{wi}"), 64)
                for ch in range(32):
                    gch = wi * 32 + ch
                    if gch >= 2:
                        sy.wait_ge(S("mm"), 2 * (gch - 1))
                    slot = (gch % 2) * 2048
                    qrs = S("qr0") if gch % 2 == 0 else S("qr1")
                    for q, (pi, hl) in enumerate(((0, 0), (0, 1), (1, 0), (1, 1))):
                        sy.dma_start(out=qrows[0:4, slot + q * 512:slot + (q + 1) * 512],
                                     in_=qv[:, pi, hl, ch * 512:(ch + 1) * 512]
                                     ).then_inc(qrs, 16)

        @block.tensor
        def _(t):
            t.wait_ge(S("ldc"), 48)
            for wi, w in enumerate(WLIST):
                for ch in range(32):
                    gch = wi * 32 + ch
                    qrs = S("qr0") if gch % 2 == 0 else S("qr1")
                    t.wait_ge(qrs, 64 * (gch // 2 + 1))
                    if gch >= 1:
                        t.wait_ge(S("ve"), VE0 + gch)
                    slot = (gch % 2) * 2048
                    for bank, pi in ((0, 0), (1, 1)):
                        ah = qrows[0:4, slot + (2 * pi) * 512:slot + (2 * pi + 1) * 512]
                        al = qrows[0:4, slot + (2 * pi + 1) * 512:slot + (2 * pi + 2) * 512]
                        t.matmul(psum[bank][:, :], vp_sb[:, 0:128], ah, start=True, stop=False)
                        t.matmul(psum[bank][:, :], vp_sb[:, 128:256], ah, start=False, stop=False)
                        t.matmul(psum[bank][:, :], vp_sb[:, 0:128], al,
                                 start=False, stop=True).then_inc(S("mm"), 1)

        @block.scalar
        def _(sc):
            for gch in range(96):
                sc.wait_ge(S("mm"), 2 * gch + 1)
                if gch >= 2:
                    ws = S("wst0") if gch % 2 == 0 else S("wst1")
                    sc.wait_ge(ws, 16 * (gch // 2))
                sc.copy(wexp[:, (gch % 2) * 512:(gch % 2 + 1) * 512],
                        psum[0][:, :]).then_inc(S("ev"), 1)

        @block.vector
        def _(v):
            v.wait_ge(S("ldc"), 48)
            for wi, w in enumerate(WLIST):
                for ch in range(32):
                    gch = wi * 32 + ch
                    v.wait_ge(S("ev"), gch + 1)
                    v.wait_ge(S("mm"), 2 * gch + 2)
                    if w == "d" and ch == 0:
                        v.wait_ge(S("cast"), 96)
                    v.copy_predicated(wexp[:, (gch % 2) * 512:(gch % 2 + 1) * 512],
                                      mask_sb[:, ch * 512:(ch + 1) * 512],
                                      psum[1][:, :]).then_inc(S("ve"), 1)

        @block.gpsimd
        def _(g):
            for wi, w in enumerate(WLIST):
                if w == "d":
                    g.wait_ge(S("ve"), VE0 + 64)
                    g.dma_start(out=mask_sb[:, :], in_=din["maskd"]).then_inc(S("cast"), 16)
                for ch in range(32):
                    gch = wi * 32 + ch
                    g.wait_ge(S("ve"), VE0 + gch + 1)
                    ws = S("wst0") if gch % 2 == 0 else S("wst1")
                    g.dma_start(out=wsh[w][:, ch * 512:(ch + 1) * 512],
                                in_=wexp[:, (gch % 2) * 512:(gch % 2 + 1) * 512]
                                ).then_inc(ws, 16)
                g.wait_ge(S("wst0"), 16 * 16 * (wi + 1))
                g.wait_ge(S("wst1"), 16 * 16 * (wi + 1))
                g.collective_compute("AllGather", ALU.bypass,
                                     replica_groups=[list(range(NCORES))],
                                     ins=[wsh[w].opt()], outs=[wfull[w].opt()]
                                     ).then_inc(S("ag"), 1)

        @block.sync
        def _(sy):
            for tc in range(NCH_E):
                if tc > 0:
                    sy.wait_ge(S("st_x1"), tc * 16)
                    if DO_LN2V:
                        sy.wait_ge(S("st_agv"), tc * 16)
                    sy.wait_ge(S("xb"), XBT * tc)
                    sy.wait_ge(S("apl"), APLT * tc)
                sy.dma_start(out=xch[:, :],
                             in_=din["xt"][:, tc * HT * TCH:(tc + 1) * HT * TCH]
                             ).then_inc(S("x"), 16)
                if not DO_GU:
                    continue
                sy.wait_ge(S("ag"), 2)
                if tc > 0 and DO_LN2V:
                    sy.wait_ge(S("vmm"), tc * 16)
                for it in range(NIT):
                    for gu, w in ((0, "g"), (1, "u")):
                        base = 0 if gu == 0 else 4096
                        slot = base + (it % 2) * 2048
                        wsl = S(f"w{'g' if gu == 0 else 'u'}{it % 2}")
                        if it >= 2:
                            sy.wait_ge(S("gu"), tc * 128 + 2 * (it - 2) + gu + 1)
                        r, il = it // 8, it % 8
                        sy.dma_start(out=arena[:, slot:slot + 2048],
                                     in_=wfull[w][r * 128:(r + 1) * 128,
                                                  il * 2048:(il + 1) * 2048]
                                     ).then_inc(wsl, 16)
                if not DO_DN:
                    continue
                sy.wait_ge(S("ag"), 3)
                sy.wait_ge(S("gu"), tc * 128 + 128)
                for ct in range(NCT):
                    slot = (ct % 2) * 8192
                    wsl = S(f"wd{ct % 2}")
                    if ct >= 2:
                        sy.wait_ge(S("dn"), tc * 16 + ct - 1)
                    r, ctl = ct // 2, ct % 2
                    sy.dma_start(out=arena[:, slot:slot + 8192],
                                 in_=wfull["d"][r * 128:(r + 1) * 128,
                                                ctl * 8192:(ctl + 1) * 8192]
                                 ).then_inc(wsl, 16)
                if not DO_LN2V:
                    continue
                sy.wait_ge(S("dn"), tc * 16 + 16)
                for dc in range(4):
                    if dc >= 2:
                        sy.wait_ge(S("vmm"), tc * 16 + 4 * (dc - 1))
                    sy.dma_start(out=arena[:, (dc % 2) * 8192:(dc % 2) * 8192 + 8192],
                                 in_=wv_bf[:, dc * 8192:(dc + 1) * 8192]
                                 ).then_inc(S("wv0") if dc % 2 == 0 else S("wvb"), 16)

        @block.scalar
        def _(sc):
            for tc in range(NCH_E):
                sc.wait_ge(S("x"), (tc + 1) * 16)
                for ln in range(NLN):
                    lnn = tc * NLN + ln
                    if ln == 1:
                        sc.wait_ge(S("dn"), tc * 16 + 16)
                    elif tc > 0:
                        sc.wait_ge(S("st_xln2") if DO_LN2V else S("st_x1"), tc * 16)
                    for ht in range(HT):
                        if ln == 1:
                            sc.wait_ge(S("res"), tc * 16 + ht + 1)
                        sc.copy(xln[:, xcol(ht)], xch[:, xcol(ht)])
                        sc.activation(inter[:, ht * 512:(ht + 1) * 512],
                                      xch[:, xcol(ht)], AF.Square).then_inc(S("xb"), 1)
                    sc.wait_ge(S("rowv"), lnn + 1)
                    sc.activation(scr[0:1, ROW_T:ROW_T + 512], scr[0:1, ROW_V:ROW_V + 512],
                                  AF.Ln, bias=eps_t[0:1, :])
                    sc.activation(scr[0:1, ROW_T:ROW_T + 512], scr[0:1, ROW_T:ROW_T + 512],
                                  AF.Exp, scale=-0.5).then_inc(S("rows"), 1)
                    if ln == 0 and DO_SILU:
                        for it in range(NIT):
                            sc.wait_ge(S("gu"), tc * 128 + 2 * it + 1)
                            sl = it % 2
                            sc.activation(xbx[:, sl * 512:(sl + 1) * 512], psum[PG][:, :],
                                          AF.Silu).then_inc(S("silu"), 1)
                if DO_LN2V:
                    for dc in range(4):
                        for tt in range(4):
                            gg = tc * 16 + dc * 4 + tt
                            sc.wait_ge(S("vmm"), gg + 1)
                            if dc == 0 and tt == 0:
                                sc.wait_ge(S("st_x1"), (tc + 1) * 16)
                            sc.copy(xch[:, tt * DM + dc * 512:tt * DM + (dc + 1) * 512],
                                    psum[PV][:, :]).then_inc(S("vev"), 1)

        @block.tensor
        def _(t):
            for tc in range(NCH_E):
                if tc == 0:
                    t.wait_ge(S("ve"), VE0 + 96)
                for ln in range(NLN):
                    lnn = tc * NLN + ln
                    if lnn >= 1:
                        t.wait_ge(S("rowv"), lnn)
                    t.wait_ge(S("xb"), lnn * 16 + 16)
                    for ht in range(HT):
                        t.matmul(psum[PS1][0:1, :], ones_bf[:, :], xln[:, xcol(ht)],
                                 start=(ht == 0), stop=(ht == HT - 1))
                    for ht in range(HT):
                        mmi = t.matmul(psum[PS2][0:1, :], ones_bf[:, :],
                                       inter[:, ht * 512:(ht + 1) * 512],
                                       start=(ht == 0), stop=(ht == HT - 1))
                        if ht == HT - 1:
                            mmi.then_inc(S("stat"), 1)
                    t.wait_ge(S("rows"), lnn + 1)
                    if lnn >= 1:
                        t.wait_ge(S("apl"), lnn * 16)
                    t.matmul(psum[PM][:, :], ones1f[:, :], scr[0:1, ROW_M:ROW_M + 512],
                             start=True, stop=True)
                    t.matmul(psum[PR][:, :], ones1f[:, :], scr[0:1, ROW_T:ROW_T + 512],
                             start=True, stop=True).then_inc(S("bc"), 1)
                    if ln == 0 and not DO_GU:
                        pass
                    elif ln == 0:
                        t.wait_ge(S("apl"), lnn * 16 + 16)
                        if tc > 0 and DO_LN2V:
                            t.wait_ge(S("vev"), tc * 16)
                        for it in range(NIT):
                            for gu, bank in ((0, PG), (1, PU)):
                                wsl = S(f"w{'g' if gu == 0 else 'u'}{it % 2}")
                                t.wait_ge(wsl, 16 * (tc * 32 + it // 2 + 1))
                                if it >= 1:
                                    if gu == 0 and DO_SILU:
                                        t.wait_ge(S("silu"), tc * 64 + it)
                                    elif gu == 1 and DO_INT:
                                        t.wait_ge(S("int_"), tc * 64 + it)
                                base = 0 if gu == 0 else 4096
                                slot = base + (it % 2) * 2048
                                for hb in range(HT):
                                    mmi = t.matmul(psum[bank][:, :],
                                                   arena[:, slot + hb * 128:slot + (hb + 1) * 128],
                                                   xln[:, xcol(hb)],
                                                   start=(hb == 0),
                                                   stop=(hb == HT - 1))
                                mmi.then_inc(S("gu"), 1)
                        for ct in range(NCT if DO_DN else 0):
                            wsl = S(f"wd{ct % 2}")
                            t.wait_ge(wsl, 16 * (tc * 8 + ct // 2 + 1))
                            t.wait_ge(S("int_"), tc * 64 + 64)
                            t.wait_ge(S("res"), tc * 16 + ct)
                            slot = (ct % 2) * 8192
                            for it in range(NIT):
                                mmi = t.matmul(psum[PD][:, :],
                                               arena[:, slot + it * 128:slot + (it + 1) * 128],
                                               inter[:, it * 512:(it + 1) * 512],
                                               start=(it == 0), stop=(it == NIT - 1))
                            mmi.then_inc(S("dn"), 1)
                    else:
                        t.wait_ge(S("apl"), lnn * 16 + 16)
                        t.wait_ge(S("silu"), (tc + 1) * 64)
                        for dc in range(4):
                            t.wait_ge(S("wv0") if dc % 2 == 0 else S("wvb"),
                                      16 * (tc * 2 + dc // 2 + 1))
                            for tt in range(4):
                                gg = tc * 16 + dc * 4 + tt
                                if gg >= 1:
                                    t.wait_ge(S("vev"), gg)
                                for hb in range(HT):
                                    mmi = t.matmul(psum[PV][:, :],
                                                   xln[:, xcol(hb, tt * 128, 128)],
                                                   arena[:, (dc % 2) * 8192 + hb * 512:
                                                          (dc % 2) * 8192 + (hb + 1) * 512],
                                                   start=(hb == 0), stop=(hb == HT - 1))
                                mmi.then_inc(S("vmm"), 1)

        @block.vector
        def _(v):
            for tc in range(NCH_E):
                for ln in range(NLN):
                    lnn = tc * NLN + ln
                    v.wait_ge(S("stat"), lnn + 1)
                    v.tensor_scalar(scr[0:1, ROW_M:ROW_M + 512], psum[PS1][0:1, :],
                                    1.0 / DM, None, ALU.mult)
                    v.tensor_scalar(scr[0:1, ROW_V:ROW_V + 512], psum[PS2][0:1, :],
                                    1.0 / DM, None, ALU.mult)
                    v.tensor_tensor(scr[0:1, ROW_T:ROW_T + 512], scr[0:1, ROW_M:ROW_M + 512],
                                    scr[0:1, ROW_M:ROW_M + 512], ALU.mult
                                    ).then_inc(S("rv2"), 1)
                    v.wait_ge(S("rv2"), lnn + 1)
                    v.tensor_tensor(scr[0:1, ROW_V:ROW_V + 512], scr[0:1, ROW_V:ROW_V + 512],
                                    scr[0:1, ROW_T:ROW_T + 512],
                                    ALU.subtract).then_inc(S("rowv"), 1)
                    v.wait_ge(S("bc"), lnn + 1)
                    if ln == 0 and tc > 0:
                        v.wait_ge(S("st_xln2") if DO_LN2V else S("st_x1"), tc * 16)
                    for ht in range(HT):
                        v.tensor_tensor(scr[:, 0:512], xch[:, xcol(ht)], psum[PM][:, :],
                                        ALU.subtract).then_inc(S("ap2"), 1)
                        v.wait_ge(S("ap2"), lnn * 16 + ht + 1)
                        v.tensor_tensor(xln[:, xcol(ht)], scr[:, 0:512], psum[PR][:, :],
                                        ALU.mult).then_inc(S("apl"), 1)
                    if ln == 0 and DO_INT:
                        for it in range(NIT):
                            v.wait_ge(S("silu"), tc * 64 + it + 1)
                            v.wait_ge(S("gu"), tc * 128 + 2 * it + 2)
                            sl = it % 2
                            v.tensor_tensor(inter[:, it * 512:(it + 1) * 512],
                                            xbx[:, sl * 512:(sl + 1) * 512], psum[PU][:, :],
                                            ALU.mult).then_inc(S("int_"), 1)
                        for ct in range(NCT if DO_DN else 0):
                            v.wait_ge(S("dn"), tc * 16 + ct + 1)
                            v.tensor_tensor(xch[:, xcol(ct)], psum[PD][:, :],
                                            xch[:, xcol(ct)], ALU.add).then_inc(S("res"), 1)

        @block.gpsimd
        def _(g):
            for tc in range(NCH_E):
                if DO_DN:
                    g.wait_ge(S("res"), tc * 16 + 16)
                elif DO_INT:
                    g.wait_ge(S("int_"), tc * 64 + 64)
                elif DO_SILU:
                    g.wait_ge(S("silu"), tc * 64 + 64)
                elif DO_GU:
                    g.wait_ge(S("gu"), tc * 128 + 128)
                else:
                    g.wait_ge(S("apl"), tc * APLT + 16)
                g.dma_start(out=resid[:, tc * HT * TCH:(tc + 1) * HT * TCH], in_=xch[:, :]
                            ).then_inc(S("st_x1"), 16)
                if not DO_LN2V:
                    continue
                g.wait_ge(S("apl"), tc * 32 + 32)
                g.dma_start(out=agx_in[tc], in_=xln[:, :]).then_inc(S("st_xln2"), 16)
                g.wait_ge(S("st_xln2"), (tc + 1) * 16)
                g.collective_compute("AllGather", ALU.bypass,
                                     replica_groups=[list(range(NCORES))],
                                     ins=[agx_in[tc].opt()], outs=[agx[tc].opt()]
                                     ).then_inc(S("ag"), 1)
                g.wait_ge(S("vev"), tc * 16 + 16)
                g.dma_start(out=agv_in[tc], in_=xch[:, :]).then_inc(S("st_agv"), 16)
                g.wait_ge(S("st_agv"), (tc + 1) * 16)
                g.collective_compute("AllGather", ALU.bypass,
                                     replica_groups=[list(range(NCORES))],
                                     ins=[agv_in[tc].opt()], outs=[agv[tc].opt()]
                                     ).then_inc(S("ag"), 1)

        if MK_PHASES >= 3 and MK_CUT == 0:
            attp = _attn_schedule()
            SCHED = attp["sched"]
            PKB = attp["packed_kbs"]
            NMASK = attp["masks"].shape[0]
            PP0 = 0
            KD0 = 1536
            VD0 = 3712
            VW0 = 5905
            Qsb = [inter[:, 0:8192], inter[:, 8192:16384]]
            Ksb = [inter[:, 16384:24576], inter[:, 24576:32768]]
            wq_sb = xln[:, 0:4096]
            wk_sb = xln[:, 4096:8192]
            COS = scr[:, 0:512]; SIN = scr[:, 512:1024]
            T1 = scr[:, 1024:1536]; QS = scr[:, 1536:2048]

            tile_tab = {}
            for s in range(2):
                for b in range(2):
                    u = s * 2 + b
                    for qc in range(16):
                        sc_ = SCHED[qc]
                        tl = []
                        npk = sc_["n_pk_tiles"]
                        for tau in range(npk):
                            mid = sc_["pk_bnd"] if (tau == npk - 1 and
                                                    sc_["n_pk_tok"] % 128 == 64) else None
                            tl.append(dict(kind="pk", idx=tau, mid=mid,
                                           qs=[True] * 4))
                        for (wt, mid, qsub) in sc_["wtiles"]:
                            tl.append(dict(kind="win", idx=wt, mid=mid, qs=qsub))
                        tile_tab[(u, qc)] = tl
            win_first = {}; win_last = {}
            for qc in range(16):
                for t_ in tile_tab[(0, qc)]:
                    if t_["kind"] == "win":
                        wt = t_["idx"]
                        win_first.setdefault(wt, qc)
                        win_last[wt] = qc
            new_tiles = {qc: sorted(wt for wt, f in win_first.items() if f == qc)
                         for qc in range(16)}
            for wt in win_first:
                if wt >= 16:
                    assert win_last[wt - 16] < win_first[wt], (wt,)
            for qc in range(16):
                wl = [t_["idx"] % 16 for t_ in tile_tab[(0, qc)]
                      if t_["kind"] == "win"]
                assert len(set(wl)) == len(wl), (qc,)
            smm_ord = {}; msk_ord = {}; exp_ord = {}; ctm_ord = {}
            vwn_cnt = {}; vdl_cnt = {}
            n_smm = n_msk = n_exp = n_ctm = n_vdl = 0
            n_vw = [0, 0, 0, 0]
            for s in range(2):
                for b in range(2):
                    u = s * 2 + b
                    n_vdl += 33
                    vdl_cnt[u] = n_vdl
                    for qc in range(16):
                        tl = tile_tab[(u, qc)]
                        n_vw[qc % 4] += len(new_tiles[qc])
                        vwn_cnt[(u, qc)] = n_vw[qc % 4]
                        for ti, t_ in enumerate(tl):
                            n_smm += 1
                            smm_ord[(u, qc, ti)] = n_smm
                            if t_["mid"] is not None:
                                n_msk += 1
                                msk_ord[(u, qc, ti)] = n_msk
                            n_exp += 1
                            exp_ord[(u, qc, ti)] = n_exp
                        for qs in range(4):
                            n_ctm += 1
                            ctm_ord[(u, qc, qs)] = n_ctm
            TOT_SMM, TOT_EXP, TOT_CTM = n_smm, n_exp, n_ctm
            ctx_fl = {}
            for key, tl in tile_tab.items():
                u, qc = key
                for qs in range(4):
                    idxs = [ti for ti, t_ in enumerate(tl) if t_["qs"][qs]]
                    ctx_fl[(u, qc, qs)] = (idxs[0], idxs[-1], idxs)
            ctp_fence = {}
            _ctp_run = 0
            for s in range(2):
                for b in range(2):
                    u = s * 2 + b
                    for qc in range(16):
                        tl = tile_tab[(u, qc)]
                        for ti, t_ in enumerate(tl):
                            o = exp_ord[(u, qc, ti)]
                            inc_qs = [qs for qs in range(4) if t_["qs"][qs]]
                            endings = [qs for qs in inc_qs
                                       if ctx_fl[(u, qc, qs)][1] == ti]
                            if inc_qs and inc_qs[-1] in endings:
                                ctp_fence[o] = ("ctm", ctm_ord[(u, qc, inc_qs[-1])])
                            else:
                                _ctp_run += 1
                                ctp_fence[o] = ("ctp", _ctp_run)

            def vsrc_col(t_, i_win):
                if t_["kind"] == "pk":
                    return VD0 + t_["idx"] * 129
                return VW0 + i_win * 129

            def win_rows(s, wt):
                g = wt * 128
                rv = 4 * s + g // 2048
                tlc = (g % 2048)
                j = tlc // 512
                tt = (tlc % 512) // 128
                return rv, j, tt

            rope_ord = lambda s, tcg, b, qk: (s * 16 + tcg) * 4 + b * 2 + qk

            ACUT = int(os.environ.get("MK_ACUT", "0")) or 99
            A_NEG = ACUT >= 2
            A_STG = ACUT >= 3
            A_SCR = ACUT >= 4
            A_CTX = ACUT >= 5
            A_TRN = ACUT >= 6
            A_EVA = ACUT >= 7
            A_STC = ACUT >= 8
            A_AGC = ACUT >= 9

            pid = nc.partition_id()
            hv_b = [pid * 2, pid * 2 + 1]

            @block.sync
            def _(sy):
                sy.wait_ge(S("vmm"), 64)
                sy.wait_ge(S("st_xln2"), 64)
                sy.wait_ge(S("cast"), 96)
                for ht in range(HT):
                    sy.dma_start(out=wq_sb[:, ht * 256:(ht + 1) * 256],
                                 in_=wq_bf[ht * 128:(ht + 1) * 128, :]).then_inc(S("wqk"), 16)
                    sy.dma_start(out=wk_sb[:, ht * 256:(ht + 1) * 256],
                                 in_=wk_bf[ht * 128:(ht + 1) * 128, :]).then_inc(S("wqk"), 16)
                sy.wait_ge(S("st_agv"), 64)
                sy.wait_ge(S("vev"), 64)
                sy.dma_start(out=xch[:, 0:NMASK * 512], in_=din["amask"]
                             ).then_inc(S("amk"), 16)
                for s in range(2):
                    for tcg in range(16):
                        o = s * 16 + tcg
                        if s == 1 and tcg <= 1 and A_CTX:
                            sy.wait_ge(S("ctm"), ctm_ord[(1, 15, 3)])
                        if o >= 2:
                            sy.wait_ge(S("qkm"), o * 4 - 4)
                        rg, j = 4 * s + tcg // 4, tcg % 4
                        sy.wait_ge(S("ag"), 4 + 2 * j)
                        axs = S("ax0") if o % 2 == 0 else S("ax1")
                        sy.dma_start(out=arena[:, (o % 2) * 8192:(o % 2 + 1) * 8192],
                                     in_=agx[j][rg * 128:(rg + 1) * 128, :]
                                     ).then_inc(axs, 16)
                        if o >= 1:
                            sy.wait_ge(S("radd"), o * 4)
                        sy.dma_start(out=COS, in_=din["ropec"][:, tcg * 512:(tcg + 1) * 512]
                                     ).then_inc(S("rop"), 16)
                        sy.dma_start(out=SIN, in_=din["ropes"][:, tcg * 512:(tcg + 1) * 512]
                                     ).then_inc(S("rop"), 16)
                    for b in range(2):
                        if not A_STG:
                            continue
                        u = s * 2 + b
                        sy.wait_ge(S("radd"), (s + 1) * 64)
                        if u > 0 and A_CTX:
                            sy.wait_ge(S("ctm"), ctm_ord[(u - 1, 15, 3)])
                        sy.wait_ge(S("msv"), u + 1)
                        sy.wait_ge(S("ag"), 11)
                        for j in range(4):
                            sy.dma_start(
                                out=vstage[j * 512:(j + 1) * 512, :].rearrange(
                                    "r (tt d) -> r tt d", tt=4),
                                in_=agv[j][4 * s * 128:(4 * s + 4) * 128, :].rearrange(
                                    "r (tt d) -> r tt d", tt=4)[:, :, bass.ts(hv_b[b], 128)],
                                bounds_check=None
                            ).then_inc(S("vst"), 16)
                        sy.wait_ge(S("vst"), 16 * 4 * (u + 1))
                        for m, kb in enumerate(PKB):
                            g0 = kb * 64
                            rv_l = (g0 // 2048)
                            tlc = g0 % 2048
                            j = tlc // 512
                            tt = (tlc % 512) // 128
                            prow = tlc % 128
                            row0 = j * 512 + rv_l * 128 + prow
                            drow = (m % 2) * 64
                            dst = arena[drow:drow + 64,
                                        VD0 + (m // 2) * 129 + 0:VD0 + (m // 2) * 129 + 128]
                            sy.dma_start(out=dst,
                                         in_=vstage[row0:row0 + 64,
                                                    tt * 128:(tt + 1) * 128]
                                         ).then_inc(S("vdl"), 16)
                        for qc in range(16):
                            for wt in new_tiles[qc]:
                                if wt >= 16 and A_CTX:
                                    sy.wait_ge(S("ctm"),
                                               ctm_ord[(u, win_last[wt - 16], 3)])
                                rv, j, tt = win_rows(s, wt)
                                rv_l = rv - 4 * s
                                row0 = j * 512 + rv_l * 128
                                sl16 = wt % 16
                                dst = arena[:, VW0 + sl16 * 129:VW0 + sl16 * 129 + 128]
                                sy.dma_start(out=dst,
                                             in_=vstage[row0:row0 + 128,
                                                        tt * 128:(tt + 1) * 128]
                                             ).then_inc(S(f"vw{qc % 4}"), 16)

            @block.vector
            def _(v):
                v.wait_ge(S("stat"), 8)
                v.memset(small[0:1, 0:16], 0.0).then_inc(S("nrm"), 1)
                n_rcp = 0
                for s in range(2):
                    for tcg in range(16):
                        o = s * 16 + tcg
                        for b in range(2):
                            for qk in range(2):
                                R = rope_ord(s, tcg, b, qk)
                                v.wait_ge(S("qkm"), o * 4 + b * 2 + qk + 1)
                                v.wait_ge(S("rop"), 32 * (o + 1))
                                if o == 0 and b == 0 and qk == 0:
                                    v.wait_ge(S("dn"), 64)
                                pb = psum[PG] if qk == 0 else psum[PU]
                                dst = (Qsb if qk == 0 else Ksb)[b][:, tcg * 512:(tcg + 1) * 512]
                                v.tensor_tensor(T1, pb[:, :], COS, ALU.mult)
                                v.tensor_copy(QS[0:64, :], pb[64:128, :])
                                v.tensor_copy(QS[64:128, :], pb[0:64, :]
                                              ).then_inc(S("rp2"), 1)
                                v.wait_ge(S("rp2"), 2 * R + 1)
                                v.tensor_tensor(QS, QS, SIN, ALU.mult
                                                ).then_inc(S("rp2"), 1)
                                v.wait_ge(S("rp2"), 2 * R + 2)
                                v.tensor_tensor(dst, T1, QS, ALU.add).then_inc(S("radd"), 1)
                                v.wait_ge(S("nmm"), R + 1)
                                v.tensor_reduce(small[0:1, 30:31], psum[PD][0:1, :],
                                                mybir.AxisListType.X, ALU.max)
                                col = (0 if qk == 0 else 4) + s * 2 + b
                                v.tensor_tensor(small[0:1, col:col + 1],
                                                small[0:1, col:col + 1],
                                                small[0:1, 30:31], ALU.max).then_inc(S("nrm"), 1)
                    for b in range(2):
                        if not A_NEG:
                            continue
                        u = s * 2 + b
                        v.wait_ge(S("nrm"), 1 + (s + 1) * 64)
                        v.tensor_tensor(small[0:1, 8 + u:9 + u], small[0:1, u:u + 1],
                                        small[0:1, 4 + u:5 + u], ALU.mult).then_inc(S("ngv"), 1)
                    for b in range(2):
                        if not A_STG:
                            continue
                        u = s * 2 + b
                        v.wait_ge(S("radd"), (s + 1) * 64)
                        if u > 0 and A_CTX:
                            v.wait_ge(S("ctm"), ctm_ord[(u - 1, 15, 3)])
                        for τ in range(17):
                            v.memset(arena[:, VD0 + τ * 129 + 128:VD0 + τ * 129 + 129], 1.0)
                        for i_w in range(16):
                            v.memset(arena[:, VW0 + i_w * 129 + 128:VW0 + i_w * 129 + 129], 1.0)
                        v.memset(arena[64:128, VD0 + 16 * 129:VD0 + 16 * 129 + 128], 0.0)
                        v.memset(arena[:, KD0 + 2112:KD0 + 2176], 0.0)
                        for m, kb in enumerate(PKB):
                            cpi = v.tensor_copy(arena[:, KD0 + m * 64:KD0 + (m + 1) * 64],
                                                Ksb[b][:, kb * 64:(kb + 1) * 64])
                        cpi.then_inc(S("msv"), 1)
                        for qc in range(16):
                            tl = tile_tab[(u, qc)]
                            for ti, t_ in enumerate(tl):
                                if t_["mid"] is None or not A_SCR:
                                    continue
                                bank = psum[(0, 1, 6)[smm_ord[(u, qc, ti)] % 3]]
                                v.wait_ge(S("smm"), smm_ord[(u, qc, ti)])
                                v.wait_ge(S("amk"), 16)
                                v.tensor_tensor(bank[:, :], bank[:, :],
                                                xch[:, t_["mid"] * 512:(t_["mid"] + 1) * 512],
                                                ALU.add).then_inc(S("msk"), 1)
                            for qs in range(4):
                                if not A_CTX:
                                    continue
                                od = ctm_ord[(u, qc, qs)]
                                v.wait_ge(S("ctm"), od)
                                if od > 4 and A_TRN:
                                    v.wait_ge(S("ctt"), od - 4)
                                v.reciprocal(small[:, 20 + qs:21 + qs],
                                             psum[2 + qs][:, 128:129]
                                             ).then_inc(S("rc2"), 1)
                                v.wait_ge(S("rc2"), od)
                                v.tensor_scalar(xbx[:, 1024 + (od % 4) * 128:
                                                    1024 + (od % 4) * 128 + 128],
                                                psum[2 + qs][:, 0:128],
                                                small[:, 20 + qs:21 + qs], None,
                                                ALU.mult).then_inc(S("rcp"), 1)

            @block.scalar
            def _(sc):
                for s in range(2):
                    for tcg in range(16):
                        for b in range(2):
                            for qk in range(2):
                                R = rope_ord(s, tcg, b, qk)
                                sc.wait_ge(S("radd"), R + 1)
                                if R >= 2:
                                    sc.wait_ge(S("nmm"), R - 1)
                                dst = (Qsb if qk == 0 else Ksb)[b][:, tcg * 512:(tcg + 1) * 512]
                                sc.activation(xbx[:, (R % 2) * 512:(R % 2 + 1) * 512],
                                              dst, AF.Square).then_inc(S("sqn"), 1)
                    for b in range(2):
                        if not A_NEG:
                            continue
                        u = s * 2 + b
                        sc.wait_ge(S("ngv"), u + 1)
                        sc.activation(small[0:1, 8 + u:9 + u], small[0:1, 8 + u:9 + u],
                                      AF.Ln, bias=eps_t[0:1, :])
                        sc.activation(small[0:1, 8 + u:9 + u], small[0:1, 8 + u:9 + u],
                                      AF.Exp, scale=0.5).then_inc(S("ngs"), 1)
                        sc.wait_ge(S("ngm"), u + 1)
                        sc.activation(small[:, 16 + u:17 + u], psum[PD][:, 0:1],
                                      AF.Copy, scale=-SCALE).then_inc(S("ngc"), 1)
                    for b in range(2):
                        u = s * 2 + b
                        if A_NEG:
                            sc.wait_ge(S("ngc"), u + 1)
                        for qc in range(16):
                            tl = tile_tab[(u, qc)]
                            for ti, t_ in enumerate(tl):
                                if not A_SCR:
                                    continue
                                o = exp_ord[(u, qc, ti)]
                                sc.wait_ge(S("smm"), smm_ord[(u, qc, ti)])
                                if t_["mid"] is not None:
                                    sc.wait_ge(S("msk"), msk_ord[(u, qc, ti)])
                                if o > 3 and A_CTX:
                                    fs, fv = ctp_fence[o - 3]
                                    sc.wait_ge(S(fs), fv)
                                sc.activation(arena[:, (o % 3) * 512:(o % 3 + 1) * 512],
                                              psum[(0, 1, 6)[smm_ord[(u, qc, ti)] % 3]][:, :], AF.Exp,
                                              bias=small[:, 16 + u:17 + u],
                                              scale=SCALE).then_inc(S("sexp"), 1)
                            def emit_evac(qcX):
                                for qs in range(4):
                                    if not A_EVA:
                                        continue
                                    od = ctm_ord[(u, qcX, qs)]
                                    sc.wait_ge(S("ctt"), od)
                                    if od > 2 and A_STC:
                                        sc.wait_ge(S("st_ctx"), 16 * (od - 2))
                                    sc.copy(q_hl[:, (od % 2) * 128:(od % 2) * 128 + 128],
                                            psbf[:, (od % 2) * 128:(od % 2) * 128 + 128]
                                            ).then_inc(S("cte"), 1)
                            if qc > 0:
                                emit_evac(qc - 1)
                            if qc == 15:
                                emit_evac(15)

            @block.tensor
            def _(t):
                t.wait_ge(S("wqk"), 32 * 16)
                t.wait_ge(S("vev"), 64)
                t.wait_ge(S("int_"), 256)
                t.wait_ge(S("res"), 64)
                t.wait_ge(S("rowv"), 8)
                t.wait_ge(S("apl"), 128)
                t.wait_ge(S("dn"), 64)
                for s in range(2):
                    if s == 1 and A_CTX:
                        t.wait_ge(S("rcp"), ctm_ord[(1, 15, 3)])
                    for tcg in range(16):
                        o = s * 16 + tcg
                        axs = S("ax0") if o % 2 == 0 else S("ax1")
                        t.wait_ge(axs, 16 * (o // 2 + 1))
                        slab = arena[:, (o % 2) * 8192:(o % 2 + 1) * 8192]
                        for b in range(2):
                            for qk in range(2):
                                R = rope_ord(s, tcg, b, qk)
                                if R >= 2:
                                    t.wait_ge(S("radd"), R - 1)
                                w_sb = wq_sb if qk == 0 else wk_sb
                                pb = psum[PG] if qk == 0 else psum[PU]
                                for ht in range(HT):
                                    mmi = t.matmul(pb[:, :],
                                                   w_sb[:, ht * 256 + b * 128:
                                                        ht * 256 + b * 128 + 128],
                                                   slab[:, ht * 512:(ht + 1) * 512],
                                                   start=(ht == 0), stop=(ht == HT - 1))
                                mmi.then_inc(S("qkm"), 1)
                                t.wait_ge(S("sqn"), R + 1)
                                if R >= 1:
                                    t.wait_ge(S("nrm"), 1 + R)
                                if R == 64 and A_NEG:
                                    t.wait_ge(S("ngc"), 2)
                                t.matmul(psum[PD][0:1, :], ones_bf[:, :],
                                         xbx[:, (R % 2) * 512:(R % 2 + 1) * 512],
                                         start=True, stop=True).then_inc(S("nmm"), 1)
                    for b in range(2):
                        if not A_NEG:
                            continue
                        u = s * 2 + b
                        t.wait_ge(S("ngs"), u + 1)
                        t.wait_ge(S("nrm"), 1 + (s + 1) * 64)
                        t.matmul(psum[PD][:, 0:1], ones1f[:, :],
                                 small[0:1, 8 + u:9 + u],
                                 start=True, stop=True).then_inc(S("ngm"), 1)
                    t.wait_ge(S("radd"), (s + 1) * 64)
                    if A_NEG:
                        t.wait_ge(S("ngc"), 2 * (s + 1))
                    for b in range(2):
                        if not A_STG:
                            continue
                        u = s * 2 + b
                        t.wait_ge(S("msv"), u + 1)
                        for qc in range(16):
                            tl = tile_tab[(u, qc)]
                            wins_before = {}
                            iw = 0
                            for ti, t_ in enumerate(tl):
                                if t_["kind"] == "win":
                                    wins_before[ti] = iw
                                    iw += 1
                            def emit_ctx(ti):
                                t_ = tl[ti]
                                o = exp_ord[(u, qc, ti)]
                                t.wait_ge(S("sexp"), o)
                                last_mmi = None
                                if t_["kind"] == "pk":
                                    t.wait_ge(S("vdl"), 16 * 33 * (u + 1))
                                    vc = VD0 + t_["idx"] * 129
                                else:
                                    t.wait_ge(S(f"vw{qc % 4}"), 16 * vwn_cnt[(u, qc)])
                                    vc = VW0 + (t_["idx"] % 16) * 129
                                for qs in range(4):
                                    if not t_["qs"][qs]:
                                        continue
                                    fi, la, _ = ctx_fl[(u, qc, qs)]
                                    if ti == fi:
                                        prev = (u, qc - 1, qs) if qc > 0 else \
                                               ((u - 1, 15, qs) if u > 0 else None)
                                        if prev is not None:
                                            t.wait_ge(S("rcp"), ctm_ord[prev])
                                    mmi = t.matmul(psum[2 + qs][:, 0:129],
                                                   arena[:, (o % 3) * 512 + qs * 128:
                                                          (o % 3) * 512 + qs * 128 + 128],
                                                   arena[:, vc:vc + 129],
                                                   start=(ti == fi), stop=(ti == la))
                                    last_mmi = mmi
                                    if ti == la:
                                        mmi.then_inc(S("ctm"), 1)
                                if last_mmi is not None and \
                                        ctp_fence[o][0] == "ctp":
                                    last_mmi.then_inc(S("ctp"), 1)
                            for ti, t_ in enumerate(tl):
                                if not A_SCR:
                                    continue
                                smo = smm_ord[(u, qc, ti)]
                                if smo > 3:
                                    t.wait_ge(S("sexp"), smo - 3)
                                if t_["kind"] == "pk":
                                    ksl = arena[:, KD0 + t_["idx"] * 128:
                                                KD0 + (t_["idx"] + 1) * 128]
                                else:
                                    ksl = Ksb[b][:, t_["idx"] * 128:(t_["idx"] + 1) * 128]
                                t.matmul(psum[(0, 1, 6)[smo % 3]][:, :], ksl,
                                         Qsb[b][:, qc * 512:(qc + 1) * 512],
                                         start=True, stop=True).then_inc(S("smm"), 1)
                                if ti >= 1 and A_CTX:
                                    emit_ctx(ti - 1)
                            if A_CTX:
                                emit_ctx(len(tl) - 1)
                            def emit_trn(qcX):
                                for qs in range(4):
                                    if not A_TRN:
                                        continue
                                    od = ctm_ord[(u, qcX, qs)]
                                    t.wait_ge(S("rcp"), od)
                                    if od > 1 and A_EVA:
                                        t.wait_ge(S("cte"), od - 1)
                                    t.transpose(psbf[:, (od % 2) * 128:(od % 2) * 128 + 128],
                                                xbx[:, 1024 + (od % 4) * 128:
                                                    1024 + (od % 4) * 128 + 128],
                                                ident[:, :]).then_inc(S("ctt"), 1)
                            if qc > 0:
                                emit_trn(qc - 1)
                            if qc == 15:
                                emit_trn(15)

            @block.gpsimd
            def _(g):
                for s in range(2):
                    for b in range(2):
                        if not A_STC:
                            continue
                        u = s * 2 + b
                        for qc in range(16):
                            for qs in range(4):
                                od = ctm_ord[(u, qc, qs)]
                                g.wait_ge(S("cte"), od)
                                g.dma_start(out=agc_in[u][:, qc * 512 + qs * 128:
                                                          qc * 512 + qs * 128 + 128],
                                            in_=q_hl[:, (od % 2) * 128:(od % 2) * 128 + 128]
                                            ).then_inc(S("st_ctx"), 16)
                        if not A_AGC:
                            continue
                        g.wait_ge(S("st_ctx"), 16 * 64 * (u + 1))
                        g.collective_compute("AllGather", ALU.bypass,
                                             replica_groups=[list(range(NCORES))],
                                             ins=[agc_in[u].opt()],
                                             outs=[agc[u * 1024:(u + 1) * 1024, :].opt()]
                                             ).then_inc(S("ag"), 1)

        if MK_PHASES >= 4 and MK_CUT == 0:
            pid4 = nc.partition_id()
            iu_b = [(pid4 // 4) * 2, (pid4 // 4) * 2 + 1]
            col_tc = [(pid4 % 4) * 4 + t_ for t_ in range(NCH)]
            CXB = 8192

            @block.sync
            def _(sy):
                sy.wait_ge(S("ag"), 15)
                for tc in range(NCH):
                    if tc > 0:
                        sy.wait_ge(S("st_out"), tc * 16)
                        sy.wait_ge(S("xb"), 128 + 16 * tc)
                        sy.wait_ge(S("apl"), 128 + 16 * tc)
                    sy.dma_start(out=xch[:, :],
                                 in_=resid[:, tc * HT * TCH:(tc + 1) * HT * TCH]
                                 ).then_inc(S("x"), 16)
                    if tc == 0:
                        from concourse.ap import AP as _BAP
                        for b in range(2):
                            off = iu_b[b] * (1024 * SEQ) + (pid4 % 4) * TPC
                            sy.dma_start(
                                out=cstage[b * 1024:(b + 1) * 1024, :],
                                in_=_BAP(agc_h, off, [[SEQ, 1024], [1, TPC]]),
                                bounds_check=None
                            ).then_inc(S("cst"), 16)
                        sy.wait_ge(S("cst"), 32)
                    if tc > 0:
                        sy.wait_ge(S("opj"), 16 * tc)
                    for b in range(2):
                        sy.dma_start(
                            out=inter[:, CXB + b * 4096:CXB + (b + 1) * 4096].rearrange(
                                "p (r c) -> p r c", r=8),
                            in_=cstage[b * 1024:(b + 1) * 1024,
                                       tc * 512:(tc + 1) * 512].rearrange(
                                "(r d) c -> d r c", d=128)
                        ).then_inc(S("cxi"), 16)
                    for ct in range(NCT):
                        wos = S("wo0") if ct % 2 == 0 else S("wo1")
                        if ct >= 2:
                            sy.wait_ge(S("opj"), tc * 16 + ct - 1)
                        sy.dma_start(out=arena[:, (ct % 2) * 2048:(ct % 2 + 1) * 2048],
                                     in_=wo_bf[:, ct * 2048:(ct + 1) * 2048]
                                     ).then_inc(wos, 16)
                    sy.wait_ge(S("opj"), 16 * (tc + 1))
                    tw = 4 + tc
                    for it in range(NIT):
                        for gu, w in ((0, "g"), (1, "u")):
                            base = 0 if gu == 0 else 4096
                            slot = base + (it % 2) * 2048
                            wsl = S(f"w{'g' if gu == 0 else 'u'}{it % 2}")
                            if it >= 2:
                                sy.wait_ge(S("gu"), 512 + tc * 128 + 2 * (it - 2) + gu + 1)
                            r, il = it // 8, it % 8
                            sy.dma_start(out=arena[:, slot:slot + 2048],
                                         in_=wfull[w][r * 128:(r + 1) * 128,
                                                      il * 2048:(il + 1) * 2048]
                                         ).then_inc(wsl, 16)
                    sy.wait_ge(S("gu"), 512 + tc * 128 + 128)
                    for ct in range(NCT):
                        slot = (ct % 2) * 8192
                        wsl = S(f"wd{ct % 2}")
                        if ct >= 2:
                            sy.wait_ge(S("dn"), 64 + tc * 16 + ct - 1)
                        r, ctl = ct // 2, ct % 2
                        sy.dma_start(out=arena[:, slot:slot + 8192],
                                     in_=wfull["d"][r * 128:(r + 1) * 128,
                                                    ctl * 8192:(ctl + 1) * 8192]
                                     ).then_inc(wsl, 16)

            @block.scalar
            def _(sc):
                sc.wait_ge(S("ag"), 15)
                for tc in range(NCH):
                    lnn = 8 + tc
                    sc.wait_ge(S("x"), (4 + tc + 1) * 16)
                    for ht in range(HT):
                        sc.wait_ge(S("res2"), tc * 16 + ht + 1)
                        sc.copy(xln[:, xcol(ht)], xch[:, xcol(ht)])
                        sc.activation(inter[:, ht * 512:(ht + 1) * 512],
                                      xch[:, xcol(ht)], AF.Square).then_inc(S("xb"), 1)
                    sc.wait_ge(S("rowv"), lnn + 1)
                    sc.activation(scr[0:1, 1536:2048], scr[0:1, 1024:1536],
                                  AF.Ln, bias=eps_t[0:1, :])
                    sc.activation(scr[0:1, 1536:2048], scr[0:1, 1536:2048],
                                  AF.Exp, scale=-0.5).then_inc(S("rows"), 1)
                    for it in range(NIT):
                        sc.wait_ge(S("gu"), 512 + tc * 128 + 2 * it + 1)
                        sl = it % 2
                        sc.activation(xbx[:, sl * 512:(sl + 1) * 512], psum[PG][:, :],
                                      AF.Silu).then_inc(S("silu"), 1)

            @block.tensor
            def _(t):
                t.wait_ge(S("ag"), 15)
                for tc in range(NCH):
                    lnn = 8 + tc
                    t.wait_ge(S("cxi"), 32 * (tc + 1))
                    for ct in range(NCT):
                        wos = S("wo0") if ct % 2 == 0 else S("wo1")
                        t.wait_ge(wos, 16 * (tc * 8 + ct // 2 + 1))
                        if ct == 0:
                            t.wait_ge(S("res"), 64 + 16 * tc)
                        t.wait_ge(S("res2"), tc * 16 + ct)
                        for b in range(2):
                            for r in range(8):
                                ht2 = b * 8 + r
                                mmi = t.matmul(
                                    psum[PD][:, :],
                                    arena[:, (ct % 2) * 2048 + ht2 * 128:
                                           (ct % 2) * 2048 + ht2 * 128 + 128],
                                    inter[:, CXB + b * 4096 + r * 512:
                                           CXB + b * 4096 + (r + 1) * 512],
                                    start=(ht2 == 0), stop=(ht2 == 15))
                        mmi.then_inc(S("opj"), 1)
                    if True:
                        t.wait_ge(S("rowv"), lnn)
                        t.wait_ge(S("xb"), 128 + tc * 16 + 16)
                        for ht in range(HT):
                            t.matmul(psum[PS1][0:1, :], ones_bf[:, :], xln[:, xcol(ht)],
                                     start=(ht == 0), stop=(ht == HT - 1))
                        for ht in range(HT):
                            mmi = t.matmul(psum[PS2][0:1, :], ones_bf[:, :],
                                           inter[:, ht * 512:(ht + 1) * 512],
                                           start=(ht == 0), stop=(ht == HT - 1))
                        mmi.then_inc(S("stat"), 1)
                        t.wait_ge(S("rows"), lnn + 1)
                        t.wait_ge(S("apl"), 128 + 16 * tc)
                        t.matmul(psum[PM][:, :], ones1f[:, :], scr[0:1, 512 + 0:512 + 512],
                                 start=True, stop=True)
                        t.matmul(psum[PR][:, :], ones1f[:, :], scr[0:1, 1536:2048],
                                 start=True, stop=True).then_inc(S("bc"), 1)
                    t.wait_ge(S("apl"), 128 + 16 * tc + 16)
                    for it in range(NIT):
                        for gu, bank in ((0, PG), (1, PU)):
                            wsl = S(f"w{'g' if gu == 0 else 'u'}{it % 2}")
                            t.wait_ge(wsl, 16 * ((4 + tc) * 32 + it // 2 + 1))
                            if it >= 1:
                                if gu == 0:
                                    t.wait_ge(S("silu"), 256 + tc * 64 + it)
                                else:
                                    t.wait_ge(S("int_"), 256 + tc * 64 + it)
                            base = 0 if gu == 0 else 4096
                            slot = base + (it % 2) * 2048
                            for hb in range(HT):
                                mmi = t.matmul(psum[bank][:, :],
                                               arena[:, slot + hb * 128:slot + (hb + 1) * 128],
                                               xln[:, xcol(hb)],
                                               start=(hb == 0), stop=(hb == HT - 1))
                            mmi.then_inc(S("gu"), 1)
                    for ct in range(NCT):
                        wsl = S(f"wd{ct % 2}")
                        t.wait_ge(wsl, 16 * ((4 + tc) * 8 + ct // 2 + 1))
                        t.wait_ge(S("int_"), 256 + tc * 64 + 64)
                        t.wait_ge(S("res"), 64 + tc * 16 + ct)
                        if ct == 0:
                            t.wait_ge(S("res2"), 16 * (tc + 1))
                        slot = (ct % 2) * 8192
                        for it in range(NIT):
                            mmi = t.matmul(psum[PD][:, :],
                                           arena[:, slot + it * 128:slot + (it + 1) * 128],
                                           inter[:, it * 512:(it + 1) * 512],
                                           start=(it == 0), stop=(it == NIT - 1))
                        mmi.then_inc(S("dn"), 1)

            @block.vector
            def _(v):
                v.wait_ge(S("ag"), 15)
                for tc in range(NCH):
                    lnn = 8 + tc
                    for ct in range(NCT):
                        v.wait_ge(S("opj"), tc * 16 + ct + 1)
                        v.tensor_tensor(xch[:, xcol(ct)], psum[PD][:, :],
                                        xch[:, xcol(ct)], ALU.add).then_inc(S("res2"), 1)
                    v.wait_ge(S("stat"), lnn + 1)
                    v.tensor_scalar(scr[0:1, 512:1024], psum[PS1][0:1, :],
                                    1.0 / DM, None, ALU.mult)
                    v.tensor_scalar(scr[0:1, 1024:1536], psum[PS2][0:1, :],
                                    1.0 / DM, None, ALU.mult)
                    v.tensor_tensor(scr[0:1, 1536:2048], scr[0:1, 512:1024],
                                    scr[0:1, 512:1024], ALU.mult).then_inc(S("rv2"), 1)
                    v.wait_ge(S("rv2"), 8 + tc + 1)
                    v.tensor_tensor(scr[0:1, 1024:1536], scr[0:1, 1024:1536],
                                    scr[0:1, 1536:2048], ALU.subtract).then_inc(S("rowv"), 1)
                    v.wait_ge(S("bc"), lnn + 1)
                    for ht in range(HT):
                        v.tensor_tensor(scr[:, 0:512], xch[:, xcol(ht)], psum[PM][:, :],
                                        ALU.subtract).then_inc(S("ap2"), 1)
                        v.wait_ge(S("ap2"), 128 + tc * 16 + ht + 1)
                        v.tensor_tensor(xln[:, xcol(ht)], scr[:, 0:512], psum[PR][:, :],
                                        ALU.mult).then_inc(S("apl"), 1)
                    for it in range(NIT):
                        v.wait_ge(S("silu"), 256 + tc * 64 + it + 1)
                        v.wait_ge(S("gu"), 512 + tc * 128 + 2 * it + 2)
                        sl = it % 2
                        v.tensor_tensor(inter[:, it * 512:(it + 1) * 512],
                                        xbx[:, sl * 512:(sl + 1) * 512], psum[PU][:, :],
                                        ALU.mult).then_inc(S("int_"), 1)
                    for ct in range(NCT):
                        v.wait_ge(S("dn"), 64 + tc * 16 + ct + 1)
                        v.tensor_tensor(xch[:, xcol(ct)], psum[PD][:, :],
                                        xch[:, xcol(ct)], ALU.add).then_inc(S("res"), 1)

            @block.gpsimd
            def _(g):
                for tc in range(NCH):
                    g.wait_ge(S("res"), 64 + tc * 16 + 16)
                    g.dma_start(out=out[:, tc * HT * TCH:(tc + 1) * HT * TCH],
                                in_=xch[:, :]).then_inc(S("st_out"), 16)
                g.wait_ge(S("st_out"), 64)

        if MK_PHASES < 4 or MK_CUT != 0:
            @block.gpsimd
            def _(g):
                if NCH_E:
                    g.wait_ge(S("st_x1"), NCH_E * 16)
                g.wait_ge(S("ag"), 3 + (2 * NCH_E if MK_CUT == 0 else 0))
                g.dma_start(out=out, in_=resid if NCH_E else din["xt"]).then_inc(S("fin"), 16)
                g.wait_ge(S("fin"), 16)

    es.close()
    return nc


def kernel(**inputs):
    in_maps = _host_prep(inputs)
    nc = build_nc()
    res = run_bass_kernel_spmd(nc, in_maps, core_ids=list(range(NCORES)))
    x = np.asarray(inputs["L0_attn_x"])
    outp = np.empty_like(x)
    for c in range(NCORES):
        s, pos = c // 4, (c % 4) * TPC
        o = res.results[c]["out"]
        oT = o.reshape(128, NCH, HT, TCH).transpose(2, 0, 1, 3).reshape(DM, TPC)
        outp[s, pos:pos + TPC, :] = oT.T
    return outp


# version 16
# speedup vs baseline: 1.0515x; 1.0038x over previous
import numpy as np
import ml_dtypes
from math import comb
import contextlib
import os

import concourse.bass as bass
import concourse.mybir as mybir
from concourse.bass_utils import run_bass_kernel_spmd

F32 = mybir.dt.float32
BF16 = mybir.dt.bfloat16
U8 = mybir.dt.uint8
AF = mybir.ActivationFunctionType
ALU = mybir.AluOpType

NCORES = 8
DM = 2048
SEQ = 8192
NH = 16
HD = 128
INTER = 8192
TPC = 2048
TCH = 512
NCH = TPC // TCH
HT = DM // 128
NIT = INTER // 128
NCT = DM // 128
NCTRL = 130156
NSAMP = DM * INTER
FGRID = NSAMP // 128
FPC = FGRID // NCORES
SCALE = float(1.0 / np.sqrt(HD))
EPS = 1e-5
BLK = 64
NQB = SEQ // BLK
WINB = 16
DILB = 4
GLOB = 2
NEG = -1.0e9

_CACHE = {}
MK_PHASES = int(os.environ.get("MK_PHASES", "9"))
MK_NORACE = bool(int(os.environ.get("MK_NORACE", "0")))
MK_CUT = int(os.environ.get("MK_CUT", "0"))


def _grid_tables():
    if "grid" in _CACHE:
        return _CACHE["grid"]
    n = NSAMP
    t = np.linspace(0.0, NCTRL - 1.0, n, dtype=np.float64)
    i_arr = np.clip(np.floor(t).astype(np.int64), 0, NCTRL - 2)
    i2 = i_arr.reshape(FGRID, 128)
    t2 = t.reshape(FGRID, 128)
    IA = i2[:, 0].copy()
    IB = np.minimum(IA + 1, NCTRL - 2)
    crosses = i2[:, 127] > IA
    first_b = (i2 > IA[:, None]).argmax(axis=1)
    pstar = np.where(crosses, first_b, 128)
    u0A = t2[:, 0] - IA
    u0B = t2[:, 0] - IB
    Bm = 0.5 * np.array([[0, 2, 0, 0], [-1, 0, 1, 0], [2, -5, 4, -1], [-1, 3, -3, 1]],
                        dtype=np.float64)
    step = (NCTRL - 1.0) / (n - 1.0)

    def g_tables(I, u0):
        idx = np.stack([np.clip(I - 1 + j, 0, NCTRL - 1) for j in range(4)], axis=0)
        G = np.zeros((4, 4, FGRID), dtype=np.float64)
        for d in range(4):
            for j in range(4):
                for e in range(d, 4):
                    G[d, j] += comb(e, d) * (u0 ** (e - d)) * Bm[e, j]
        return idx.astype(np.int32), G

    idxA, GA = g_tables(IA, u0A)
    idxB, GB = g_tables(IB, u0B)
    vp = np.stack([(np.arange(128) * step) ** d for d in range(4)], axis=0)
    _CACHE["grid"] = dict(idxA=idxA, idxB=idxB, GA=GA, GB=GB, pstar=pstar, vp=vp)
    return _CACHE["grid"]


def _col_order(core, which):
    if which == "gu":
        il = np.arange(8)[:, None, None]
        hb = np.arange(HT)[None, :, None]
        iin = np.arange(128)[None, None, :]
        i_glob = (core * 8 + il) * 128 + iin
        return (i_glob * HT + hb).reshape(-1)
    ctl = np.arange(2)[:, None, None]
    it = np.arange(NIT)[None, :, None]
    hin = np.arange(128)[None, None, :]
    hout = core * 256 + ctl * 128 + hin
    return (hout * NIT + it).reshape(-1)


def _pack_f(vals):
    v = vals.reshape(32, 512)
    return np.ascontiguousarray(np.broadcast_to(v[None], (4, 32, 512)).reshape(128, 512))


def _pack_df(vals):
    return np.ascontiguousarray(vals.reshape(4, 32, 512).reshape(128, 512))


def _expansion_consts(core):
    key = ("exp", core)
    if key in _CACHE:
        return _CACHE[key]
    g = _grid_tables()
    out = {}
    for which in ("gu", "d"):
        fg = _col_order(core, which)
        scale = 0.5 if which == "d" else 1.0
        for pl, G, idx in (("A", g["GA"], g["idxA"]), ("B", g["GB"], g["idxB"])):
            Gc = G[:, :, fg] * scale
            for j in range(4):
                out[f"G_{which}_{pl}_{j}"] = _pack_df(Gc[:, j, :].astype(np.float32))
            out[f"idx_{which}_{pl}"] = idx[:, fg]
        ps = g["pstar"][fg]
        out[f"mask_{which}"] = np.ascontiguousarray(
            (np.arange(128)[:, None] >= ps[None, :]).astype(np.uint8))
    vph = g["vp"].astype(ml_dtypes.bfloat16)
    vpl = (g["vp"] - vph.astype(np.float64)).astype(ml_dtypes.bfloat16)
    out["vph"] = np.asarray(vph)
    out["vpl"] = np.asarray(vpl)
    _CACHE[key] = out
    return out


def _rope_tables():
    if "rope" in _CACHE:
        return _CACHE["rope"]
    t = np.arange(SEQ, dtype=np.float64)
    inv = 1.0 / (10000.0 ** (np.arange(0, HD, 2, dtype=np.float64) / HD))
    fr = np.outer(t, inv)
    emb = np.concatenate([fr, fr], axis=1)
    cosT = np.ascontiguousarray(np.cos(emb).T.astype(np.float32))
    sinT = np.sin(emb).T
    sg = np.ones((HD, 1)); sg[:64] = -1.0
    sinS = np.ascontiguousarray((sinT * sg).astype(np.float32))
    _CACHE["rope"] = (cosT, sinS)
    return _CACHE["rope"]


def _allowed(qb, kb):
    if kb > qb:
        return False
    return (qb - kb <= WINB) or (kb < GLOB) or (kb % DILB == 0)


def _attn_schedule():
    if "attn" in _CACHE:
        return _CACHE["attn"]
    packed_kbs = [0, 1] + list(range(4, NQB, 4))
    masks = {}
    mask_list = []

    def mask_id(bitmap):
        key = tuple(map(tuple, bitmap))
        if key not in masks:
            m = np.zeros((128, 512), dtype=np.float32)
            for kk in range(2):
                for qq in range(8):
                    if not bitmap[kk][qq]:
                        m[kk * 64:(kk + 1) * 64, qq * 64:(qq + 1) * 64] = NEG
            masks[key] = len(mask_list)
            mask_list.append(m)
        return masks[key]

    sched = []
    for qc in range(16):
        qbs = list(range(8 * qc, 8 * qc + 8))
        wlo = max(0, 8 * qc - WINB)
        pk = [kb for kb in packed_kbs if kb < wlo]
        for kb in pk:
            assert all(_allowed(qb, kb) for qb in qbs)
        n_pk_tok = len(pk) * BLK
        n_pk_tiles = (n_pk_tok + 127) // 128
        pk_bnd = mask_id([[True] * 8, [False] * 8]) if n_pk_tok % 128 == 64 else None
        wtiles = []
        kb0 = wlo - (wlo % 2)
        for kbt in range(kb0, 8 * qc + 8, 2):
            kbs = [kbt, kbt + 1]
            bm = [[_allowed(qb, kb) for qb in qbs] for kb in kbs]
            if not any(any(r) for r in bm):
                continue
            mid = None if all(all(r) for r in bm) else mask_id(bm)
            qsub = [any(bm[kk][qq] for kk in range(2) for qq in (2 * qs, 2 * qs + 1))
                    for qs in range(4)]
            wtiles.append((kbt // 2, mid, qsub))
        sched.append(dict(n_pk_tiles=n_pk_tiles, n_pk_tok=n_pk_tok, pk_bnd=pk_bnd,
                          wtiles=wtiles))
    res = dict(sched=sched,
               masks=np.stack(mask_list) if mask_list else np.zeros((1, 128, 512), np.float32),
               packed_kbs=packed_kbs)
    _CACHE["attn"] = res
    return res


def _host_prep(inputs):
    x = np.asarray(inputs["L0_attn_x"])
    wq = np.asarray(inputs["L0_attn_w_q"]); wk = np.asarray(inputs["L0_attn_w_k"])
    wv = np.asarray(inputs["L0_attn_w_v"]); wo = np.asarray(inputs["L0_attn_w_o"])
    cps = {"g": np.asarray(inputs["L0_ffn_gate_cp"]),
           "u": np.asarray(inputs["L0_ffn_up_cp"]),
           "d": np.asarray(inputs["L0_ffn_down_cp"])}

    wqT = np.ascontiguousarray(wq.T); wkT = np.ascontiguousarray(wk.T)
    wvT = wv.T.reshape(HT, 128, 4, TCH).transpose(1, 2, 0, 3).reshape(128, 4 * HT * TCH)
    wvP = np.ascontiguousarray(wvT)
    woT = wo.T
    perm = np.empty(DM, dtype=np.int64)
    for b in range(2):
        for r in range(8):
            perm[(b * 8 + r) * 128:(b * 8 + r + 1) * 128] = np.arange(
                (2 * r + b) * 128, (2 * r + b + 1) * 128)
    woP0 = woT[perm]
    woP = np.ascontiguousarray(
        woP0.reshape(HT, 128, NCT, 128).transpose(1, 2, 0, 3).reshape(128, NCT * HT * 128))

    rope_cos, rope_sin = _rope_tables()
    att = _attn_schedule()
    amask = np.ascontiguousarray(att["masks"].transpose(1, 0, 2).reshape(128, -1))

    in_maps = []
    for c in range(NCORES):
        s, pos = c // 4, (c % 4) * TPC
        xcT = x[s, pos:pos + TPC, :].T
        xp = np.ascontiguousarray(
            xcT.reshape(HT, 128, NCH, TCH).transpose(1, 2, 0, 3).reshape(128, NCH * HT * TCH))
        ec = _expansion_consts(c)
        m = {
            "xt": xp.astype(np.float32),
            "wqsl": np.ascontiguousarray(wqT[:, 2 * c * 128:(2 * c + 2) * 128]),
            "wksl": np.ascontiguousarray(wkT[:, 2 * c * 128:(2 * c + 2) * 128]),
            "wvt": wvP.astype(np.float32),
            "wop": woP.astype(np.float32),
            "ropec": rope_cos, "ropes": rope_sin, "amask": amask,
            "identin": np.asarray(np.eye(128, dtype=np.float32).astype(ml_dtypes.bfloat16)),
            "vph": ec["vph"], "vpl": ec["vpl"],
            "maskgu": ec["mask_gu"], "maskd": ec["mask_d"],
        }
        for which in ("gu", "d"):
            for pl in ("A", "B"):
                for j in range(4):
                    m[f"G{which}{pl}{j}"] = ec[f"G_{which}_{pl}_{j}"]
        for w in ("g", "u", "d"):
            which = "gu" if w != "d" else "d"
            for pl in ("A", "B"):
                idx = ec[f"idx_{which}_{pl}"]
                for j in range(4):
                    m[f"C{w}{pl}{j}"] = _pack_f(cps[w][idx[j]]).astype(np.float32)
        in_maps.append(m)
    return in_maps


WLIST = ("g", "u", "d")
XB_TC, APL_TC = 32, 32


def build_nc():
    att = _attn_schedule()
    n_amask = att["masks"].shape[0]
    assert n_amask <= 16, n_amask
    nc = bass.Bass("TRN2", target_bir_lowering=False,
                   detect_race_conditions=not MK_NORACE)

    din = {}
    def DI(name, shape, dt=F32):
        din[name] = nc.dram_tensor(name, list(shape), dt, kind="ExternalInput").ap()
        return din[name]

    DI("xt", [128, NCH * HT * TCH])
    DI("wqsl", [DM, 256]); DI("wksl", [DM, 256])
    DI("wvt", [128, 4 * HT * TCH]); DI("wop", [128, NCT * HT * 128])
    DI("ropec", [HD, SEQ]); DI("ropes", [HD, SEQ])
    DI("amask", [128, n_amask * 512])
    DI("identin", [128, 128], BF16)
    DI("vph", [4, 128], BF16); DI("vpl", [4, 128], BF16)
    DI("maskgu", [128, FPC], U8); DI("maskd", [128, FPC], U8)
    for which in ("gu", "d"):
        for pl in ("A", "B"):
            for j in range(4):
                DI(f"G{which}{pl}{j}", [128, 512])
    for w in WLIST:
        for pl in ("A", "B"):
            for j in range(4):
                DI(f"C{w}{pl}{j}", [128, 512])

    out = nc.dram_tensor("out", [128, NCH * HT * TCH], F32, kind="ExternalOutput").ap()

    resid = nc.dram_tensor("resid", [128, NCH * HT * TCH], F32).ap()
    qdram = {w: nc.dram_tensor(f"qdram_{w}", [4, 4 * FPC], BF16).ap() for w in WLIST}
    wsh = {w: nc.dram_tensor(f"wsh_{w}", [128, FPC], BF16).ap() for w in WLIST}
    wfull = {w: nc.dram_tensor(f"wfull_{w}", [NCORES * 128, FPC], BF16,
                               addr_space="Shared").ap()
             for w in WLIST}
    wq_bf = nc.dram_tensor("wq_bf", [DM, 256], BF16).ap()
    wk_bf = nc.dram_tensor("wk_bf", [DM, 256], BF16).ap()
    wv_bf = nc.dram_tensor("wv_bf", [128, 4 * HT * TCH], BF16).ap()
    wo_bf = nc.dram_tensor("wo_bf", [128, NCT * HT * 128], BF16).ap()
    agx_in = [nc.dram_tensor(f"agxin{j}", [128, HT * TCH], BF16).ap() for j in range(NCH)]
    agx = [nc.dram_tensor(f"agx{j}", [NCORES * 128, HT * TCH], BF16,
                          addr_space="Shared").ap() for j in range(NCH)]
    agv_in = [nc.dram_tensor(f"agvin{j}", [128, 4 * DM], BF16).ap() for j in range(NCH)]
    agv = [nc.dram_tensor(f"agv{j}", [NCORES * 128, 4 * DM], BF16,
                          addr_space="Shared").ap() for j in range(NCH)]
    agc_in = [nc.dram_tensor(f"agcin{u}", [128, SEQ], BF16).ap() for u in range(4)]
    vstage = nc.dram_tensor("vstage", [4 * 512, 512], BF16).ap()
    cstage_h = nc.dram_tensor("cstage", [2048, 2048], BF16)
    cstage = cstage_h.ap()
    agc_h = nc.dram_tensor("agc", [4 * NCORES * 128, SEQ], BF16,
                           addr_space="Shared")
    agc = agc_h.ap()

    es = contextlib.ExitStack()
    def SB(name, shape, dt):
        return es.enter_context(nc.sbuf_tensor(name, list(shape), dt))

    ones_bf = SB("ones_bf", [128, 1], BF16)
    ones1f = SB("ones1f", [1, 128], F32)
    ident = SB("ident", [128, 128], BF16)
    eps_t = SB("eps_t", [1, 1], F32)
    gc_area = SB("gc_area", [128, 4 * 512], F32)
    q_scr = SB("q_scr", [128, 2 * 512], F32)
    q_hl = SB("q_hl", [128, 2 * 512], BF16)
    qrows = SB("qrows", [4, 2 * 2048], BF16)
    vp_sb = SB("vp_sb", [4, 2 * 128], BF16)
    mask_sb = SB("mask_sb", [128, FPC], U8)
    wexp = SB("wexp", [128, 2 * 512], BF16)
    xch = SB("xch", [128, HT * TCH], F32)
    xbx = SB("xbx", [128, 4 * 512], BF16)
    xln = SB("xln", [128, HT * TCH], BF16)
    inter = SB("inter", [128, NIT * TCH], BF16)
    arena = SB("arena", [128, 16384], BF16)
    scr = SB("scr", [128, 2048], F32)
    small = SB("small", [128, 256], F32)

    ROW_M, ROW_V, ROW_T = 512, 1024, 1536

    psum = [es.enter_context(nc.psum_tensor(f"pb{i}", [128, 512], F32)) for i in range(7)]
    psbf = es.enter_context(nc.psum_tensor("pb7", [128, 1024], BF16))

    sem_names = [
        "ldc", "gc0", "gc1", "cast", "qst0", "qst1", "qst2", "qr0", "qr1",
        "wst0", "wst1", "x", "wg0", "wg1", "wu0", "wu1", "wd0", "wd1", "wv0",
        "vw0", "vw1", "vw2", "vw3", "wvb", "ap2", "rv2",
        "st_x1", "st_xln2", "st_agv", "fin",
        "ag",
        "ax0", "ax1", "rop", "qkm", "radd", "sqn", "nmm", "nrm", "msv",
        "ngv", "ngs", "ngm", "ngc", "vdl", "vwn", "amk", "smm", "msk",
        "sexp", "ctm", "ctp", "rcp", "rc2", "rp2", "ctt", "cte", "st_ctx", "wqk", "vst", "cst",
        "cxi", "wo0", "wo1", "opj", "res2", "st_out",
        "qprep", "mm", "ev", "ve", "xb", "stat", "rowv", "rows", "bc", "apl",
        "silu", "int_", "gu", "dn", "res", "vmm", "vev",
    ]
    sems = {n: es.enter_context(nc.semaphore(n)) for n in sem_names}
    S = lambda n: sems[n]

    PS1, PS2, PM, PR, PG, PU, PD = 0, 1, 2, 3, 4, 5, 6
    PV = 4
    VE0 = 1

    def xcol(ht, c0=0, n=TCH):
        return slice(ht * TCH + c0, ht * TCH + c0 + n)

    NCH_E = NCH if MK_PHASES >= 2 else 0
    DO_GU = MK_CUT == 0 or MK_CUT >= 2
    DO_SILU = MK_CUT in (0, 22, 2, 3)
    DO_INT = MK_CUT in (0, 2, 3)
    DO_DN = MK_CUT in (0, 3)
    DO_LN2V = MK_CUT == 0
    NLN = 2 if DO_LN2V else 1
    XBT = 16 * NLN
    APLT = 16 * NLN

    with nc.Block() as block:
        nc.cache_partition_id()
        @block.sync
        def _(sy):
            sy.dma_start(out=vp_sb[:, 0:128], in_=din["vph"]).then_inc(S("ldc"), 16)
            sy.dma_start(out=vp_sb[:, 128:256], in_=din["vpl"]).then_inc(S("ldc"), 16)
            sy.dma_start(out=mask_sb[:, :], in_=din["maskgu"]).then_inc(S("ldc"), 16)

        @block.gpsimd
        def _(g):
            g.dma_start(out=wq_bf, in_=din["wqsl"]).then_inc(S("cast"), 16)
            g.dma_start(out=wk_bf, in_=din["wksl"]).then_inc(S("cast"), 16)
            g.dma_start(out=wv_bf, in_=din["wvt"]).then_inc(S("cast"), 16)
            g.dma_start(out=wo_bf, in_=din["wop"]).then_inc(S("cast"), 16)
            g.dma_start(out=ident[:, :], in_=din["identin"]).then_inc(S("cast"), 16)

        @block.vector
        def _(v):
            v.memset(ones_bf[:, :], 1.0)
            v.memset(ones1f[:, :], 1.0)
            v.memset(eps_t[:, :], EPS).then_inc(S("ve"), 1)

        @block.sync
        def _(sy):
            for wi, w in enumerate(WLIST):
                which = "gu" if w != "d" else "d"
                for pi, pl in enumerate(("A", "B")):
                    wpl = wi * 2 + pi
                    for j in range(4):
                        gp = wpl * 4 + j
                        if gp >= 2:
                            w2, j2 = divmod(gp - 2, 4)
                            sy.wait_ge(S("qprep"), w2 * 5 + j2 + 1)
                        slot = (gp % 2) * 1024
                        gcs = S("gc0") if gp % 2 == 0 else S("gc1")
                        sy.dma_start(out=gc_area[:, slot:slot + 512],
                                     in_=din[f"G{which}{pl}{j}"]).then_inc(gcs, 16)
                        sy.dma_start(out=gc_area[:, slot + 512:slot + 1024],
                                     in_=din[f"C{w}{pl}{j}"]).then_inc(gcs, 16)

        @block.vector
        def _(v):
            for wi, w in enumerate(WLIST):
                for pi in range(2):
                    wpl = wi * 2 + pi
                    acc = q_scr[:, 0:512]; tmp = q_scr[:, 512:1024]
                    for j in range(4):
                        gp = wpl * 4 + j
                        gcs = S("gc0") if gp % 2 == 0 else S("gc1")
                        v.wait_ge(gcs, 32 * (gp // 2 + 1))
                        slot = (gp % 2) * 1024
                        if j == 0:
                            v.tensor_tensor(acc, gc_area[:, slot:slot + 512],
                                            gc_area[:, slot + 512:slot + 1024],
                                            ALU.mult).then_inc(S("qprep"), 1)
                        else:
                            v.tensor_tensor(tmp, gc_area[:, slot:slot + 512],
                                            gc_area[:, slot + 512:slot + 1024],
                                            ALU.mult).then_inc(S("qprep"), 1)
                            v.wait_ge(S("qprep"), wpl * 5 + j + 1)
                            v.tensor_tensor(acc, acc, tmp, ALU.add)
                    if wpl > 0:
                        qs_prev = S(f"qst{(wpl - 1) // 2}")
                        v.wait_ge(qs_prev, 32 * (2 if (wpl - 1) % 2 else 1))
                    v.tensor_copy(q_hl[:, 0:512], acc)
                    v.tensor_tensor(q_hl[:, 512:1024], acc, q_hl[:, 0:512],
                                    ALU.subtract).then_inc(S("qprep"), 1)

        @block.scalar
        def _(sc):
            for wi, w in enumerate(WLIST):
                qv = qdram[w].rearrange("d (pl hl f) -> d pl hl f", pl=2, hl=2)
                for pi in range(2):
                    wpl = wi * 2 + pi
                    sc.wait_ge(S("qprep"), (wpl + 1) * 5)
                    qs = S(f"qst{wi}")
                    sc.dma_start(out=qv[:, pi, 0, :].rearrange("d (b n) -> d b n", b=32),
                                 in_=q_hl[:, 0:512]).then_inc(qs, 16)
                    sc.dma_start(out=qv[:, pi, 1, :].rearrange("d (b n) -> d b n", b=32),
                                 in_=q_hl[:, 512:1024]).then_inc(qs, 16)

        @block.sync
        def _(sy):
            for wi, w in enumerate(WLIST):
                qv = qdram[w].rearrange("d (pl hl f) -> d pl hl f", pl=2, hl=2)
                sy.wait_ge(S(f"qst{wi}"), 64)
                for ch in range(32):
                    gch = wi * 32 + ch
                    if gch >= 2:
                        sy.wait_ge(S("mm"), 2 * (gch - 1))
                    slot = (gch % 2) * 2048
                    qrs = S("qr0") if gch % 2 == 0 else S("qr1")
                    for q, (pi, hl) in enumerate(((0, 0), (0, 1), (1, 0), (1, 1))):
                        sy.dma_start(out=qrows[0:4, slot + q * 512:slot + (q + 1) * 512],
                                     in_=qv[:, pi, hl, ch * 512:(ch + 1) * 512]
                                     ).then_inc(qrs, 16)

        @block.tensor
        def _(t):
            t.wait_ge(S("ldc"), 48)
            for wi, w in enumerate(WLIST):
                for ch in range(32):
                    gch = wi * 32 + ch
                    qrs = S("qr0") if gch % 2 == 0 else S("qr1")
                    t.wait_ge(qrs, 64 * (gch // 2 + 1))
                    if gch >= 1:
                        t.wait_ge(S("ve"), VE0 + gch)
                    slot = (gch % 2) * 2048
                    for bank, pi in ((0, 0), (1, 1)):
                        ah = qrows[0:4, slot + (2 * pi) * 512:slot + (2 * pi + 1) * 512]
                        al = qrows[0:4, slot + (2 * pi + 1) * 512:slot + (2 * pi + 2) * 512]
                        t.matmul(psum[bank][:, :], vp_sb[:, 0:128], ah, start=True, stop=False)
                        t.matmul(psum[bank][:, :], vp_sb[:, 128:256], ah, start=False, stop=False)
                        t.matmul(psum[bank][:, :], vp_sb[:, 0:128], al,
                                 start=False, stop=True).then_inc(S("mm"), 1)

        @block.scalar
        def _(sc):
            for gch in range(96):
                sc.wait_ge(S("mm"), 2 * gch + 1)
                if gch >= 2:
                    ws = S("wst0") if gch % 2 == 0 else S("wst1")
                    sc.wait_ge(ws, 16 * (gch // 2))
                sc.copy(wexp[:, (gch % 2) * 512:(gch % 2 + 1) * 512],
                        psum[0][:, :]).then_inc(S("ev"), 1)

        @block.vector
        def _(v):
            v.wait_ge(S("ldc"), 48)
            for wi, w in enumerate(WLIST):
                for ch in range(32):
                    gch = wi * 32 + ch
                    v.wait_ge(S("ev"), gch + 1)
                    v.wait_ge(S("mm"), 2 * gch + 2)
                    if w == "d" and ch == 0:
                        v.wait_ge(S("cast"), 96)
                    v.copy_predicated(wexp[:, (gch % 2) * 512:(gch % 2 + 1) * 512],
                                      mask_sb[:, ch * 512:(ch + 1) * 512],
                                      psum[1][:, :]).then_inc(S("ve"), 1)

        @block.gpsimd
        def _(g):
            for wi, w in enumerate(WLIST):
                if w == "d":
                    g.wait_ge(S("ve"), VE0 + 64)
                    g.dma_start(out=mask_sb[:, :], in_=din["maskd"]).then_inc(S("cast"), 16)
                for ch in range(32):
                    gch = wi * 32 + ch
                    g.wait_ge(S("ve"), VE0 + gch + 1)
                    ws = S("wst0") if gch % 2 == 0 else S("wst1")
                    g.dma_start(out=wsh[w][:, ch * 512:(ch + 1) * 512],
                                in_=wexp[:, (gch % 2) * 512:(gch % 2 + 1) * 512]
                                ).then_inc(ws, 16)
                g.wait_ge(S("wst0"), 16 * 16 * (wi + 1))
                g.wait_ge(S("wst1"), 16 * 16 * (wi + 1))
                g.collective_compute("AllGather", ALU.bypass,
                                     replica_groups=[list(range(NCORES))],
                                     ins=[wsh[w].opt()], outs=[wfull[w].opt()]
                                     ).then_inc(S("ag"), 1)

        @block.sync
        def _(sy):
            for tc in range(NCH_E):
                if tc > 0:
                    sy.wait_ge(S("st_x1"), tc * 16)
                    if DO_LN2V:
                        sy.wait_ge(S("st_agv"), tc * 16)
                    sy.wait_ge(S("xb"), XBT * tc)
                    sy.wait_ge(S("apl"), APLT * tc)
                sy.dma_start(out=xch[:, :],
                             in_=din["xt"][:, tc * HT * TCH:(tc + 1) * HT * TCH]
                             ).then_inc(S("x"), 16)
                if not DO_GU:
                    continue
                sy.wait_ge(S("ag"), 2)
                if tc > 0 and DO_LN2V:
                    sy.wait_ge(S("vmm"), tc * 16)
                for it in range(NIT):
                    for gu, w in ((0, "g"), (1, "u")):
                        base = 0 if gu == 0 else 4096
                        slot = base + (it % 2) * 2048
                        wsl = S(f"w{'g' if gu == 0 else 'u'}{it % 2}")
                        if it >= 2:
                            sy.wait_ge(S("gu"), tc * 128 + 2 * (it - 2) + gu + 1)
                        r, il = it // 8, it % 8
                        sy.dma_start(out=arena[:, slot:slot + 2048],
                                     in_=wfull[w][r * 128:(r + 1) * 128,
                                                  il * 2048:(il + 1) * 2048]
                                     ).then_inc(wsl, 16)
                if not DO_DN:
                    continue
                sy.wait_ge(S("ag"), 3)
                sy.wait_ge(S("gu"), tc * 128 + 128)
                for ct in range(NCT):
                    slot = (ct % 2) * 8192
                    wsl = S(f"wd{ct % 2}")
                    if ct >= 2:
                        sy.wait_ge(S("dn"), tc * 16 + ct - 1)
                    r, ctl = ct // 2, ct % 2
                    sy.dma_start(out=arena[:, slot:slot + 8192],
                                 in_=wfull["d"][r * 128:(r + 1) * 128,
                                                ctl * 8192:(ctl + 1) * 8192]
                                 ).then_inc(wsl, 16)
                if not DO_LN2V:
                    continue
                sy.wait_ge(S("dn"), tc * 16 + 16)
                for dc in range(4):
                    if dc >= 2:
                        sy.wait_ge(S("vmm"), tc * 16 + 4 * (dc - 1))
                    sy.dma_start(out=arena[:, (dc % 2) * 8192:(dc % 2) * 8192 + 8192],
                                 in_=wv_bf[:, dc * 8192:(dc + 1) * 8192]
                                 ).then_inc(S("wv0") if dc % 2 == 0 else S("wvb"), 16)

        @block.scalar
        def _(sc):
            for tc in range(NCH_E):
                sc.wait_ge(S("x"), (tc + 1) * 16)
                for ln in range(NLN):
                    lnn = tc * NLN + ln
                    if ln == 1:
                        sc.wait_ge(S("dn"), tc * 16 + 16)
                    elif tc > 0:
                        sc.wait_ge(S("st_xln2") if DO_LN2V else S("st_x1"), tc * 16)
                    for ht in range(HT):
                        if ln == 1:
                            sc.wait_ge(S("res"), tc * 16 + ht + 1)
                        sc.copy(xln[:, xcol(ht)], xch[:, xcol(ht)])
                        sc.activation(inter[:, ht * 512:(ht + 1) * 512],
                                      xch[:, xcol(ht)], AF.Square).then_inc(S("xb"), 1)
                    sc.wait_ge(S("rowv"), lnn + 1)
                    sc.activation(scr[0:1, ROW_T:ROW_T + 512], scr[0:1, ROW_V:ROW_V + 512],
                                  AF.Ln, bias=eps_t[0:1, :])
                    sc.activation(scr[0:1, ROW_T:ROW_T + 512], scr[0:1, ROW_T:ROW_T + 512],
                                  AF.Exp, scale=-0.5).then_inc(S("rows"), 1)
                    if ln == 0 and DO_SILU:
                        for it in range(NIT):
                            sc.wait_ge(S("gu"), tc * 128 + 2 * it + 1)
                            sl = it % 2
                            sc.activation(xbx[:, sl * 512:(sl + 1) * 512], psum[PG][:, :],
                                          AF.Silu).then_inc(S("silu"), 1)
                if DO_LN2V:
                    for dc in range(4):
                        for tt in range(4):
                            gg = tc * 16 + dc * 4 + tt
                            sc.wait_ge(S("vmm"), gg + 1)
                            if dc == 0 and tt == 0:
                                sc.wait_ge(S("st_x1"), (tc + 1) * 16)
                            sc.copy(xch[:, tt * DM + dc * 512:tt * DM + (dc + 1) * 512],
                                    psum[PV][:, :]).then_inc(S("vev"), 1)

        @block.tensor
        def _(t):
            for tc in range(NCH_E):
                if tc == 0:
                    t.wait_ge(S("ve"), VE0 + 96)
                for ln in range(NLN):
                    lnn = tc * NLN + ln
                    if lnn >= 1:
                        t.wait_ge(S("rowv"), lnn)
                    t.wait_ge(S("xb"), lnn * 16 + 16)
                    for ht in range(HT):
                        t.matmul(psum[PS1][0:1, :], ones_bf[:, :], xln[:, xcol(ht)],
                                 start=(ht == 0), stop=(ht == HT - 1))
                    for ht in range(HT):
                        mmi = t.matmul(psum[PS2][0:1, :], ones_bf[:, :],
                                       inter[:, ht * 512:(ht + 1) * 512],
                                       start=(ht == 0), stop=(ht == HT - 1))
                        if ht == HT - 1:
                            mmi.then_inc(S("stat"), 1)
                    t.wait_ge(S("rows"), lnn + 1)
                    if lnn >= 1:
                        t.wait_ge(S("apl"), lnn * 16)
                    t.matmul(psum[PM][:, :], ones1f[:, :], scr[0:1, ROW_M:ROW_M + 512],
                             start=True, stop=True)
                    t.matmul(psum[PR][:, :], ones1f[:, :], scr[0:1, ROW_T:ROW_T + 512],
                             start=True, stop=True).then_inc(S("bc"), 1)
                    if ln == 0 and not DO_GU:
                        pass
                    elif ln == 0:
                        t.wait_ge(S("apl"), lnn * 16 + 16)
                        if tc > 0 and DO_LN2V:
                            t.wait_ge(S("vev"), tc * 16)
                        for it in range(NIT):
                            for gu, bank in ((0, PG), (1, PU)):
                                wsl = S(f"w{'g' if gu == 0 else 'u'}{it % 2}")
                                t.wait_ge(wsl, 16 * (tc * 32 + it // 2 + 1))
                                if it >= 1:
                                    if gu == 0 and DO_SILU:
                                        t.wait_ge(S("silu"), tc * 64 + it)
                                    elif gu == 1 and DO_INT:
                                        t.wait_ge(S("int_"), tc * 64 + it)
                                base = 0 if gu == 0 else 4096
                                slot = base + (it % 2) * 2048
                                for hb in range(HT):
                                    mmi = t.matmul(psum[bank][:, :],
                                                   arena[:, slot + hb * 128:slot + (hb + 1) * 128],
                                                   xln[:, xcol(hb)],
                                                   start=(hb == 0),
                                                   stop=(hb == HT - 1))
                                mmi.then_inc(S("gu"), 1)
                        for ct in range(NCT if DO_DN else 0):
                            wsl = S(f"wd{ct % 2}")
                            t.wait_ge(wsl, 16 * (tc * 8 + ct // 2 + 1))
                            t.wait_ge(S("int_"), tc * 64 + 64)
                            t.wait_ge(S("res"), tc * 16 + ct)
                            slot = (ct % 2) * 8192
                            for it in range(NIT):
                                mmi = t.matmul(psum[PD][:, :],
                                               arena[:, slot + it * 128:slot + (it + 1) * 128],
                                               inter[:, it * 512:(it + 1) * 512],
                                               start=(it == 0), stop=(it == NIT - 1))
                            mmi.then_inc(S("dn"), 1)
                    else:
                        t.wait_ge(S("apl"), lnn * 16 + 16)
                        t.wait_ge(S("silu"), (tc + 1) * 64)
                        for dc in range(4):
                            t.wait_ge(S("wv0") if dc % 2 == 0 else S("wvb"),
                                      16 * (tc * 2 + dc // 2 + 1))
                            for tt in range(4):
                                gg = tc * 16 + dc * 4 + tt
                                if gg >= 1:
                                    t.wait_ge(S("vev"), gg)
                                for hb in range(HT):
                                    mmi = t.matmul(psum[PV][:, :],
                                                   xln[:, xcol(hb, tt * 128, 128)],
                                                   arena[:, (dc % 2) * 8192 + hb * 512:
                                                          (dc % 2) * 8192 + (hb + 1) * 512],
                                                   start=(hb == 0), stop=(hb == HT - 1))
                                mmi.then_inc(S("vmm"), 1)

        @block.vector
        def _(v):
            for tc in range(NCH_E):
                for ln in range(NLN):
                    lnn = tc * NLN + ln
                    v.wait_ge(S("stat"), lnn + 1)
                    v.tensor_scalar(scr[0:1, ROW_M:ROW_M + 512], psum[PS1][0:1, :],
                                    1.0 / DM, None, ALU.mult)
                    v.tensor_scalar(scr[0:1, ROW_V:ROW_V + 512], psum[PS2][0:1, :],
                                    1.0 / DM, None, ALU.mult)
                    v.tensor_tensor(scr[0:1, ROW_T:ROW_T + 512], scr[0:1, ROW_M:ROW_M + 512],
                                    scr[0:1, ROW_M:ROW_M + 512], ALU.mult
                                    ).then_inc(S("rv2"), 1)
                    v.wait_ge(S("rv2"), lnn + 1)
                    v.tensor_tensor(scr[0:1, ROW_V:ROW_V + 512], scr[0:1, ROW_V:ROW_V + 512],
                                    scr[0:1, ROW_T:ROW_T + 512],
                                    ALU.subtract).then_inc(S("rowv"), 1)
                    v.wait_ge(S("bc"), lnn + 1)
                    if ln == 0 and tc > 0:
                        v.wait_ge(S("st_xln2") if DO_LN2V else S("st_x1"), tc * 16)
                    for ht in range(HT):
                        v.tensor_tensor(scr[:, 0:512], xch[:, xcol(ht)], psum[PM][:, :],
                                        ALU.subtract).then_inc(S("ap2"), 1)
                        v.wait_ge(S("ap2"), lnn * 16 + ht + 1)
                        v.tensor_tensor(xln[:, xcol(ht)], scr[:, 0:512], psum[PR][:, :],
                                        ALU.mult).then_inc(S("apl"), 1)
                    if ln == 0 and DO_INT:
                        for it in range(NIT):
                            v.wait_ge(S("silu"), tc * 64 + it + 1)
                            v.wait_ge(S("gu"), tc * 128 + 2 * it + 2)
                            sl = it % 2
                            v.tensor_tensor(inter[:, it * 512:(it + 1) * 512],
                                            xbx[:, sl * 512:(sl + 1) * 512], psum[PU][:, :],
                                            ALU.mult).then_inc(S("int_"), 1)
                        for ct in range(NCT if DO_DN else 0):
                            v.wait_ge(S("dn"), tc * 16 + ct + 1)
                            v.tensor_tensor(xch[:, xcol(ct)], psum[PD][:, :],
                                            xch[:, xcol(ct)], ALU.add).then_inc(S("res"), 1)

        @block.gpsimd
        def _(g):
            for tc in range(NCH_E):
                if DO_DN:
                    g.wait_ge(S("res"), tc * 16 + 16)
                elif DO_INT:
                    g.wait_ge(S("int_"), tc * 64 + 64)
                elif DO_SILU:
                    g.wait_ge(S("silu"), tc * 64 + 64)
                elif DO_GU:
                    g.wait_ge(S("gu"), tc * 128 + 128)
                else:
                    g.wait_ge(S("apl"), tc * APLT + 16)
                g.dma_start(out=resid[:, tc * HT * TCH:(tc + 1) * HT * TCH], in_=xch[:, :]
                            ).then_inc(S("st_x1"), 16)
                if not DO_LN2V:
                    continue
                g.wait_ge(S("apl"), tc * 32 + 32)
                g.dma_start(out=agx_in[tc], in_=xln[:, :]).then_inc(S("st_xln2"), 16)
                g.wait_ge(S("st_xln2"), (tc + 1) * 16)
                g.collective_compute("AllGather", ALU.bypass,
                                     replica_groups=[list(range(NCORES))],
                                     ins=[agx_in[tc].opt()], outs=[agx[tc].opt()]
                                     ).then_inc(S("ag"), 1)
                g.wait_ge(S("vev"), tc * 16 + 16)
                g.dma_start(out=agv_in[tc], in_=xch[:, :]).then_inc(S("st_agv"), 16)
                g.wait_ge(S("st_agv"), (tc + 1) * 16)
                g.collective_compute("AllGather", ALU.bypass,
                                     replica_groups=[list(range(NCORES))],
                                     ins=[agv_in[tc].opt()], outs=[agv[tc].opt()]
                                     ).then_inc(S("ag"), 1)

        if MK_PHASES >= 3 and MK_CUT == 0:
            attp = _attn_schedule()
            SCHED = attp["sched"]
            PKB = attp["packed_kbs"]
            NMASK = attp["masks"].shape[0]
            PP0 = 0
            KD0 = 1536
            VD0 = 3712
            VW0 = 5905
            Qsb = [inter[:, 0:8192], inter[:, 8192:16384]]
            Ksb = [inter[:, 16384:24576], inter[:, 24576:32768]]
            wq_sb = xln[:, 0:4096]
            wk_sb = xln[:, 4096:8192]
            COS = scr[:, 0:512]; SIN = scr[:, 512:1024]
            T1 = scr[:, 1024:1536]; QS = scr[:, 1536:2048]

            tile_tab = {}
            for s in range(2):
                for b in range(2):
                    u = s * 2 + b
                    for qc in range(16):
                        sc_ = SCHED[qc]
                        tl = []
                        npk = sc_["n_pk_tiles"]
                        for tau in range(npk):
                            mid = sc_["pk_bnd"] if (tau == npk - 1 and
                                                    sc_["n_pk_tok"] % 128 == 64) else None
                            tl.append(dict(kind="pk", idx=tau, mid=mid,
                                           qs=[True] * 4))
                        for (wt, mid, qsub) in sc_["wtiles"]:
                            tl.append(dict(kind="win", idx=wt, mid=mid, qs=qsub))
                        tile_tab[(u, qc)] = tl
            win_first = {}; win_last = {}
            for qc in range(16):
                for t_ in tile_tab[(0, qc)]:
                    if t_["kind"] == "win":
                        wt = t_["idx"]
                        win_first.setdefault(wt, qc)
                        win_last[wt] = qc
            new_tiles = {qc: sorted(wt for wt, f in win_first.items() if f == qc)
                         for qc in range(16)}
            for wt in win_first:
                if wt >= 16:
                    assert win_last[wt - 16] < win_first[wt], (wt,)
            for qc in range(16):
                wl = [t_["idx"] % 16 for t_ in tile_tab[(0, qc)]
                      if t_["kind"] == "win"]
                assert len(set(wl)) == len(wl), (qc,)
            smm_ord = {}; msk_ord = {}; exp_ord = {}; ctm_ord = {}
            vwn_cnt = {}; vdl_cnt = {}
            n_smm = n_msk = n_exp = n_ctm = n_vdl = 0
            n_vw = [0, 0, 0, 0]
            for s in range(2):
                for b in range(2):
                    u = s * 2 + b
                    n_vdl += 33
                    vdl_cnt[u] = n_vdl
                    for qc in range(16):
                        tl = tile_tab[(u, qc)]
                        n_vw[qc % 4] += len(new_tiles[qc])
                        vwn_cnt[(u, qc)] = n_vw[qc % 4]
                        for ti, t_ in enumerate(tl):
                            n_smm += 1
                            smm_ord[(u, qc, ti)] = n_smm
                            if t_["mid"] is not None:
                                n_msk += 1
                                msk_ord[(u, qc, ti)] = n_msk
                            n_exp += 1
                            exp_ord[(u, qc, ti)] = n_exp
                        for qs in range(4):
                            n_ctm += 1
                            ctm_ord[(u, qc, qs)] = n_ctm
            TOT_SMM, TOT_EXP, TOT_CTM = n_smm, n_exp, n_ctm
            ctx_fl = {}
            for key, tl in tile_tab.items():
                u, qc = key
                for qs in range(4):
                    idxs = [ti for ti, t_ in enumerate(tl) if t_["qs"][qs]]
                    ctx_fl[(u, qc, qs)] = (idxs[0], idxs[-1], idxs)
            ctp_fence = {}
            _ctp_run = 0
            for s in range(2):
                for b in range(2):
                    u = s * 2 + b
                    for qc in range(16):
                        tl = tile_tab[(u, qc)]
                        for ti, t_ in enumerate(tl):
                            o = exp_ord[(u, qc, ti)]
                            inc_qs = [qs for qs in range(4) if t_["qs"][qs]]
                            endings = [qs for qs in inc_qs
                                       if ctx_fl[(u, qc, qs)][1] == ti]
                            if inc_qs and inc_qs[-1] in endings:
                                ctp_fence[o] = ("ctm", ctm_ord[(u, qc, inc_qs[-1])])
                            else:
                                _ctp_run += 1
                                ctp_fence[o] = ("ctp", _ctp_run)

            def vsrc_col(t_, i_win):
                if t_["kind"] == "pk":
                    return VD0 + t_["idx"] * 129
                return VW0 + i_win * 129

            def win_rows(s, wt):
                g = wt * 128
                rv = 4 * s + g // 2048
                tlc = (g % 2048)
                j = tlc // 512
                tt = (tlc % 512) // 128
                return rv, j, tt

            rope_ord = lambda s, tcg, b, qk: (s * 16 + tcg) * 4 + b * 2 + qk

            ACUT = int(os.environ.get("MK_ACUT", "0")) or 99
            A_NEG = ACUT >= 2
            A_STG = ACUT >= 3
            A_SCR = ACUT >= 4
            A_CTX = ACUT >= 5
            A_TRN = ACUT >= 6
            A_EVA = ACUT >= 7
            A_STC = ACUT >= 8
            A_AGC = ACUT >= 9

            pid = nc.partition_id()
            hv_b = [pid * 2, pid * 2 + 1]

            @block.sync
            def _(sy):
                sy.wait_ge(S("vmm"), 64)
                sy.wait_ge(S("st_xln2"), 64)
                sy.wait_ge(S("cast"), 96)
                for ht in range(HT):
                    sy.dma_start(out=wq_sb[:, ht * 256:(ht + 1) * 256],
                                 in_=wq_bf[ht * 128:(ht + 1) * 128, :]).then_inc(S("wqk"), 16)
                    sy.dma_start(out=wk_sb[:, ht * 256:(ht + 1) * 256],
                                 in_=wk_bf[ht * 128:(ht + 1) * 128, :]).then_inc(S("wqk"), 16)
                sy.wait_ge(S("st_agv"), 64)
                sy.wait_ge(S("vev"), 64)
                sy.dma_start(out=xch[:, 0:NMASK * 512], in_=din["amask"]
                             ).then_inc(S("amk"), 16)
                for s in range(2):
                    for tcg in range(16):
                        o = s * 16 + tcg
                        if s == 1 and tcg <= 1 and A_CTX:
                            sy.wait_ge(S("ctm"), ctm_ord[(1, 15, 3)])
                        if o >= 2:
                            sy.wait_ge(S("qkm"), o * 4 - 4)
                        rg, j = 4 * s + tcg // 4, tcg % 4
                        sy.wait_ge(S("ag"), 4 + 2 * j)
                        axs = S("ax0") if o % 2 == 0 else S("ax1")
                        sy.dma_start(out=arena[:, (o % 2) * 8192:(o % 2 + 1) * 8192],
                                     in_=agx[j][rg * 128:(rg + 1) * 128, :]
                                     ).then_inc(axs, 16)
                        if o >= 1:
                            sy.wait_ge(S("radd"), o * 4)
                        sy.dma_start(out=COS, in_=din["ropec"][:, tcg * 512:(tcg + 1) * 512]
                                     ).then_inc(S("rop"), 16)
                        sy.dma_start(out=SIN, in_=din["ropes"][:, tcg * 512:(tcg + 1) * 512]
                                     ).then_inc(S("rop"), 16)
                    for b in range(2):
                        if not A_STG:
                            continue
                        u = s * 2 + b
                        sy.wait_ge(S("radd"), (s + 1) * 64)
                        if u > 0 and A_CTX:
                            sy.wait_ge(S("ctm"), ctm_ord[(u - 1, 15, 3)])
                        sy.wait_ge(S("msv"), u + 1)
                        sy.wait_ge(S("ag"), 11)
                        for j in range(4):
                            sy.dma_start(
                                out=vstage[j * 512:(j + 1) * 512, :].rearrange(
                                    "r (tt d) -> r tt d", tt=4),
                                in_=agv[j][4 * s * 128:(4 * s + 4) * 128, :].rearrange(
                                    "r (tt d) -> r tt d", tt=4)[:, :, bass.ts(hv_b[b], 128)],
                                bounds_check=None
                            ).then_inc(S("vst"), 16)
                        sy.wait_ge(S("vst"), 16 * 4 * (u + 1))
                        for m, kb in enumerate(PKB):
                            g0 = kb * 64
                            rv_l = (g0 // 2048)
                            tlc = g0 % 2048
                            j = tlc // 512
                            tt = (tlc % 512) // 128
                            prow = tlc % 128
                            row0 = j * 512 + rv_l * 128 + prow
                            drow = (m % 2) * 64
                            dst = arena[drow:drow + 64,
                                        VD0 + (m // 2) * 129 + 0:VD0 + (m // 2) * 129 + 128]
                            sy.dma_start(out=dst,
                                         in_=vstage[row0:row0 + 64,
                                                    tt * 128:(tt + 1) * 128]
                                         ).then_inc(S("vdl"), 16)
                        for qc in range(16):
                            for wt in new_tiles[qc]:
                                if wt >= 16 and A_CTX:
                                    sy.wait_ge(S("ctm"),
                                               ctm_ord[(u, win_last[wt - 16], 3)])
                                rv, j, tt = win_rows(s, wt)
                                rv_l = rv - 4 * s
                                row0 = j * 512 + rv_l * 128
                                sl16 = wt % 16
                                dst = arena[:, VW0 + sl16 * 129:VW0 + sl16 * 129 + 128]
                                sy.dma_start(out=dst,
                                             in_=vstage[row0:row0 + 128,
                                                        tt * 128:(tt + 1) * 128]
                                             ).then_inc(S(f"vw{qc % 4}"), 16)

            @block.vector
            def _(v):
                v.wait_ge(S("stat"), 8)
                v.memset(small[0:1, 0:16], 0.0).then_inc(S("nrm"), 1)
                n_rcp = 0
                for s in range(2):
                    for tcg in range(16):
                        o = s * 16 + tcg
                        for b in range(2):
                            for qk in range(2):
                                R = rope_ord(s, tcg, b, qk)
                                v.wait_ge(S("qkm"), o * 4 + b * 2 + qk + 1)
                                v.wait_ge(S("rop"), 32 * (o + 1))
                                if o == 0 and b == 0 and qk == 0:
                                    v.wait_ge(S("dn"), 64)
                                pb = psum[PG] if qk == 0 else psum[PU]
                                dst = (Qsb if qk == 0 else Ksb)[b][:, tcg * 512:(tcg + 1) * 512]
                                v.tensor_tensor(T1, pb[:, :], COS, ALU.mult)
                                v.tensor_copy(QS[0:64, :], pb[64:128, :])
                                v.tensor_copy(QS[64:128, :], pb[0:64, :]
                                              ).then_inc(S("rp2"), 1)
                                v.wait_ge(S("rp2"), 2 * R + 1)
                                v.tensor_tensor(QS, QS, SIN, ALU.mult
                                                ).then_inc(S("rp2"), 1)
                                v.wait_ge(S("rp2"), 2 * R + 2)
                                v.tensor_tensor(dst, T1, QS, ALU.add).then_inc(S("radd"), 1)
                                v.wait_ge(S("nmm"), R + 1)
                                v.tensor_reduce(small[0:1, 30:31], psum[PD][0:1, :],
                                                mybir.AxisListType.X, ALU.max)
                                col = (0 if qk == 0 else 4) + s * 2 + b
                                v.tensor_tensor(small[0:1, col:col + 1],
                                                small[0:1, col:col + 1],
                                                small[0:1, 30:31], ALU.max).then_inc(S("nrm"), 1)
                    for b in range(2):
                        if not A_NEG:
                            continue
                        u = s * 2 + b
                        v.wait_ge(S("nrm"), 1 + (s + 1) * 64)
                        v.tensor_tensor(small[0:1, 8 + u:9 + u], small[0:1, u:u + 1],
                                        small[0:1, 4 + u:5 + u], ALU.mult).then_inc(S("ngv"), 1)
                    for b in range(2):
                        if not A_STG:
                            continue
                        u = s * 2 + b
                        v.wait_ge(S("radd"), (s + 1) * 64)
                        if u > 0 and A_CTX:
                            v.wait_ge(S("ctm"), ctm_ord[(u - 1, 15, 3)])
                        for τ in range(17):
                            v.memset(arena[:, VD0 + τ * 129 + 128:VD0 + τ * 129 + 129], 1.0)
                        for i_w in range(16):
                            v.memset(arena[:, VW0 + i_w * 129 + 128:VW0 + i_w * 129 + 129], 1.0)
                        v.memset(arena[64:128, VD0 + 16 * 129:VD0 + 16 * 129 + 128], 0.0)
                        v.memset(arena[:, KD0 + 2112:KD0 + 2176], 0.0)
                        for m, kb in enumerate(PKB):
                            cpi = v.tensor_copy(arena[:, KD0 + m * 64:KD0 + (m + 1) * 64],
                                                Ksb[b][:, kb * 64:(kb + 1) * 64])
                        cpi.then_inc(S("msv"), 1)
                        for qc in range(16):
                            tl = tile_tab[(u, qc)]
                            for ti, t_ in enumerate(tl):
                                if t_["mid"] is None or not A_SCR:
                                    continue
                                bank = psum[(0, 1, 6)[smm_ord[(u, qc, ti)] % 3]]
                                v.wait_ge(S("smm"), smm_ord[(u, qc, ti)])
                                v.wait_ge(S("amk"), 16)
                                v.tensor_tensor(bank[:, :], bank[:, :],
                                                xch[:, t_["mid"] * 512:(t_["mid"] + 1) * 512],
                                                ALU.add).then_inc(S("msk"), 1)
                            for qs in range(4):
                                if not A_CTX:
                                    continue
                                od = ctm_ord[(u, qc, qs)]
                                v.wait_ge(S("ctm"), od)
                                if od > 4 and A_TRN:
                                    v.wait_ge(S("ctt"), od - 4)
                                v.reciprocal(small[:, 20 + qs:21 + qs],
                                             psum[2 + qs][:, 128:129]
                                             ).then_inc(S("rc2"), 1)
                                v.wait_ge(S("rc2"), od)
                                v.tensor_scalar(xbx[:, 1024 + (od % 4) * 128:
                                                    1024 + (od % 4) * 128 + 128],
                                                psum[2 + qs][:, 0:128],
                                                small[:, 20 + qs:21 + qs], None,
                                                ALU.mult).then_inc(S("rcp"), 1)

            @block.scalar
            def _(sc):
                for s in range(2):
                    for tcg in range(16):
                        for b in range(2):
                            for qk in range(2):
                                R = rope_ord(s, tcg, b, qk)
                                sc.wait_ge(S("radd"), R + 1)
                                if R >= 2:
                                    sc.wait_ge(S("nmm"), R - 1)
                                dst = (Qsb if qk == 0 else Ksb)[b][:, tcg * 512:(tcg + 1) * 512]
                                sc.activation(xbx[:, (R % 2) * 512:(R % 2 + 1) * 512],
                                              dst, AF.Square).then_inc(S("sqn"), 1)
                    for b in range(2):
                        if not A_NEG:
                            continue
                        u = s * 2 + b
                        sc.wait_ge(S("ngv"), u + 1)
                        sc.activation(small[0:1, 8 + u:9 + u], small[0:1, 8 + u:9 + u],
                                      AF.Ln, bias=eps_t[0:1, :])
                        sc.activation(small[0:1, 8 + u:9 + u], small[0:1, 8 + u:9 + u],
                                      AF.Exp, scale=0.5).then_inc(S("ngs"), 1)
                        sc.wait_ge(S("ngm"), u + 1)
                        sc.activation(small[:, 16 + u:17 + u], psum[PD][:, 0:1],
                                      AF.Copy, scale=-SCALE).then_inc(S("ngc"), 1)
                    for b in range(2):
                        u = s * 2 + b
                        if A_NEG:
                            sc.wait_ge(S("ngc"), u + 1)
                        for qc in range(16):
                            tl = tile_tab[(u, qc)]
                            for ti, t_ in enumerate(tl):
                                if not A_SCR:
                                    continue
                                o = exp_ord[(u, qc, ti)]
                                sc.wait_ge(S("smm"), smm_ord[(u, qc, ti)])
                                if t_["mid"] is not None:
                                    sc.wait_ge(S("msk"), msk_ord[(u, qc, ti)])
                                if o > 3 and A_CTX:
                                    fs, fv = ctp_fence[o - 3]
                                    sc.wait_ge(S(fs), fv)
                                sc.activation(arena[:, (o % 3) * 512:(o % 3 + 1) * 512],
                                              psum[(0, 1, 6)[smm_ord[(u, qc, ti)] % 3]][:, :], AF.Exp,
                                              bias=small[:, 16 + u:17 + u],
                                              scale=SCALE).then_inc(S("sexp"), 1)
                            def emit_evac(qcX):
                                for qs in range(4):
                                    if not A_EVA:
                                        continue
                                    od = ctm_ord[(u, qcX, qs)]
                                    sc.wait_ge(S("ctt"), od)
                                    if od > 2 and A_STC:
                                        sc.wait_ge(S("st_ctx"), 16 * (od - 2))
                                    sc.copy(q_hl[:, (od % 2) * 128:(od % 2) * 128 + 128],
                                            psbf[:, (od % 2) * 128:(od % 2) * 128 + 128]
                                            ).then_inc(S("cte"), 1)
                            if qc > 0:
                                emit_evac(qc - 1)
                            if qc == 15:
                                emit_evac(15)

            @block.tensor
            def _(t):
                t.wait_ge(S("wqk"), 32 * 16)
                t.wait_ge(S("vev"), 64)
                t.wait_ge(S("int_"), 256)
                t.wait_ge(S("res"), 64)
                t.wait_ge(S("rowv"), 8)
                t.wait_ge(S("apl"), 128)
                t.wait_ge(S("dn"), 64)
                for s in range(2):
                    if s == 1 and A_CTX:
                        t.wait_ge(S("rcp"), ctm_ord[(1, 15, 3)])
                    for tcg in range(16):
                        o = s * 16 + tcg
                        axs = S("ax0") if o % 2 == 0 else S("ax1")
                        t.wait_ge(axs, 16 * (o // 2 + 1))
                        slab = arena[:, (o % 2) * 8192:(o % 2 + 1) * 8192]
                        for b in range(2):
                            for qk in range(2):
                                R = rope_ord(s, tcg, b, qk)
                                if R >= 2:
                                    t.wait_ge(S("radd"), R - 1)
                                w_sb = wq_sb if qk == 0 else wk_sb
                                pb = psum[PG] if qk == 0 else psum[PU]
                                for ht in range(HT):
                                    mmi = t.matmul(pb[:, :],
                                                   w_sb[:, ht * 256 + b * 128:
                                                        ht * 256 + b * 128 + 128],
                                                   slab[:, ht * 512:(ht + 1) * 512],
                                                   start=(ht == 0), stop=(ht == HT - 1))
                                mmi.then_inc(S("qkm"), 1)
                                t.wait_ge(S("sqn"), R + 1)
                                if R >= 1:
                                    t.wait_ge(S("nrm"), 1 + R)
                                if R == 64 and A_NEG:
                                    t.wait_ge(S("ngc"), 2)
                                t.matmul(psum[PD][0:1, :], ones_bf[:, :],
                                         xbx[:, (R % 2) * 512:(R % 2 + 1) * 512],
                                         start=True, stop=True).then_inc(S("nmm"), 1)
                    for b in range(2):
                        if not A_NEG:
                            continue
                        u = s * 2 + b
                        t.wait_ge(S("ngs"), u + 1)
                        t.wait_ge(S("nrm"), 1 + (s + 1) * 64)
                        t.matmul(psum[PD][:, 0:1], ones1f[:, :],
                                 small[0:1, 8 + u:9 + u],
                                 start=True, stop=True).then_inc(S("ngm"), 1)
                    t.wait_ge(S("radd"), (s + 1) * 64)
                    if A_NEG:
                        t.wait_ge(S("ngc"), 2 * (s + 1))
                    for b in range(2):
                        if not A_STG:
                            continue
                        u = s * 2 + b
                        t.wait_ge(S("msv"), u + 1)
                        for qc in range(16):
                            tl = tile_tab[(u, qc)]
                            wins_before = {}
                            iw = 0
                            for ti, t_ in enumerate(tl):
                                if t_["kind"] == "win":
                                    wins_before[ti] = iw
                                    iw += 1
                            def emit_ctx(ti):
                                t_ = tl[ti]
                                o = exp_ord[(u, qc, ti)]
                                t.wait_ge(S("sexp"), o)
                                last_mmi = None
                                if t_["kind"] == "pk":
                                    t.wait_ge(S("vdl"), 16 * 33 * (u + 1))
                                    vc = VD0 + t_["idx"] * 129
                                else:
                                    t.wait_ge(S(f"vw{qc % 4}"), 16 * vwn_cnt[(u, qc)])
                                    vc = VW0 + (t_["idx"] % 16) * 129
                                for qs in range(4):
                                    if not t_["qs"][qs]:
                                        continue
                                    fi, la, _ = ctx_fl[(u, qc, qs)]
                                    if ti == fi:
                                        prev = (u, qc - 1, qs) if qc > 0 else \
                                               ((u - 1, 15, qs) if u > 0 else None)
                                        if prev is not None:
                                            t.wait_ge(S("rcp"), ctm_ord[prev])
                                    mmi = t.matmul(psum[2 + qs][:, 0:129],
                                                   arena[:, (o % 3) * 512 + qs * 128:
                                                          (o % 3) * 512 + qs * 128 + 128],
                                                   arena[:, vc:vc + 129],
                                                   start=(ti == fi), stop=(ti == la))
                                    last_mmi = mmi
                                    if ti == la:
                                        mmi.then_inc(S("ctm"), 1)
                                if last_mmi is not None and \
                                        ctp_fence[o][0] == "ctp":
                                    last_mmi.then_inc(S("ctp"), 1)
                            for ti, t_ in enumerate(tl):
                                if not A_SCR:
                                    continue
                                smo = smm_ord[(u, qc, ti)]
                                if smo > 3:
                                    t.wait_ge(S("sexp"), smo - 3)
                                if t_["kind"] == "pk":
                                    ksl = arena[:, KD0 + t_["idx"] * 128:
                                                KD0 + (t_["idx"] + 1) * 128]
                                else:
                                    ksl = Ksb[b][:, t_["idx"] * 128:(t_["idx"] + 1) * 128]
                                t.matmul(psum[(0, 1, 6)[smo % 3]][:, :], ksl,
                                         Qsb[b][:, qc * 512:(qc + 1) * 512],
                                         start=True, stop=True).then_inc(S("smm"), 1)
                                if ti >= 1 and A_CTX:
                                    emit_ctx(ti - 1)
                            if A_CTX:
                                emit_ctx(len(tl) - 1)
                            def emit_trn(qcX):
                                for qs in range(4):
                                    if not A_TRN:
                                        continue
                                    od = ctm_ord[(u, qcX, qs)]
                                    t.wait_ge(S("rcp"), od)
                                    if od > 1 and A_EVA:
                                        t.wait_ge(S("cte"), od - 1)
                                    t.transpose(psbf[:, (od % 2) * 128:(od % 2) * 128 + 128],
                                                xbx[:, 1024 + (od % 4) * 128:
                                                    1024 + (od % 4) * 128 + 128],
                                                ident[:, :]).then_inc(S("ctt"), 1)
                            if qc > 0:
                                emit_trn(qc - 1)
                            if qc == 15:
                                emit_trn(15)

            @block.gpsimd
            def _(g):
                for s in range(2):
                    for b in range(2):
                        if not A_STC:
                            continue
                        u = s * 2 + b
                        for qc in range(16):
                            for qs in range(4):
                                od = ctm_ord[(u, qc, qs)]
                                g.wait_ge(S("cte"), od)
                                g.dma_start(out=agc_in[u][:, qc * 512 + qs * 128:
                                                          qc * 512 + qs * 128 + 128],
                                            in_=q_hl[:, (od % 2) * 128:(od % 2) * 128 + 128]
                                            ).then_inc(S("st_ctx"), 16)
                        if not A_AGC:
                            continue
                        g.wait_ge(S("st_ctx"), 16 * 64 * (u + 1))
                        g.collective_compute("AllGather", ALU.bypass,
                                             replica_groups=[list(range(NCORES))],
                                             ins=[agc_in[u].opt()],
                                             outs=[agc[u * 1024:(u + 1) * 1024, :].opt()]
                                             ).then_inc(S("ag"), 1)

        if MK_PHASES >= 4 and MK_CUT == 0:
            pid4 = nc.partition_id()
            iu_b = [(pid4 // 4) * 2, (pid4 // 4) * 2 + 1]
            col_tc = [(pid4 % 4) * 4 + t_ for t_ in range(NCH)]
            CXB = 8192

            @block.sync
            def _(sy):
                sy.wait_ge(S("ag"), 15)
                for tc in range(NCH):
                    if tc > 0:
                        sy.wait_ge(S("st_out"), tc * 16)
                        sy.wait_ge(S("xb"), 128 + 16 * tc)
                        sy.wait_ge(S("apl"), 128 + 16 * tc)
                    sy.dma_start(out=xch[:, :],
                                 in_=resid[:, tc * HT * TCH:(tc + 1) * HT * TCH]
                                 ).then_inc(S("x"), 16)
                    if tc > 0:
                        sy.wait_ge(S("opj"), 16 * tc)
                    from concourse.ap import AP as _BAP
                    for b in range(2):
                        off = iu_b[b] * (1024 * SEQ) + (pid4 % 4) * TPC + tc * 512
                        sy.dma_start(
                            out=inter[:, CXB + b * 4096:CXB + (b + 1) * 4096].rearrange(
                                "p (r c) -> p r c", r=8),
                            in_=_BAP(agc_h, off,
                                     [[SEQ, 128], [128 * SEQ, 8], [1, 512]]),
                            bounds_check=None
                        ).then_inc(S("cxi"), 16)
                    for ct in range(NCT):
                        wos = S("wo0") if ct % 2 == 0 else S("wo1")
                        if ct >= 2:
                            sy.wait_ge(S("opj"), tc * 16 + ct - 1)
                        sy.dma_start(out=arena[:, (ct % 2) * 2048:(ct % 2 + 1) * 2048],
                                     in_=wo_bf[:, ct * 2048:(ct + 1) * 2048]
                                     ).then_inc(wos, 16)
                    sy.wait_ge(S("opj"), 16 * (tc + 1))
                    tw = 4 + tc
                    for it in range(NIT):
                        for gu, w in ((0, "g"), (1, "u")):
                            base = 0 if gu == 0 else 4096
                            slot = base + (it % 2) * 2048
                            wsl = S(f"w{'g' if gu == 0 else 'u'}{it % 2}")
                            if it >= 2:
                                sy.wait_ge(S("gu"), 512 + tc * 128 + 2 * (it - 2) + gu + 1)
                            r, il = it // 8, it % 8
                            sy.dma_start(out=arena[:, slot:slot + 2048],
                                         in_=wfull[w][r * 128:(r + 1) * 128,
                                                      il * 2048:(il + 1) * 2048]
                                         ).then_inc(wsl, 16)
                    sy.wait_ge(S("gu"), 512 + tc * 128 + 128)
                    for ct in range(NCT):
                        slot = (ct % 2) * 8192
                        wsl = S(f"wd{ct % 2}")
                        if ct >= 2:
                            sy.wait_ge(S("dn"), 64 + tc * 16 + ct - 1)
                        r, ctl = ct // 2, ct % 2
                        sy.dma_start(out=arena[:, slot:slot + 8192],
                                     in_=wfull["d"][r * 128:(r + 1) * 128,
                                                    ctl * 8192:(ctl + 1) * 8192]
                                     ).then_inc(wsl, 16)

            @block.scalar
            def _(sc):
                sc.wait_ge(S("ag"), 15)
                for tc in range(NCH):
                    lnn = 8 + tc
                    sc.wait_ge(S("x"), (4 + tc + 1) * 16)
                    for ht in range(HT):
                        sc.wait_ge(S("res2"), tc * 16 + ht + 1)
                        sc.copy(xln[:, xcol(ht)], xch[:, xcol(ht)])
                        sc.activation(inter[:, ht * 512:(ht + 1) * 512],
                                      xch[:, xcol(ht)], AF.Square).then_inc(S("xb"), 1)
                    sc.wait_ge(S("rowv"), lnn + 1)
                    sc.activation(scr[0:1, 1536:2048], scr[0:1, 1024:1536],
                                  AF.Ln, bias=eps_t[0:1, :])
                    sc.activation(scr[0:1, 1536:2048], scr[0:1, 1536:2048],
                                  AF.Exp, scale=-0.5).then_inc(S("rows"), 1)
                    for it in range(NIT):
                        sc.wait_ge(S("gu"), 512 + tc * 128 + 2 * it + 1)
                        sl = it % 2
                        sc.activation(xbx[:, sl * 512:(sl + 1) * 512], psum[PG][:, :],
                                      AF.Silu).then_inc(S("silu"), 1)

            @block.tensor
            def _(t):
                t.wait_ge(S("ag"), 15)
                for tc in range(NCH):
                    lnn = 8 + tc
                    t.wait_ge(S("cxi"), 32 * (tc + 1))
                    for ct in range(NCT):
                        wos = S("wo0") if ct % 2 == 0 else S("wo1")
                        t.wait_ge(wos, 16 * (tc * 8 + ct // 2 + 1))
                        if ct == 0:
                            t.wait_ge(S("res"), 64 + 16 * tc)
                        t.wait_ge(S("res2"), tc * 16 + ct)
                        for b in range(2):
                            for r in range(8):
                                ht2 = b * 8 + r
                                mmi = t.matmul(
                                    psum[PD][:, :],
                                    arena[:, (ct % 2) * 2048 + ht2 * 128:
                                           (ct % 2) * 2048 + ht2 * 128 + 128],
                                    inter[:, CXB + b * 4096 + r * 512:
                                           CXB + b * 4096 + (r + 1) * 512],
                                    start=(ht2 == 0), stop=(ht2 == 15))
                        mmi.then_inc(S("opj"), 1)
                    if True:
                        t.wait_ge(S("rowv"), lnn)
                        t.wait_ge(S("xb"), 128 + tc * 16 + 16)
                        for ht in range(HT):
                            t.matmul(psum[PS1][0:1, :], ones_bf[:, :], xln[:, xcol(ht)],
                                     start=(ht == 0), stop=(ht == HT - 1))
                        for ht in range(HT):
                            mmi = t.matmul(psum[PS2][0:1, :], ones_bf[:, :],
                                           inter[:, ht * 512:(ht + 1) * 512],
                                           start=(ht == 0), stop=(ht == HT - 1))
                        mmi.then_inc(S("stat"), 1)
                        t.wait_ge(S("rows"), lnn + 1)
                        t.wait_ge(S("apl"), 128 + 16 * tc)
                        t.matmul(psum[PM][:, :], ones1f[:, :], scr[0:1, 512 + 0:512 + 512],
                                 start=True, stop=True)
                        t.matmul(psum[PR][:, :], ones1f[:, :], scr[0:1, 1536:2048],
                                 start=True, stop=True).then_inc(S("bc"), 1)
                    t.wait_ge(S("apl"), 128 + 16 * tc + 16)
                    for it in range(NIT):
                        for gu, bank in ((0, PG), (1, PU)):
                            wsl = S(f"w{'g' if gu == 0 else 'u'}{it % 2}")
                            t.wait_ge(wsl, 16 * ((4 + tc) * 32 + it // 2 + 1))
                            if it >= 1:
                                if gu == 0:
                                    t.wait_ge(S("silu"), 256 + tc * 64 + it)
                                else:
                                    t.wait_ge(S("int_"), 256 + tc * 64 + it)
                            base = 0 if gu == 0 else 4096
                            slot = base + (it % 2) * 2048
                            for hb in range(HT):
                                mmi = t.matmul(psum[bank][:, :],
                                               arena[:, slot + hb * 128:slot + (hb + 1) * 128],
                                               xln[:, xcol(hb)],
                                               start=(hb == 0), stop=(hb == HT - 1))
                            mmi.then_inc(S("gu"), 1)
                    for ct in range(NCT):
                        wsl = S(f"wd{ct % 2}")
                        t.wait_ge(wsl, 16 * ((4 + tc) * 8 + ct // 2 + 1))
                        t.wait_ge(S("int_"), 256 + tc * 64 + 64)
                        t.wait_ge(S("res"), 64 + tc * 16 + ct)
                        if ct == 0:
                            t.wait_ge(S("res2"), 16 * (tc + 1))
                        slot = (ct % 2) * 8192
                        for it in range(NIT):
                            mmi = t.matmul(psum[PD][:, :],
                                           arena[:, slot + it * 128:slot + (it + 1) * 128],
                                           inter[:, it * 512:(it + 1) * 512],
                                           start=(it == 0), stop=(it == NIT - 1))
                        mmi.then_inc(S("dn"), 1)

            @block.vector
            def _(v):
                v.wait_ge(S("ag"), 15)
                for tc in range(NCH):
                    lnn = 8 + tc
                    for ct in range(NCT):
                        v.wait_ge(S("opj"), tc * 16 + ct + 1)
                        v.tensor_tensor(xch[:, xcol(ct)], psum[PD][:, :],
                                        xch[:, xcol(ct)], ALU.add).then_inc(S("res2"), 1)
                    v.wait_ge(S("stat"), lnn + 1)
                    v.tensor_scalar(scr[0:1, 512:1024], psum[PS1][0:1, :],
                                    1.0 / DM, None, ALU.mult)
                    v.tensor_scalar(scr[0:1, 1024:1536], psum[PS2][0:1, :],
                                    1.0 / DM, None, ALU.mult)
                    v.tensor_tensor(scr[0:1, 1536:2048], scr[0:1, 512:1024],
                                    scr[0:1, 512:1024], ALU.mult).then_inc(S("rv2"), 1)
                    v.wait_ge(S("rv2"), 8 + tc + 1)
                    v.tensor_tensor(scr[0:1, 1024:1536], scr[0:1, 1024:1536],
                                    scr[0:1, 1536:2048], ALU.subtract).then_inc(S("rowv"), 1)
                    v.wait_ge(S("bc"), lnn + 1)
                    for ht in range(HT):
                        v.tensor_tensor(scr[:, 0:512], xch[:, xcol(ht)], psum[PM][:, :],
                                        ALU.subtract).then_inc(S("ap2"), 1)
                        v.wait_ge(S("ap2"), 128 + tc * 16 + ht + 1)
                        v.tensor_tensor(xln[:, xcol(ht)], scr[:, 0:512], psum[PR][:, :],
                                        ALU.mult).then_inc(S("apl"), 1)
                    for it in range(NIT):
                        v.wait_ge(S("silu"), 256 + tc * 64 + it + 1)
                        v.wait_ge(S("gu"), 512 + tc * 128 + 2 * it + 2)
                        sl = it % 2
                        v.tensor_tensor(inter[:, it * 512:(it + 1) * 512],
                                        xbx[:, sl * 512:(sl + 1) * 512], psum[PU][:, :],
                                        ALU.mult).then_inc(S("int_"), 1)
                    for ct in range(NCT):
                        v.wait_ge(S("dn"), 64 + tc * 16 + ct + 1)
                        v.tensor_tensor(xch[:, xcol(ct)], psum[PD][:, :],
                                        xch[:, xcol(ct)], ALU.add).then_inc(S("res"), 1)

            @block.gpsimd
            def _(g):
                for tc in range(NCH):
                    g.wait_ge(S("res"), 64 + tc * 16 + 16)
                    g.dma_start(out=out[:, tc * HT * TCH:(tc + 1) * HT * TCH],
                                in_=xch[:, :]).then_inc(S("st_out"), 16)
                g.wait_ge(S("st_out"), 64)

        if MK_PHASES < 4 or MK_CUT != 0:
            @block.gpsimd
            def _(g):
                if NCH_E:
                    g.wait_ge(S("st_x1"), NCH_E * 16)
                g.wait_ge(S("ag"), 3 + (2 * NCH_E if MK_CUT == 0 else 0))
                g.dma_start(out=out, in_=resid if NCH_E else din["xt"]).then_inc(S("fin"), 16)
                g.wait_ge(S("fin"), 16)

    es.close()
    return nc


def kernel(**inputs):
    in_maps = _host_prep(inputs)
    nc = build_nc()
    res = run_bass_kernel_spmd(nc, in_maps, core_ids=list(range(NCORES)))
    x = np.asarray(inputs["L0_attn_x"])
    outp = np.empty_like(x)
    for c in range(NCORES):
        s, pos = c // 4, (c % 4) * TPC
        o = res.results[c]["out"]
        oT = o.reshape(128, NCH, HT, TCH).transpose(2, 0, 1, 3).reshape(DM, TPC)
        outp[s, pos:pos + TPC, :] = oT.T
    return outp
